# version 04
# baseline (speedup 1.0000x reference)
import numpy as np
import ml_dtypes

import concourse.bass as bass
import concourse.mybir as mybir
from concourse.bass_utils import run_bass_kernel_spmd

F32 = mybir.dt.float32
BF16 = mybir.dt.bfloat16
I32 = mybir.dt.int32
U16 = mybir.dt.uint16
Alu = mybir.AluOpType
Act = mybir.ActivationFunctionType
AxX = mybir.AxisListType.X

B, T, C = 2, 4096, 2048
NCORE = 8
OWN = 1024
SP = 512
NSP = T // SP
NCH = C // 128
FS = 2048
W_LOCAL = 512
SC0 = float(1.0 / np.sqrt(128.0))
SCG = float(SC0 * 128.0 / 64.0)
SC1 = float(1.0 / np.sqrt(64.0))
MB_L, MB_G, MB_1 = 20.0, 15.0, 25.0
GROUPS = [[0, 1, 2, 3], [4, 5, 6, 7]]

BC = dict(sc0=0, mbl=1, scg=2, mbg=3, sc1=4, mb1=5, bq=10, bk=14)


def _f32c(x):
    return np.ascontiguousarray(x, dtype=np.float32)


def _host_tables():
    t = {}
    inv = (1.0 / (10000.0 ** (np.arange(0, 128, 2, dtype=np.float32) / 128.0))).astype(np.float32)
    tt = np.arange(T, dtype=np.float32)
    fr = tt[:, None] * inv[None, :]
    emb = np.concatenate([fr, fr], axis=1)
    t["cos0"] = _f32c(np.cos(emb).T)
    t["sin0"] = _f32c(np.sin(emb).T)
    P = np.zeros((128, 128), np.float32)
    for i in range(64):
        P[2 * i, 2 * i + 1] = -1.0
        P[2 * i + 1, 2 * i] = 1.0
    t["rotP0T"] = np.ascontiguousarray(P.T, dtype=ml_dtypes.bfloat16)
    th = (10000.0 ** (-np.arange(0, 64, 2, dtype=np.float32) / 64.0)).astype(np.float32)
    tid = np.floor((np.arange(T) // 576).astype(np.float32) * 0.5).astype(np.float32)
    ang = (tt[:, None] + tid[:, None]) * th[None, :]
    ang2 = np.concatenate([ang, ang], axis=1)
    c1, s1 = np.cos(ang2).T, np.sin(ang2).T
    t["cos1"] = _f32c(np.concatenate([c1, c1], axis=0))
    t["sin1"] = _f32c(np.concatenate([s1, s1], axis=0))
    P1 = np.zeros((64, 64), np.float32)
    for i in range(32):
        P1[i, i + 32] = -1.0
        P1[i + 32, i] = 1.0
    z = np.zeros_like(P1)
    t["rotP1T"] = np.ascontiguousarray(np.block([[P1.T, z], [z, P1.T]]),
                                       dtype=ml_dtypes.bfloat16)
    m0 = np.zeros((128, 8 * SP), np.float32)
    for q in range(8):
        rho = -512 + 128 * q
        srel = np.arange(128)[:, None] + rho
        trel = np.arange(SP)[None, :]
        m0[:, q * SP:(q + 1) * SP] = (srel <= trel) & (srel >= trel - (W_LOCAL - 1))
    t["msk0"] = np.ascontiguousarray(m0, dtype=ml_dtypes.bfloat16)
    m1 = np.zeros((128, 4 * SP), np.float32)
    for q in range(4):
        m1[:, q * SP:(q + 1) * SP] = (np.arange(128)[:, None] + 128 * q) <= np.arange(SP)[None, :]
    t["msk1"] = np.ascontiguousarray(m1, dtype=ml_dtypes.bfloat16)
    t["idt32"] = np.eye(128, dtype=np.float32)
    return t


_TAB = None


def prep_core_inputs(core, inp):
    global _TAB
    if _TAB is None:
        _TAB = _host_tables()
    b, j = core // 4, core % 4
    d = {}
    x = np.asarray(inp["L0_attn_x"])
    d["xin"] = _f32c(x[b, OWN * j:OWN * j + OWN, :])
    wqkv = np.asarray(inp["L0_attn_w_qkv"])
    h0 = 4 * j
    d["wqT"] = _f32c(wqkv[128 * h0:128 * h0 + 512, :].T)
    d["wkT"] = _f32c(wqkv[2048 + 128 * h0:2048 + 128 * h0 + 512, :].T)
    d["wvT"] = _f32c(wqkv[4096 + 128 * h0:4096 + 128 * h0 + 512, :].T)
    d["wvN"] = _f32c(wqkv[4096 + 128 * h0:4096 + 128 * h0 + 512, :])
    d["pkT"] = _f32c(np.asarray(inp["L0_attn_pk"]).T)
    d["pvT"] = _f32c(np.asarray(inp["L0_attn_pv"]).T)
    d["uoT"] = _f32c(np.asarray(inp["L0_attn_uo"]).T)
    d["w1T"] = _f32c(np.asarray(inp["L0_attn_info_w1"]).T)
    d["w2c"] = _f32c(np.asarray(inp["L0_attn_info_w2"]).reshape(4, 128).T)
    d["woT0s"] = _f32c(np.asarray(inp["L0_attn_w_o"]).T[512 * j:512 * j + 512, :])
    d["wupT"] = _f32c(np.asarray(inp["L0_ffn_up_kernel"]).T[FS * j:FS * j + FS, :])
    d["wdnT"] = _f32c(np.asarray(inp["L0_ffn_down_kernel"]).T[:, FS * j:FS * j + FS])
    d["bupP"] = _f32c(np.asarray(inp["L0_ffn_up_bias"])[FS * j:FS * j + FS].reshape(16, 128).T)
    d["bdn"] = _f32c(np.asarray(inp["L0_ffn_down_bias"]).reshape(1, C))
    hb = 512 * j
    d["wq1T"] = _f32c(np.asarray(inp["L1_attn_Wq"]).T[:, hb:hb + 512])
    d["wk1T"] = _f32c(np.asarray(inp["L1_attn_Wk"]).T[:, hb:hb + 512])
    d["wv1T"] = _f32c(np.asarray(inp["L1_attn_Wv"]).T[:, hb:hb + 512])
    d["bq1"] = _f32c(np.asarray(inp["L1_attn_bq"])[hb:hb + 512].reshape(4, 128).T)
    d["bk1"] = _f32c(np.asarray(inp["L1_attn_bk"])[hb:hb + 512].reshape(4, 128).T)
    d["bv1"] = _f32c(np.asarray(inp["L1_attn_bv"])[hb:hb + 512].reshape(1, 512))
    d["bo1"] = _f32c(np.asarray(inp["L1_attn_bo"]).reshape(1, C))
    d["wo1Ts"] = _f32c(np.asarray(inp["L1_attn_Wo"]).T[hb:hb + 512, :])
    d["wt1"] = _f32c(np.asarray(inp["L1_ffn_weight"]))
    d["bff1"] = _f32c(np.asarray(inp["L1_ffn_bias"]).reshape(1, C))
    d.update(_TAB)
    return d


class Kern:
    def __init__(self, dbg=(), upto="full"):
        self.dbg = set(dbg)
        self.upto = upto
        self.dbg_out = {}
        self.nc = bass.Bass("TRN2", target_bir_lowering=False, debug=False)
        self.tok = {}

    def sem(self, name):
        self.h[name] = self.stk.enter_context(self.nc.semaphore(name))
        self.cnt[name] = 0

    def W(self, eng, ename, token):
        if token is None:
            return
        if isinstance(token, list):
            for t in token:
                self.W(eng, ename, t)
            return
        sname, val = token
        if sname[0] == "d" and sname[:2] in ("dl", "dg", "dt"):
            val = self.cnt[sname]
        key = (ename, sname)
        if val <= 0 or self.wm.get(key, 0) >= val:
            return
        eng.wait_ge(self.h[sname], val)
        self.wm[key] = val

    def _selfwait(self, eng, ename, sname):
        self.W(eng, ename, (sname, self.cnt[sname]))

    def op(self, eng, ename, sname, fname, *a, **kw):
        self._selfwait(eng, ename, sname)
        inst = getattr(eng, fname)(*a, **kw)
        inst.then_inc(self.h[sname], 1)
        self.cnt[sname] += 1
        return (sname, self.cnt[sname])

    def mm(self, Te, out, l, r, start, stop, tp=None, inc=False):
        kw = {}
        if tp is not None:
            kw["tile_position"] = tp
        if inc:
            self._selfwait(Te, "tensor", "ts")
        inst = Te.matmul(out, l, r, start=start, stop=stop, **kw)
        if inc:
            inst.then_inc(self.h["ts"], 1)
            self.cnt["ts"] += 1
            return ("ts", self.cnt["ts"])
        return None

    def tr(self, Te, out, in_, ident):
        self._selfwait(Te, "tensor", "ts")
        inst = Te.transpose(out, in_, ident)
        inst.then_inc(self.h["ts"], 1)
        self.cnt["ts"] += 1
        return ("ts", self.cnt["ts"])

    def dma(self, eng, qname, out, in_, transpose=False, accum=False):
        if self._dmasec.get(qname) != self._secn:
            self._dmasec[qname] = self._secn
            self.qi[qname] += 1
        i = self.qi[qname] % 4
        sname = "%s%d" % (qname, i)
        if transpose:
            inst = eng.dma_start_transpose(out=out, in_=in_)
        elif accum:
            inst = eng.dma_start(out=out, in_=in_, accum_op=Alu.add)
        else:
            inst = eng.dma_start(out=out, in_=in_)
        inst.then_inc(self.h[sname], 16)
        self.cnt[sname] += 16
        return (sname, self.cnt[sname])

    def qall(self, qname):
        return [("%s%d" % (qname, i), self.cnt["%s%d" % (qname, i)]) for i in range(4)]

    def allq(self):
        out = []
        for q in ("dl", "dg", "dt"):
            out += self.qall(q)
        return out

    def coll(self, kind, aop, ins, outs, name):
        tok = self.tok

        @self.sec("gpsimd")
        def _(G):
            self.W(G, "gpsimd", self.allq())
            for s in ("ts", "vs", "ss"):
                self.W(G, "gpsimd", (s, self.cnt[s]))
            self._selfwait(G, "gpsimd", "cc")
            inst = G.collective_compute(kind, aop, replica_groups=GROUPS,
                                        ins=[ins.ap().opt()], outs=[outs.ap().opt()])
            inst.then_inc(self.h["cc"], 1)
            self.cnt["cc"] += 1
            tok[name] = ("cc", self.cnt["cc"])

    def declare(self):
        nc = self.nc
        A = {}
        def ein(name, shape, dt=F32):
            A[name] = nc.dram_tensor(name, list(shape), dt, kind="ExternalInput").ap()
        ein("xin", [OWN, C])
        ein("wqT", [C, 512]); ein("wkT", [C, 512]); ein("wvT", [C, 512])
        ein("wvN", [512, C])
        ein("pkT", [128, 64]); ein("pvT", [128, 64]); ein("uoT", [64, 128])
        ein("w1T", [C, 512]); ein("w2c", [128, 4])
        ein("woT0s", [512, C])
        ein("wupT", [FS, C]); ein("wdnT", [C, FS])
        ein("bupP", [128, 16]); ein("bdn", [1, C])
        ein("wq1T", [C, 512]); ein("wk1T", [C, 512]); ein("wv1T", [C, 512])
        ein("bq1", [128, 4]); ein("bk1", [128, 4]); ein("bv1", [1, 512])
        ein("bo1", [1, C]); ein("wo1Ts", [512, C]); ein("wt1", [C, C])
        ein("bff1", [1, C])
        ein("cos0", [128, T]); ein("sin0", [128, T])
        ein("cos1", [128, T]); ein("sin1", [128, T])
        ein("rotP0T", [128, 128], BF16); ein("rotP1T", [128, 128], BF16)
        ein("msk0", [128, 8 * SP], BF16); ein("msk1", [128, 4 * SP], BF16)
        ein("idt32", [128, 128])
        self.I = A
        self.yout = nc.dram_tensor("yout", [OWN, C], F32, kind="ExternalOutput").ap()
        Hh = {}
        def hin(name, shape, dt=BF16):
            Hh[name] = nc.dram_tensor(name, list(shape), dt, kind="Internal")
        hin("ag_in", [OWN, C]); hin("ag_out", [T, C])
        hin("rs_in", [T, C]); hin("rs_out", [OWN, C])
        hin("x1h", [OWN, C], F32)
        hin("q0h", [512, T])
        hin("kh", [128, NSP * 4 * 512])
        hin("vh", [128, 32 * 520])
        hin("wqupT", [FS, C]); hin("wqdnT", [C, FS]); hin("wt1q", [C, C])
        hin("xqh", [OWN, C])
        hin("gscr", [1, C], F32)
        hin("nscr", [1, 2 * SP], F32)
        hin("ctlh", [128, NSP * 2048])
        hin("wvrh", [256, C])
        self.Hh = Hh

    def dump(self, name, shape, dt=F32):
        if name in self.dbg:
            self.dbg_out[name] = self.nc.dram_tensor(
                "dbg_" + name, list(shape), dt, kind="ExternalOutput").ap()

    def build(self):
        from contextlib import ExitStack
        nc = self.nc
        self.declare()
        self.dump("ag_in", [OWN, C], BF16); self.dump("ag_out", [T, C], BF16)
        self.dump("q0h", [512, T], BF16); self.dump("kh", [128, NSP * 4 * 512], BF16)
        self.dump("vh", [128, 32 * 520], BF16)
        self.dump("kr", [128, 2 * T], BF16); self.dump("vr", [128, 32 * 260], BF16)
        self.dump("gate", [1, T]); self.dump("wqupT", [FS, C], BF16)
        self.dump("wqdnT", [C, FS], BF16); self.dump("wt1q", [C, C], BF16)
        self.dump("gscr", [1, C]); self.dump("rs_out", [OWN, C], BF16)
        self.dump("x1h", [OWN, C]); self.dump("xqh", [OWN, C], BF16)
        self.dump("rs_in", [T, C], BF16); self.dump("wvrh", [256, C], BF16)
        self.dump("gbc", [128, 2048])
        with ExitStack() as stk:
            self.stk = stk
            self.h, self.cnt, self.wm = {}, {}, {}
            self.qi = {"dl": 0, "dg": 0, "dt": 0}
            self._secn = 0
            self._dmasec = {}
            S = {}
            def sb(name, shape, dt=BF16):
                S[name] = stk.enter_context(nc.sbuf_tensor(name, shape, dt))
            sb("xnsp", [128, NCH * SP])
            sb("wA", [128, NCH * 512])
            sb("wB", [128, NCH * 512])
            sb("wC", [128, NCH * 512])
            sb("wD", [128, NCH * 256])
            sb("w1b", [128, 2 * 512])
            sb("kr", [128, 2 * T])
            sb("vr", [128, 32 * 260])
            sb("qsp", [128, 4 * SP])
            sb("qr", [128, 2 * SP])
            sb("tab", [128, 2 * SP], F32)
            sb("msk", [128, 8 * SP])
            sb("att", [128, 4 * 1024])
            sb("hsp", [128, NCH * SP])
            sb("ev", [128, 8 * 512])
            sb("ctl", [128, 4 * 512])
            sb("xt", [128, 2048], F32)
            sb("xt2", [128, 2048], F32)
            sb("cx", [128, 2048])
            sb("sm", [128, 256], F32)
            sb("smb", [128, 128])
            sb("nbc", [128, 2 * SP], F32)
            sb("bias", [128, 40], F32)
            sb("bcst", [128, 2048])
            sb("gbc", [128, 2048])
            sb("gcol", [128, 16], F32)
            sb("tiny", [128, 640])
            sb("id32", [128, 128], F32)
            sb("gate", [1, T])
            S["kl"] = S["wC"]
            S["vl"] = S["wA"]
            S["qr"] = S["wD"]
            S["svec"] = S["xt2"][0:1, 0:2048]
            S["kl1"] = S["kr"]
            S["vl1"] = S["vr"]
            self.S = S
            PS = {}
            def psm(name, shape):
                PS[name] = stk.enter_context(nc.psum_tensor(name, shape, F32))
            psm("qA", [128, 1024]); psm("qB", [128, 1024])
            psm("qC", [128, 1024]); psm("qD", [128, 1024])
            PS["pA"] = PS["qA"][:, :]
            PS["pB"] = PS["qB"][:, :]
            PS["p0"] = PS["qC"][:, 0:512]
            PS["p1"] = PS["qC"][:, 512:1024]
            PS["p2"] = PS["qD"][:, 0:512]
            PS["p3"] = PS["qD"][:, 512:1024]
            self.PS = PS
            for q in ("dl", "dg", "dt"):
                for i in range(4):
                    self.sem("%s%d" % (q, i))
            for s in ("ts", "vs", "ss", "cc", "fin"):
                self.sem(s)
            self.blk = stk.enter_context(nc.Block())
            self.emit()
        return nc

    def sec(self, name):
        self._secn += 1
        return getattr(self.blk, name)

    def emit(self):
        upto = self.upto
        tok = self.tok
        self.p0()
        self.p1(layer=0)
        self.coll("AllGather", Alu.bypass, self.Hh["ag_in"], self.Hh["ag_out"], "ag0")
        if upto == "p1":
            return self.finish()
        self.quant_mx("up")
        self.quant_mx("dn")
        self.quant_ternary()
        if upto == "p2":
            return self.finish()
        self.l0_wload()
        self.wvr_build()
        for sp in range(NSP):
            self.qkv_span(0, sp)
        if upto == "p3":
            return self.finish()
        for tau in range(NSP):
            self.attn0_local_span(tau)
        self.l0_wo_load()
        for sp in range(NSP):
            self.ffn0_span(sp)
            self.attn0_global_span(sp)
        self.coll("ReduceScatter", Alu.add, self.Hh["rs_in"], self.Hh["rs_out"], "rs0")
        if upto in ("p4", "p5"):
            return self.finish()
        self.p6()
        self.coll("AllGather", Alu.bypass, self.Hh["ag_in"], self.Hh["ag_out"], "ag1")
        if upto == "p6":
            return self.finish()
        self.l1_wload()
        self.l1_wo_load()
        if upto == "p7":
            for sp in range(NSP):
                self.qkv_span(1, sp)
            return self.finish()
        for sp in range(NSP):
            self.qkv_span(1, sp)
            self.attn1_span(sp)
        self.coll("ReduceScatter", Alu.add, self.Hh["rs_in"], self.Hh["rs_out"], "rs1")
        if upto == "p8":
            return self.finish()
        self.p9_p10()
        self.finish()

    def p0(self):
        S, I, tok = self.S, self.I, self.tok

        @self.sec("vector")
        def _(V):
            for name, val in [("sc0", SC0), ("mbl", -MB_L), ("scg", SCG),
                              ("mbg", -MB_G), ("sc1", SC1), ("mb1", -MB_1)]:
                self.op(V, "vector", "vs", "memset", S["bias"][:, BC[name]:BC[name] + 1], val)
            tok["consts"] = self.op(V, "vector", "vs", "memset",
                                    S["tiny"][:, 516:517], 1.0)

        @self.sec("sync")
        def _(SY):
            tok["msk"] = self.dma(SY, "dl", S["msk"][:, 0:8 * SP], I["msk0"])
            tok["id32"] = self.dma(SY, "dl", S["id32"][:, :], I["idt32"])
            tok["biasq"] = self.dma(SY, "dl", S["bias"][:, BC["bq"]:BC["bq"] + 4], I["bq1"])
            tok["biask"] = self.dma(SY, "dl", S["bias"][:, BC["bk"]:BC["bk"] + 4], I["bk1"])

        @self.sec("gpsimd")
        def _(G):
            t = S["tiny"]
            self.dma(G, "dg", t[:, 0:128], I["rotP0T"])
            self.dma(G, "dg", t[:, 128:256], I["rotP1T"])
            self.dma(G, "dg", t[:, 256:320], I["pkT"])
            self.dma(G, "dg", t[0:64, 320:448], I["uoT"])
            self.dma(G, "dg", t[:, 448:512], I["pvT"])
            self.dma(G, "dg", t[:, 520:524], I["w2c"])
            self.dma(G, "dg", S["nbc"][:, SP:2 * SP],
                     I["bv1"].partition_broadcast(128).squeeze(1))
            tok["bupP"] = self.dma(G, "dg", S["bias"][:, 20:36], I["bupP"])
            tok["tiny"] = self.qall("dg")

    def p1(self, layer):
        S, I, H, tok = self.S, self.I, self.Hh, self.tok
        src = I["xin"] if layer == 0 else H["x1h"].ap()
        for ti in range(8):
            self.ln_tile(ti, src[128 * ti:128 * ti + 128, :], layer)

            @self.sec("scalar")
            def _(Sc):
                self.W(Sc, "scalar", tok["lnxn"])
                tok["agw%d" % ti] = self.dma(
                    Sc, "dt", H["ag_in"].ap()[128 * ti:128 * ti + 128, :], S["cx"][:, :])

            if layer == 1:
                self.xq_tile(ti)

    def ln_tile(self, ti, src_ap, layer):
        S, tok = self.S, self.tok
        xt, xt2, cx, sm = S["xt"], S["xt2"], S["cx"], S["sm"]
        c = 192 + (ti % 2) * 16

        @self.sec("sync")
        def _(SY):
            self.W(SY, "sync", ("vs", self.cnt["vs"]))
            self.W(SY, "sync", ("ss", self.cnt["ss"]))
            tok["lnx"] = self.dma(SY, "dl", xt[:, :], src_ap)

        @self.sec("scalar")
        def _(Sc):
            self.W(Sc, "scalar", tok["lnx"])
            tok["lnsq"] = self.op(Sc, "scalar", "ss", "activation",
                                  xt2[:, :], xt[:, :], Act.Square)

        @self.sec("vector")
        def _(V):
            self.W(V, "vector", tok["lnx"])
            self.W(V, "vector", tok["lnsq"])
            self.op(V, "vector", "vs", "reduce_sum", sm[:, c:c + 1], xt[:, :], AxX)
            self.op(V, "vector", "vs", "reduce_sum", sm[:, c + 1:c + 2], xt2[:, :], AxX)
            self.op(V, "vector", "vs", "tensor_scalar", sm[:, c:c + 1], sm[:, c:c + 1],
                    1.0 / C, None, Alu.mult)
            self.op(V, "vector", "vs", "tensor_scalar", sm[:, c + 1:c + 2],
                    sm[:, c + 1:c + 2], 1.0 / C, None, Alu.mult)
            self.op(V, "vector", "vs", "tensor_scalar", sm[:, c + 2:c + 3],
                    sm[:, c:c + 1], sm[:, c:c + 1], None, Alu.mult)
            self.op(V, "vector", "vs", "tensor_sub", sm[:, c + 3:c + 4],
                    sm[:, c + 1:c + 2], sm[:, c + 2:c + 3])
            self.op(V, "vector", "vs", "tensor_scalar", sm[:, c + 4:c + 5],
                    sm[:, c + 3:c + 4], 1e-5, None, Alu.add)
            tok["lnv"] = self.op(V, "vector", "vs", "reciprocal",
                                 sm[:, c + 5:c + 6], sm[:, c + 4:c + 5])

        @self.sec("scalar")
        def _(Sc):
            self.W(Sc, "scalar", tok["lnv"])
            tok["lnrs"] = self.op(Sc, "scalar", "ss", "activation",
                                  sm[:, c + 6:c + 7], sm[:, c + 5:c + 6], Act.Sqrt)

        @self.sec("vector")
        def _(V):
            self.W(V, "vector", tok["lnrs"])
            self.W(V, "vector", tok.get("agw%d" % (ti - 1)))
            tok["lnxn"] = self.op(V, "vector", "vs", "tensor_scalar", cx[:, :], xt[:, :],
                                  sm[:, c:c + 1], sm[:, c + 6:c + 7],
                                  Alu.subtract, Alu.mult)
            if layer == 1:
                self.op(V, "vector", "vs", "reduce_max", sm[:, c + 7:c + 8], cx[:, :],
                        AxX, apply_absolute_value=True)
                self.op(V, "vector", "vs", "tensor_scalar", sm[:, 40 + ti:41 + ti],
                        sm[:, c + 7:c + 8], 1.0 / 127.0, None, Alu.mult)
                self.op(V, "vector", "vs", "reciprocal", sm[:, c + 8:c + 9],
                        sm[:, c + 4:c + 5])
                self.op(V, "vector", "vs", "tensor_mul", sm[:, c + 9:c + 10],
                        sm[:, c + 8:c + 9], sm[:, c + 3:c + 4])
                tok["rms2"] = self.op(V, "vector", "vs", "tensor_scalar",
                                      sm[:, c + 9:c + 10], sm[:, c + 9:c + 10],
                                      1e-8, None, Alu.add)

        if layer == 1:
            @self.sec("scalar")
            def _(Sc):
                self.W(Sc, "scalar", tok["rms2"])
                tok["rms"] = self.op(Sc, "scalar", "ss", "activation",
                                     sm[:, c + 10:c + 11], sm[:, c + 9:c + 10], Act.Sqrt)

            @self.sec("vector")
            def _(V):
                self.W(V, "vector", tok["rms"])
                self.op(V, "vector", "vs", "tensor_scalar", sm[:, c + 11:c + 12],
                        sm[:, 40 + ti:41 + ti], 1e-8, None, Alu.add)
                self.op(V, "vector", "vs", "tensor_mul", sm[:, c + 12:c + 13],
                        sm[:, c + 11:c + 12], sm[:, c + 10:c + 11])
                tok["cb"] = self.op(V, "vector", "vs", "reciprocal",
                                    sm[:, 48 + ti:49 + ti], sm[:, c + 12:c + 13])

    def xq_tile(self, ti):
        S, H, tok = self.S, self.Hh, self.tok
        xt, xt2, cx, sm = S["xt"], S["xt2"], S["cx"], S["sm"]

        @self.sec("vector")
        def _(V):
            self.W(V, "vector", tok["cb"])
            self.W(V, "vector", tok.get("xqw"))
            self.op(V, "vector", "vs", "tensor_scalar", xt2[:, :], cx[:, :],
                    sm[:, 48 + ti:49 + ti], None, Alu.mult)
            self.op(V, "vector", "vs", "tensor_scalar", xt[:, :], xt2[:, :],
                    12582912.0, None, Alu.add)
            self.op(V, "vector", "vs", "tensor_scalar", xt2[:, :], xt[:, :],
                    12582912.0, None, Alu.subtract)
            self.op(V, "vector", "vs", "tensor_scalar", xt[:, :], xt2[:, :],
                    127.0, -127.0, Alu.min, Alu.max)
            tok["xq"] = self.op(V, "vector", "vs", "tensor_copy",
                                S["ev"][:, 0:2048], xt[:, :])

        @self.sec("scalar")
        def _(Sc):
            self.W(Sc, "scalar", tok["xq"])
            tok["xqw"] = self.dma(Sc, "dt", H["xqh"].ap()[128 * ti:128 * ti + 128, :],
                                  S["ev"][:, 0:2048])

    def quant_mx(self, which):
        S, I, H, tok = self.S, self.I, self.Hh, self.tok
        src = I["wupT" if which == "up" else "wdnT"]
        dsth = H["wqupT" if which == "up" else "wqdnT"]
        xt, xt2, hs, sm, smb = S["xt"], S["xt2"], S["hsp"], S["sm"], S["smb"]
        for ti in range(16):
            @self.sec("sync")
            def _(SY):
                self.W(SY, "sync", ("vs", self.cnt["vs"]))
                tok["qzl"] = self.dma(SY, "dl", xt[:, :],
                                      src[128 * ti:128 * ti + 128, :])

            @self.sec("vector")
            def _(V):
                self.W(V, "vector", tok["qzl"])
                if ti == 0:
                    self.W(V, "vector", self.qall("dt"))
                self.W(V, "vector", tok.get("qzw%s%d" % (which, ti - 1)))
                w3 = xt[:, :].rearrange("p (nb k) -> p nb k", k=32)
                self.op(V, "vector", "vs", "tensor_reduce", sm[:, 0:64], w3, AxX,
                        Alu.max, apply_absolute_value=True)
                a_i = sm[:, 0:64].bitcast(I32)
                s_i = sm[:, 64:128].bitcast(I32)
                r_i = sm[:, 128:192].bitcast(I32)
                p_i = sm[:, 192:256].bitcast(I32)
                self.op(V, "vector", "vs", "tensor_scalar", p_i, a_i,
                        0x7F800000, None, Alu.bitwise_and)
                self.op(V, "vector", "vs", "tensor_scalar", s_i, p_i,
                        0x01000000, None, Alu.subtract)
                self.op(V, "vector", "vs", "tensor_scalar", r_i, s_i,
                        0x7F000000, -1, Alu.subtract, Alu.mult)
                self.op(V, "vector", "vs", "tensor_copy", smb[:, 0:64], sm[:, 128:192])
                self.op(V, "vector", "vs", "tensor_copy", smb[:, 64:128], sm[:, 64:128])
                acc = hs[:, 0:2048]; tmp = hs[:, 2048:4096]
                sgn = hs[:, 4096:6144]; qs = hs[:, 6144:8192]
                rb = smb[:, 0:64].unsqueeze(2).broadcast_to([128, 64, 32])
                sbb = smb[:, 64:128].unsqueeze(2).broadcast_to([128, 64, 32])
                self.op(V, "vector", "vs", "tensor_tensor",
                        xt2[:, :].rearrange("p (nb k) -> p nb k", k=32), w3, rb,
                        Alu.mult)
                self.op(V, "vector", "vs", "tensor_scalar", sgn, xt2[:, :],
                        0.0, 2.0, Alu.is_ge, Alu.mult)
                self.op(V, "vector", "vs", "tensor_scalar", xt2[:, :].bitcast(I32),
                        xt2[:, :].bitcast(I32), 0x7FFFFFFF, None, Alu.bitwise_and)
                self.op(V, "vector", "vs", "tensor_scalar", acc, xt2[:, :],
                        0.25, 0.5, Alu.is_ge, Alu.mult)
                for th, dl_ in [(0.75, 0.5), (1.25, 0.5), (1.75, 0.5),
                                (2.5, 1.0), (3.5, 1.0), (5.0, 2.0)]:
                    self.op(V, "vector", "vs", "tensor_scalar", tmp, xt2[:, :],
                            th, dl_, Alu.is_ge, Alu.mult)
                    self.op(V, "vector", "vs", "tensor_add", acc, acc, tmp)
                self.op(V, "vector", "vs", "tensor_tensor",
                        qs.rearrange("p (nb k) -> p nb k", k=32),
                        acc.rearrange("p (nb k) -> p nb k", k=32), sbb, Alu.mult)
                self.op(V, "vector", "vs", "tensor_scalar", sgn, sgn,
                        1.0, None, Alu.subtract)
                tok["qz%s%d" % (which, ti)] = self.op(
                    V, "vector", "vs", "tensor_tensor", acc, qs, sgn, Alu.mult)

            @self.sec("scalar")
            def _(Sc):
                self.W(Sc, "scalar", tok["qz%s%d" % (which, ti)])
                tok["qzw%s%d" % (which, ti)] = self.dma(
                    Sc, "dt", dsth.ap()[128 * ti:128 * ti + 128, :], hs[:, 0:2048])

    def quant_ternary(self):
        S, I, H, tok = self.S, self.I, self.Hh, self.tok
        xt, xt2, sm = S["xt"], S["xt2"], S["sm"]
        for ti in range(16):
            @self.sec("sync")
            def _(SY):
                self.W(SY, "sync", ("vs", self.cnt["vs"]))
                tok["tql"] = self.dma(SY, "dl", xt[:, :],
                                      I["wt1"][128 * ti:128 * ti + 128, :])

            @self.sec("vector")
            def _(V):
                self.W(V, "vector", tok["tql"])
                if ti == 0:
                    self.W(V, "vector", self.qall("dt"))
                self.W(V, "vector", tok.get("tqw%d" % (ti - 1)))
                self.op(V, "vector", "vs", "tensor_reduce", sm[:, 0:1], xt[:, :],
                        AxX, Alu.add, apply_absolute_value=True)
                self.op(V, "vector", "vs", "tensor_scalar", S["gcol"][:, ti:ti + 1],
                        sm[:, 0:1], 1.0 / 2048.0, None, Alu.mult)
                self.op(V, "vector", "vs", "tensor_scalar", sm[:, 2:3],
                        S["gcol"][:, ti:ti + 1], 1e-8, None, Alu.add)
                self.op(V, "vector", "vs", "reciprocal", sm[:, 3:4], sm[:, 2:3])
                self.op(V, "vector", "vs", "tensor_scalar", xt2[:, :], xt[:, :],
                        sm[:, 3:4], None, Alu.mult)
                self.op(V, "vector", "vs", "tensor_scalar", xt[:, :], xt2[:, :],
                        12582912.0, None, Alu.add)
                self.op(V, "vector", "vs", "tensor_scalar", xt2[:, :], xt[:, :],
                        12582912.0, None, Alu.subtract)
                self.op(V, "vector", "vs", "tensor_scalar", xt[:, :], xt2[:, :],
                        1.0, -1.0, Alu.min, Alu.max)
                tok["tq%d" % ti] = self.op(V, "vector", "vs", "tensor_copy",
                                           S["ev"][:, 0:2048], xt[:, :])

            @self.sec("scalar")
            def _(Sc):
                self.W(Sc, "scalar", tok["tq%d" % ti])
                tok["tqw%d" % ti] = self.dma(
                    Sc, "dt", H["wt1q"].ap()[128 * ti:128 * ti + 128, :],
                    S["ev"][:, 0:2048])

        @self.sec("tensor")
        def _(Te):
            self.W(Te, "tensor", ("vs", self.cnt["vs"]))
            self.W(Te, "tensor", self.tok["id32"])
            tok["gtr"] = self.tr(Te, self.PS["p0"][0:16, 0:128], S["gcol"][:, :],
                                 S["id32"][:, :])

        @self.sec("vector")
        def _(V):
            self.W(V, "vector", tok["gtr"])
            tok["gev"] = self.op(V, "vector", "vs", "tensor_copy",
                                 sm[0:16, 128:256], self.PS["p0"][0:16, 0:128])

        @self.sec("scalar")
        def _(Sc):
            self.W(Sc, "scalar", tok["gev"])
            tok["gw"] = self.dma(Sc, "dt", H["gscr"].ap(), sm[0:16, 128:256])

        @self.sec("gpsimd")
        def _(G):
            self.W(G, "gpsimd", tok["gw"])
            tok["gbc"] = self.dma(G, "dg", S["gbc"][:, :],
                                  H["gscr"].ap().partition_broadcast(128).squeeze(1))

    def l0_wload(self):
        S, I, tok = self.S, self.I, self.tok

        @self.sec("gpsimd")
        def _(G):
            self.W(G, "gpsimd", ("vs", self.cnt["vs"]))
            tk = None
            for cc in range(NCH):
                self.dma(G, "dg", S["wA"][:, 512 * cc:512 * cc + 512],
                         I["wqT"][128 * cc:128 * cc + 128, :])
                self.dma(G, "dg", S["wB"][:, 512 * cc:512 * cc + 512],
                         I["wkT"][128 * cc:128 * cc + 128, :])
                self.dma(G, "dg", S["wC"][:, 512 * cc:512 * cc + 512],
                         I["wvT"][128 * cc:128 * cc + 128, :])
            tok["lw"] = self.qall("dg")

    def l1_wload(self):
        S, I, tok = self.S, self.I, self.tok

        @self.sec("gpsimd")
        def _(G):
            self.W(G, "gpsimd", ("ts", self.cnt["ts"]))
            for cc in range(NCH):
                self.dma(G, "dg", S["wA"][:, 512 * cc:512 * cc + 512],
                         I["wq1T"][128 * cc:128 * cc + 128, :])
                self.dma(G, "dg", S["wB"][:, 512 * cc:512 * cc + 512],
                         I["wk1T"][128 * cc:128 * cc + 128, :])
                self.dma(G, "dg", S["wC"][:, 512 * cc:512 * cc + 512],
                         I["wv1T"][128 * cc:128 * cc + 128, :])
            tok["lw"] = self.qall("dg")

        @self.sec("sync")
        def _(SY):
            self.W(SY, "sync", ("vs", self.cnt["vs"]))
            tok["msk"] = self.dma(SY, "dl", S["msk"][:, 0:4 * SP], I["msk1"])

    def l0_wo_load(self):
        S, I, tok = self.S, self.I, self.tok

        @self.sec("gpsimd")
        def _(G):
            self.W(G, "gpsimd", ("ts", self.cnt["ts"]))
            self.W(G, "gpsimd", ("vs", self.cnt["vs"]))
            for cc in range(4):
                self.dma(G, "dg", S["wC"][:, 2048 * cc:2048 * cc + 2048],
                         I["woT0s"][128 * cc:128 * cc + 128, :])
            tok["wo"] = self.qall("dg")

    def l1_wo_load(self):
        S, I, tok = self.S, self.I, self.tok

        @self.sec("gpsimd")
        def _(G):
            self.W(G, "gpsimd", ("ts", self.cnt["ts"]))
            self.W(G, "gpsimd", ("ss", self.cnt["ss"]))
            for cc in range(4):
                self.dma(G, "dg", S["hsp"][:, 2048 * cc:2048 * cc + 2048],
                         I["wo1Ts"][128 * cc:128 * cc + 128, :])
            tok["wo"] = self.qall("dg")

    def wvr_build(self):
        S, I, H, PS, tok = self.S, self.I, self.Hh, self.PS, self.tok
        for h in range(4):
            buf = (h % 2) * 2048

            @self.sec("gpsimd")
            def _(G):
                self.W(G, "gpsimd", tok.get("wvrw%d" % (h - 2)))
                tok["wvrl%d" % h] = self.dma(G, "dg", S["att"][:, buf:buf + 2048],
                                             I["wvN"][128 * h:128 * h + 128, :])

            @self.sec("tensor")
            def _(Te):
                self.W(Te, "tensor", tok["wvrl%d" % h])
                self.W(Te, "tensor", tok["tiny"])
                for csp in range(4):
                    self.W(Te, "tensor", tok.get("wvrev%d_%d" % (h - 1, csp)))
                    tok["wvrp%d_%d" % (h, csp)] = self.mm(
                        Te, PS["p%d" % csp][0:64, :], S["tiny"][:, 448:512],
                        S["att"][:, buf + 512 * csp:buf + 512 * csp + 512],
                        start=True, stop=True, inc=True)

            @self.sec("vector")
            def _(V):
                if h == 0:
                    self.W(V, "vector", self.qall("dt"))
                self.W(V, "vector", tok.get("wvrw%d" % (h - 1)))
                for csp in range(4):
                    self.W(V, "vector", tok["wvrp%d_%d" % (h, csp)])
                    tok["wvrev%d_%d" % (h, csp)] = self.op(
                        V, "vector", "vs", "tensor_copy",
                        S["ev"][0:64, 512 * csp:512 * csp + 512],
                        PS["p%d" % csp][0:64, :])

            @self.sec("scalar")
            def _(Sc):
                self.W(Sc, "scalar", tok["wvrev%d_3" % h])
                tok["wvrw%d" % h] = self.dma(
                    Sc, "dt", H["wvrh"].ap()[64 * h:64 * h + 64, :],
                    S["ev"][0:64, 0:2048])

        @self.sec("sync")
        def _(SY):
            self.W(SY, "sync", [tok["wvrw%d" % h] for h in range(4)])
            tk = None
            for cc in range(NCH):
                tk = self.dma(SY, "dl", S["wD"][:, 256 * cc:256 * cc + 256],
                              H["wvrh"].ap()[0:256, 128 * cc:128 * cc + 128],
                              transpose=True)
            tok["wDl"] = self.qall("dl")

    def qkv_span(self, layer, sp):
        S, I, H, PS, tok = self.S, self.I, self.Hh, self.PS, self.tok
        xb = 0
        agt = tok["ag%d" % layer]

        @self.sec("sync")
        def _(SY):
            self.W(SY, "sync", agt)
            self.W(SY, "sync", tok.get("qkvdone%d" % (sp - 1)))
            lt = []
            for cc in range(NCH):
                lt.append(self.dma(SY, "dl",
                                   S["xnsp"][:, xb + SP * cc:xb + SP * cc + SP],
                                   H["ag_out"].ap()[SP * sp:SP * sp + SP,
                                                    128 * cc:128 * cc + 128],
                                   transpose=True))
            self.W(SY, "sync", tok.get("ropedone%d" % (sp - 1)))
            lt.append(self.dma(SY, "dl", S["tab"][:, 0:SP],
                               I["cos%d" % layer][:, SP * sp:SP * sp + SP]))
            lt.append(self.dma(SY, "dl", S["tab"][:, SP:2 * SP],
                               I["sin%d" % layer][:, SP * sp:SP * sp + SP]))
            tok["xnl"] = lt

        for oc in range(8):
            isq = oc < 4
            wsb = S["wA"] if isq else S["wB"]
            hcol = oc % 4

            @self.sec("tensor")
            def _(Te):
                self.W(Te, "tensor", tok["xnl"])
                self.W(Te, "tensor", tok["lw"])
                if layer == 0:
                    ps = PS["p%d" % (oc % 2)]
                    self.W(Te, "tensor", tok.get("qkev%d" % ((oc - 2) % 8)))
                else:
                    ps = PS["p0"]
                    if oc == 0:
                        self.W(Te, "tensor", ("vs", self.cnt["vs"]))
                    self.W(Te, "tensor", tok.get("qkev%d" % ((oc - 1) % 8)))
                for cc in range(NCH):
                    self.mm(Te, ps[:, :],
                            wsb[:, 512 * cc + 128 * hcol:512 * cc + 128 * hcol + 128],
                            S["xnsp"][:, xb + SP * cc:xb + SP * cc + SP],
                            start=(cc == 0), stop=(cc == NCH - 1),
                            inc=(cc == NCH - 1))
                tok["qkmm%d" % oc] = ("ts", self.cnt["ts"])

            if layer == 1:
                @self.sec("vector")
                def _(V):
                    self.W(V, "vector", tok["qkmm%d" % oc])
                    ps = PS["p0"]
                    slot = S["ev"][:, 512 * oc:512 * oc + 512]
                    bcol = (BC["bq"] if isq else BC["bk"]) + hcol
                    self.W(V, "vector", tok["biasq"])
                    self.W(V, "vector", tok["biask"])
                    tok["qkev%d" % oc] = self.op(
                        V, "vector", "vs", "tensor_scalar", slot, ps[:, :],
                        S["bias"][:, bcol:bcol + 1], None, Alu.add)
            else:
                @self.sec("scalar")
                def _(Sc):
                    self.W(Sc, "scalar", tok["qkmm%d" % oc])
                    ps = PS["p%d" % (oc % 2)]
                    slot = S["ev"][:, 512 * oc:512 * oc + 512]
                    self.W(Sc, "scalar", tok.get("rotmm%d" % ((oc - 2) % 8)))
                    tok["qkev%d" % oc] = self.op(Sc, "scalar", "ss", "copy",
                                                 slot, ps[:, :])

            @self.sec("tensor")
            def _(Te):
                self.W(Te, "tensor", tok["qkev%d" % oc])
                self.W(Te, "tensor", tok["tiny"])
                if layer == 0:
                    prot = PS["p%d" % (2 + oc % 2)]
                    if oc < 2:
                        self.W(Te, "tensor", ("vs", self.cnt["vs"]))
                    self.W(Te, "tensor", tok.get("rope%d" % ((oc - 2) % 8)))
                else:
                    prot = PS["p1"]
                    self.W(Te, "tensor", tok.get("rope%d" % ((oc - 1) % 8)))
                rp = S["tiny"][:, 0:128] if layer == 0 else S["tiny"][:, 128:256]
                tok["rotmm%d" % oc] = self.mm(
                    Te, prot[:, :], rp, S["ev"][:, 512 * oc:512 * oc + 512],
                    start=True, stop=True, inc=True)

            @self.sec("vector")
            def _(V):
                self.W(V, "vector", tok["rotmm%d" % oc])
                self.W(V, "vector", tok["xnl"])
                prot = PS["p%d" % (2 + oc % 2)] if layer == 0 else PS["p1"]
                slot = S["ev"][:, 512 * oc:512 * oc + 512]
                t1 = S["xt"][:, 0:512]
                t2 = S["xt"][:, 512:1024]
                self.W(V, "vector", tok.get("qkw%d" % oc))
                self.op(V, "vector", "vs", "tensor_tensor", t1, slot,
                        S["tab"][:, 0:SP], Alu.mult)
                self.op(V, "vector", "vs", "tensor_tensor", t2, prot[:, :],
                        S["tab"][:, SP:2 * SP], Alu.mult)
                dst = (S["att"] if layer == 0 else S["wD"])[:, 512 * oc:512 * oc + 512]
                tok["rope%d" % oc] = self.op(V, "vector", "vs", "tensor_add",
                                             dst, t1, t2)

            @self.sec("scalar")
            def _(Sc):
                self.W(Sc, "scalar", tok["rope%d" % oc])
                if layer == 0 and sp == 0:
                    self.W(Sc, "scalar", tok.get("wDl"))
                ssrc = (S["att"] if layer == 0 else S["wD"])[:, 512 * oc:512 * oc + 512]
                if isq:
                    tok["qkw%d" % oc] = self.dma(
                        Sc, "dt", H["q0h"].ap()[128 * oc:128 * oc + 128,
                                                SP * sp:SP * sp + SP], ssrc)
                else:
                    kt = oc - 4
                    tok["qkw%d" % oc] = self.dma(
                        Sc, "dt", H["kh"].ap()[:, 2048 * sp + 512 * kt:
                                               2048 * sp + 512 * kt + 512], ssrc)

        if layer == 0:
            for pr in range(2):
                @self.sec("tensor")
                def _(Te):
                    self.W(Te, "tensor", tok["rope%d" % (4 + 2 * pr)])
                    self.W(Te, "tensor", tok["rope%d" % (5 + 2 * pr)])
                    ps = PS["pA"]
                    self.W(Te, "tensor", tok.get("krev%d" % ((pr - 1) % 2)))
                    if pr == 0:
                        self.W(Te, "tensor", tok.get("infog0"))
                        self.W(Te, "tensor", tok.get("infog1"))
                    self.mm(Te, ps[0:64, 0:512], S["tiny"][:, 256:320],
                            S["att"][:, 512 * (4 + 2 * pr):512 * (4 + 2 * pr) + 512],
                            start=True, stop=True, tp=(0, 0))
                    tok["krmm%d" % pr] = self.mm(
                        Te, ps[64:128, 0:512], S["tiny"][:, 256:320],
                        S["att"][:, 512 * (5 + 2 * pr):512 * (5 + 2 * pr) + 512],
                        start=True, stop=True, tp=(0, 64), inc=True)

                @self.sec("vector")
                def _(V):
                    self.W(V, "vector", tok["krmm%d" % pr])
                    tok["krev%d" % pr] = self.op(
                        V, "vector", "vs", "tensor_copy",
                        S["kr"][:, T * pr + SP * sp:T * pr + SP * sp + SP],
                        PS["pA"][:, 0:512])

            self.info_span(sp)

        for st in range(4):
            sig = 4 * sp + st

            @self.sec("tensor")
            def _(Te):
                self.W(Te, "tensor", tok["xnl"])
                self.W(Te, "tensor", tok["lw"])
                ps = PS["p%d" % (st % 2)]
                self.W(Te, "tensor", tok.get("vev%d" % ((st - 2) % 4)))
                if layer == 1 and st < 2:
                    self.W(Te, "tensor", ("vs", self.cnt["vs"]))
                for cc in range(NCH):
                    self.mm(Te, ps[:, :],
                            S["xnsp"][:, xb + SP * cc + 128 * st:
                                      xb + SP * cc + 128 * st + 128],
                            S["wC"][:, 512 * cc:512 * cc + 512],
                            start=(cc == 0), stop=(cc == NCH - 1),
                            inc=(cc == NCH - 1))
                tok["vmm%d" % st] = ("ts", self.cnt["ts"])
                if layer == 0:
                    pvr = PS["p%d" % (2 + st % 2)]
                    self.W(Te, "tensor", tok.get("vrev%d" % ((st - 2) % 4)))
                    if st < 2:
                        self.W(Te, "tensor", tok.get("infos"))
                    self.W(Te, "tensor", tok.get("wDl"))
                    for cc in range(NCH):
                        self.mm(Te, pvr[:, 0:256],
                                S["xnsp"][:, xb + SP * cc + 128 * st:
                                          xb + SP * cc + 128 * st + 128],
                                S["wD"][:, 256 * cc:256 * cc + 256],
                                start=(cc == 0), stop=(cc == NCH - 1),
                                inc=(cc == NCH - 1))
                    tok["vrmm%d" % st] = ("ts", self.cnt["ts"])

            @self.sec("vector")
            def _(V):
                self.W(V, "vector", tok["vmm%d" % st])
                ps = PS["p%d" % (st % 2)]
                varena = S["cx"] if layer == 0 else S["wD"]
                vbuf = varena[:, 1040 * (st % 2):1040 * (st % 2) + 520]
                if layer == 1 and st < 2:
                    self.W(V, "vector", self.qall("dt"))
                self.W(V, "vector", tok.get("vw%d" % ((st - 2) % 4)))
                dst = vbuf.rearrange("p (h e) -> p h e", e=65)[:, :, 0:64]
                srcv = ps[:, :].rearrange("p (h e) -> p h e", e=64)
                if layer == 0:
                    self.op(V, "vector", "vs", "tensor_copy", dst, srcv)
                else:
                    bv = S["nbc"][:, SP:2 * SP].rearrange("p (h e) -> p h e", e=64)
                    self.op(V, "vector", "vs", "tensor_tensor", dst, srcv, bv, Alu.add)
                tok["vev%d" % st] = self.op(
                    V, "vector", "vs", "memset",
                    vbuf.rearrange("p (h e) -> p h e", e=65)[:, :, 64:65], 1.0)
                if layer == 0:
                    self.W(V, "vector", tok["vrmm%d" % st])
                    pvr = PS["p%d" % (2 + st % 2)]
                    dvr = S["vr"][:, 260 * sig:260 * sig + 260].rearrange(
                        "p (h e) -> p h e", e=65)[:, :, 0:64]
                    svr = pvr[:, 0:256].rearrange("p (h e) -> p h e", e=64)
                    self.op(V, "vector", "vs", "tensor_copy", dvr, svr)
                    tok["vrev%d" % st] = self.op(
                        V, "vector", "vs", "memset",
                        S["vr"][:, 260 * sig:260 * sig + 260].rearrange(
                            "p (h e) -> p h e", e=65)[:, :, 64:65], 1.0)

            @self.sec("scalar")
            def _(Sc):
                self.W(Sc, "scalar", tok["vev%d" % st])
                varena = S["cx"] if layer == 0 else S["wD"]
                vbuf = varena[:, 1040 * (st % 2):1040 * (st % 2) + 520]
                tok["vw%d" % st] = self.dma(
                    Sc, "dt", H["vh"].ap()[:, 520 * sig:520 * sig + 520], vbuf)

        tok["qkvdone%d" % sp] = [("ts", self.cnt["ts"]), ("vs", self.cnt["vs"]),
                                 ("ss", self.cnt["ss"])]
        tok["ropedone%d" % sp] = ("vs", self.cnt["vs"])

    def info_span(self, sp):
        S, I, H, PS, tok = self.S, self.I, self.Hh, self.PS, self.tok
        xb = 0

        for cc in range(NCH):
            @self.sec("gpsimd")
            def _(G):
                self.W(G, "gpsimd", tok.get("infomm_c%d" % ((cc - 2) % NCH)))
                tok["w1l%d" % cc] = self.dma(
                    G, "dg", S["w1b"][:, 512 * (cc % 2):512 * (cc % 2) + 512],
                    I["w1T"][128 * cc:128 * cc + 128, :])

            @self.sec("tensor")
            def _(Te):
                self.W(Te, "tensor", tok["w1l%d" % cc])
                self.W(Te, "tensor", tok["xnl"])
                if cc == 0:
                    self.W(Te, "tensor", tok.get("infog3"))
                    self.W(Te, "tensor", tok.get("krev0"))
                    self.W(Te, "tensor", tok.get("krev1"))
                wchunk = S["w1b"][:, 512 * (cc % 2):512 * (cc % 2) + 512]
                for fc in range(4):
                    ps = PS["pA" if fc < 2 else "pB"]
                    self.mm(Te, ps[:, 512 * (fc % 2):512 * (fc % 2) + 512],
                            wchunk[:, 128 * fc:128 * fc + 128],
                            S["xnsp"][:, xb + SP * cc:xb + SP * cc + SP],
                            start=(cc == 0), stop=(cc == NCH - 1),
                            inc=(fc == 3))
                tok["infomm_c%d" % cc] = ("ts", self.cnt["ts"])
        tok["infomm"] = ("ts", self.cnt["ts"])

        for fc in range(4):
            @self.sec("scalar")
            def _(Sc):
                self.W(Sc, "scalar", tok["infomm"])
                ps = PS["pA" if fc < 2 else "pB"]
                tok["infog%d" % fc] = self.op(
                    Sc, "scalar", "ss", "activation",
                    S["hsp"][:, 512 * fc:512 * fc + 512],
                    ps[:, 512 * (fc % 2):512 * (fc % 2) + 512], Act.Gelu)

        @self.sec("tensor")
        def _(Te):
            for fc in range(4):
                self.W(Te, "tensor", tok["infog%d" % fc])
            self.W(Te, "tensor", tok["consts"])
            self.W(Te, "tensor", tok["tiny"])
            ps = PS["p2"]
            self.W(Te, "tensor", tok.get("infoev"))
            self.W(Te, "tensor", tok.get("rope6"))
            for fc in range(4):
                self.mm(Te, ps[32 * fc:32 * fc + 1, :],
                        S["tiny"][:, 520 + fc:521 + fc],
                        S["hsp"][:, 512 * fc:512 * fc + 512],
                        start=True, stop=True, tp=(0, 32 * fc),
                        inc=(fc == 3))
            tok["infomm2"] = ("ts", self.cnt["ts"])

        @self.sec("vector")
        def _(V):
            self.W(V, "vector", tok["infomm2"])
            ps = PS["p2"]
            for fc in range(4):
                self.op(V, "vector", "vs", "tensor_copy",
                        S["svec"][:, 512 * fc:512 * fc + 512], ps[32 * fc:32 * fc + 1, :])
            self.op(V, "vector", "vs", "tensor_add", S["svec"][:, 0:512],
                    S["svec"][:, 0:512], S["svec"][:, 512:1024])
            self.op(V, "vector", "vs", "tensor_add", S["svec"][:, 1024:1536],
                    S["svec"][:, 1024:1536], S["svec"][:, 1536:2048])
            tok["infos"] = self.op(V, "vector", "vs", "tensor_add",
                                   S["svec"][:, 0:512], S["svec"][:, 0:512],
                                   S["svec"][:, 1024:1536])

        @self.sec("scalar")
        def _(Sc):
            self.W(Sc, "scalar", tok["infos"])
            tok["infosg"] = self.op(Sc, "scalar", "ss", "activation",
                                    S["svec"][:, 512:1024], S["svec"][:, 0:512],
                                    Act.Sigmoid)

        @self.sec("vector")
        def _(V):
            self.W(V, "vector", tok["infosg"])
            tok["infoev"] = self.op(V, "vector", "vs", "tensor_scalar",
                                    S["gate"][0:1, SP * sp:SP * sp + SP],
                                    S["svec"][:, 512:1024], 0.75, 1.0,
                                    Alu.is_gt, Alu.mult)

    def ffn0_span(self, sp):
        S, I, H, PS, tok = self.S, self.I, self.Hh, self.PS, self.tok
        xb = 0
        agt = self.tok["ag0"]

        @self.sec("sync")
        def _(SY):
            self.W(SY, "sync", agt)
            self.W(SY, "sync", tok.get("ffdone%d" % (sp - 1)))
            if sp == 0:
                self.W(SY, "sync", tok.get("qkvdone%d" % (NSP - 1)))
                self.W(SY, "sync", tok.get("aLdone%d" % (NSP - 1)))
            lt = []
            for cc in range(NCH):
                lt.append(self.dma(SY, "dl",
                                   S["xnsp"][:, xb + SP * cc:xb + SP * cc + SP],
                                   H["ag_out"].ap()[SP * sp:SP * sp + SP,
                                                    128 * cc:128 * cc + 128],
                                   transpose=True))
            tok["fxnl"] = lt

        for fs4 in range(4):
            @self.sec("sync")
            def _(SY):
                self.W(SY, "sync", tok["qzwup15"])
                self.W(SY, "sync", tok.get("upmm%d" % (fs4 - 1)))
                lt = []
                for cc in range(NCH):
                    lt.append(self.dma(
                        SY, "dl", S["wA"][:, 512 * cc:512 * cc + 512],
                        H["wqupT"].ap()[512 * fs4:512 * fs4 + 512,
                                        128 * cc:128 * cc + 128],
                        transpose=True))
                tok["upwl"] = lt

            ss_p = self.cnt["ss"]
            t_gel = {fi: ss_p + fi + 1 for fi in range(4)}
            ts_p = self.cnt["ts"]
            t_up = {fi: ts_p + 16 * (fi + 1) for fi in range(4)}

            @self.sec("tensor")
            def _(Te):
                self.W(Te, "tensor", tok["upwl"])
                self.W(Te, "tensor", tok["fxnl"])
                if sp == 0 and fs4 == 0:
                    self.W(Te, "tensor", ("vs", self.cnt["vs"]))
                for fi in range(4):
                    fcg = 4 * fs4 + fi
                    ps = PS["p%d" % (fi % 2)]
                    if fi >= 2:
                        self.W(Te, "tensor", ("ss", t_gel[fi - 2]))
                    elif fs4 > 0:
                        self.W(Te, "tensor", ("ss", tok["gelt%d" % (4 * fs4 - 2 + fi)][1]))
                    for cc in range(NCH):
                        v = self.mm(Te, ps[:, :],
                                    S["wA"][:, 512 * cc + 128 * fi:512 * cc + 128 * fi + 128],
                                    S["xnsp"][:, xb + SP * cc:xb + SP * cc + SP],
                                    start=(cc == 0), stop=(cc == NCH - 1), inc=True)
                    assert v[1] == t_up[fi]

            @self.sec("scalar")
            def _(Sc):
                self.W(Sc, "scalar", tok["bupP"])
                for fi in range(4):
                    fcg = 4 * fs4 + fi
                    self.W(Sc, "scalar", ("ts", t_up[fi]))
                    ps = PS["p%d" % (fi % 2)]
                    v = self.op(Sc, "scalar", "ss", "activation",
                                S["hsp"][:, 512 * fcg:512 * fcg + 512], ps[:, :],
                                Act.Gelu_apprx_tanh, S["bias"][:, 20 + fcg:21 + fcg])
                    assert v[1] == t_gel[fi]
                    tok["gelt%d" % fcg] = ("ss", v[1])
            tok["upmm%d" % fs4] = ("ts", self.cnt["ts"])
        tok["gel15"] = tok["gelt15"]

        for ns in range(4):
            @self.sec("sync")
            def _(SY):
                self.W(SY, "sync", tok["qzwdn15"])
                self.W(SY, "sync", tok.get("dnmm%d" % (ns - 1)))
                lt = []
                for fc in range(NCH):
                    lt.append(self.dma(
                        SY, "dl", S["wB"][:, 512 * fc:512 * fc + 512],
                        H["wqdnT"].ap()[512 * ns:512 * ns + 512,
                                        128 * fc:128 * fc + 128],
                        transpose=True))
                tok["dnwl"] = lt

            for tt in range(4):
                @self.sec("tensor")
                def _(Te):
                    self.W(Te, "tensor", tok["dnwl"])
                    self.W(Te, "tensor", tok["gel15"])
                    ps = PS["p%d" % (tt % 2)]
                    pv = (ns, tt - 2) if tt >= 2 else ((ns - 1) % 4, tt + 2)
                    self.W(Te, "tensor", tok.get("yev%d_%d" % pv))
                    if ns == 0 and tt < 2:
                        self.W(Te, "tensor", ("ss", self.cnt["ss"]))
                    for fc in range(NCH):
                        self.mm(Te, ps[:, :],
                                S["hsp"][:, 512 * fc + 128 * tt:512 * fc + 128 * tt + 128],
                                S["wB"][:, 512 * fc:512 * fc + 512],
                                start=(fc == 0), stop=(fc == NCH - 1),
                                inc=(fc == NCH - 1))
                    tok["dnps"] = ("ts", self.cnt["ts"])

                @self.sec("vector")
                def _(V):
                    self.W(V, "vector", tok["dnps"])
                    ps = PS["p%d" % (tt % 2)]
                    slot = S["ev"][:, 2048 + 512 * tt:2048 + 512 * tt + 512]
                    pv = (ns, tt - 2) if tt >= 2 else ((ns - 1) % 4, tt + 2)
                    self.W(V, "vector", tok.get("yw%d_%d" % pv, self.qall("dt")))
                    tok["yev%d_%d" % (ns, tt)] = self.op(
                        V, "vector", "vs", "tensor_copy", slot, ps[:, :])

                @self.sec("scalar")
                def _(Sc):
                    self.W(Sc, "scalar", tok["yev%d_%d" % (ns, tt)])
                    tok["yw%d_%d" % (ns, tt)] = self.dma(
                        Sc, "dt",
                        H["rs_in"].ap()[SP * sp + 128 * tt:SP * sp + 128 * tt + 128,
                                        512 * ns:512 * ns + 512],
                        S["ev"][:, 2048 + 512 * tt:2048 + 512 * tt + 512])
            tok["dnmm%d" % ns] = ("ts", self.cnt["ts"])

        tok["ffdone%d" % sp] = [("ts", self.cnt["ts"]), ("vs", self.cnt["vs"]),
                                ("ss", self.cnt["ss"])]

    def attn0_local_span(self, tau):
        S, I, H, PS, tok = self.S, self.I, self.Hh, self.PS, self.tok
        t0 = SP * tau
        sigs = list(range(max(0, 4 * tau - 4), 4 * tau + 4))
        spb = sorted(set(s // 4 for s in sigs))
        spb_ix = {s: i for i, s in enumerate(spb)}

        @self.sec("sync")
        def _(SY):
            self.W(SY, "sync", tok.get("aLdone%d" % (tau - 1)))
            if tau == 0:
                self.W(SY, "sync", tok.get("qkvdone%d" % (NSP - 1)))
            self.W(SY, "sync", self.qall("dt"))
            lt = []
            for i in range(4):
                lt.append(self.dma(SY, "dl", S["qsp"][:, 512 * i:512 * i + 512],
                                   H["q0h"].ap()[128 * i:128 * i + 128, t0:t0 + SP]))
            for bi, sb_ in enumerate(spb):
                for kt in range(4):
                    lt.append(self.dma(
                        SY, "dl",
                        S["kl"][:, 2048 * bi + 512 * kt:2048 * bi + 512 * kt + 512],
                        H["kh"].ap()[:, 2048 * sb_ + 512 * kt:2048 * sb_ + 512 * kt + 512]))
            for i, s in enumerate(sigs):
                lt.append(self.dma(SY, "dl", S["vl"][:, 520 * i:520 * i + 520],
                                   H["vh"].ap()[:, 520 * s:520 * s + 520]))
            tok["ql"] = lt

        for h in range(4):
            pairs = [sigs[2 * i:2 * i + 2] for i in range((len(sigs) + 1) // 2)]
            n = len(pairs)
            ts_p, ss_p, vs_p = self.cnt["ts"], self.cnt["ss"], self.cnt["vs"]
            t_sc, t_av, t_exp, t_msk = {}, {}, {}, {}
            for k in range(n + 2):
                if k < n:
                    ts_p += 1
                    t_sc[k] = ts_p
                if 0 <= k - 2 < n:
                    ts_p += 1
                    t_av[k - 2] = ts_p
            for k in range(n):
                ss_p += 1
                t_exp[k] = ss_p
            for k in range(n):
                vs_p += len(pairs[k])
                t_msk[k] = vs_p

            @self.sec("tensor")
            def _(Te):
                self.W(Te, "tensor", tok["ql"])
                if h == 0:
                    self.W(Te, "tensor", ("vs", self.cnt["vs"]))
                else:
                    self.W(Te, "tensor", tok["ctxl%d" % (h - 1)])
                def emit_sc(k):
                    psc = PS["pA" if k % 2 == 0 else "pB"]
                    if k >= 2:
                        self.W(Te, "tensor", ("ss", t_exp[k - 2]))
                    for k_, s in enumerate(pairs[k]):
                        sb_, st_ = s // 4, s % 4
                        ksl = S["kl"][:, 2048 * spb_ix[sb_] + 512 * h + 128 * st_:
                                      2048 * spb_ix[sb_] + 512 * h + 128 * st_ + 128]
                        v = self.mm(Te, psc[:, 512 * k_:512 * k_ + 512], ksl,
                                    S["qsp"][:, 512 * h:512 * h + 512],
                                    start=True, stop=True,
                                    inc=(k_ == len(pairs[k]) - 1))
                    assert v[1] == t_sc[k]
                def emit_av(k):
                    self.W(Te, "tensor", ("vs", t_msk[k]))
                    base = 2048 if k % 2 == 0 else 3072
                    for k_, s in enumerate(pairs[k]):
                        vi = sigs.index(s)
                        va = S["vl"][:, 520 * vi + 130 * h:520 * vi + 130 * h + 65]
                        vb = S["vl"][:, 520 * vi + 130 * h + 65:520 * vi + 130 * h + 130]
                        first = (k == 0 and k_ == 0)
                        last = (k == n - 1 and k_ == len(pairs[k]) - 1)
                        self.mm(Te, PS["p0"][0:65, :], va,
                                S["att"][:, base + 512 * k_:base + 512 * k_ + 512],
                                start=first, stop=last)
                        v = self.mm(Te, PS["p1"][0:65, :], vb,
                                    S["att"][:, base + 512 * k_:base + 512 * k_ + 512],
                                    start=first, stop=last,
                                    inc=(k_ == len(pairs[k]) - 1))
                    assert v[1] == t_av[k]
                for k in range(n + 2):
                    if k < n:
                        emit_sc(k)
                    if k - 2 >= 0:
                        emit_av(k - 2)
            tok["lav%d" % h] = ("ts", self.cnt["ts"])

            @self.sec("scalar")
            def _(Sc):
                self.W(Sc, "scalar", tok["consts"])
                for k in range(n):
                    self.W(Sc, "scalar", ("ts", t_sc[k]))
                    if k >= 2:
                        self.W(Sc, "scalar", ("ts", t_av[k - 2]))
                    w = 512 * len(pairs[k])
                    base = 0 if k % 2 == 0 else 1024
                    psc = PS["pA" if k % 2 == 0 else "pB"]
                    v = self.op(Sc, "scalar", "ss", "activation",
                                S["att"][:, base:base + w], psc[:, 0:w], Act.Exp,
                                S["bias"][:, BC["mbl"]:BC["mbl"] + 1],
                                S["bias"][:, BC["sc0"]:BC["sc0"] + 1])
                    assert v[1] == t_exp[k]

            @self.sec("vector")
            def _(V):
                self.W(V, "vector", tok["msk"])
                for k in range(n):
                    self.W(V, "vector", ("ss", t_exp[k]))
                    if k >= 2:
                        self.W(V, "vector", ("ts", t_av[k - 2]))
                    base = 0 if k % 2 == 0 else 1024
                    mb = 2048 if k % 2 == 0 else 3072
                    for k_, s in enumerate(pairs[k]):
                        qd = s - 4 * tau + 4
                        v = self.op(V, "vector", "vs", "tensor_tensor",
                                    S["att"][:, mb + 512 * k_:mb + 512 * k_ + 512],
                                    S["att"][:, base + 512 * k_:base + 512 * k_ + 512],
                                    S["msk"][:, SP * qd:SP * qd + SP], Alu.mult)
                    assert v[1] == t_msk[k]

            @self.sec("vector")
            def _(V):
                self.W(V, "vector", tok["lav%d" % h])
                tok["nl%d" % h] = self.op(V, "vector", "vs", "reciprocal",
                                          S["svec"][:, 0:512], PS["p0"][64:65, :])

            @self.sec("scalar")
            def _(Sc):
                self.W(Sc, "scalar", tok["nl%d" % h])
                tok["nlw%d" % h] = self.dma(Sc, "dt", H["nscr"].ap()[0:1, 0:512],
                                            S["svec"][:, 0:512])

            @self.sec("gpsimd")
            def _(G):
                self.W(G, "gpsimd", tok["nlw%d" % h])
                tok["nlb%d" % h] = self.dma(
                    G, "dg", S["nbc"][:, 0:512],
                    H["nscr"].ap()[0:1, 0:512].partition_broadcast(128).squeeze(1))

            @self.sec("vector")
            def _(V):
                self.W(V, "vector", tok["nlb%d" % h])
                self.W(V, "vector", tok.get("ctlw%d" % (tau - 1)))
                self.op(V, "vector", "vs", "tensor_tensor",
                        S["ctl"][0:64, 512 * h:512 * h + 512], PS["p0"][0:64, :],
                        S["nbc"][0:64, 0:512], Alu.mult)
                tok["ctxl%d" % h] = self.op(
                    V, "vector", "vs", "tensor_tensor",
                    S["ctl"][64:128, 512 * h:512 * h + 512], PS["p1"][0:64, :],
                    S["nbc"][64:128, 0:512], Alu.mult)

        @self.sec("scalar")
        def _(Sc):
            self.W(Sc, "scalar", tok["ctxl3"])
            tok["ctlw%d" % tau] = self.dma(
                Sc, "dt", H["ctlh"].ap()[:, 2048 * tau:2048 * tau + 2048],
                S["ctl"][:, :])
        tok["aLdone%d" % tau] = ("ts", self.cnt["ts"])

    def attn0_global_span(self, tau):
        S, I, H, PS, tok = self.S, self.I, self.Hh, self.PS, self.tok
        t0 = SP * tau

        @self.sec("sync")
        def _(SY):
            self.W(SY, "sync", tok.get("aGdone%d" % (tau - 1)))
            if tau == 0:
                self.W(SY, "sync", tok.get("aLdone%d" % (NSP - 1)))
                self.W(SY, "sync", tok.get("wo"))
            self.W(SY, "sync", self.qall("dt"))
            lt = []
            for i in range(4):
                lt.append(self.dma(SY, "dl", S["qsp"][:, 512 * i:512 * i + 512],
                                   H["q0h"].ap()[128 * i:128 * i + 128, t0:t0 + SP]))
            lt.append(self.dma(SY, "dl", S["ctl"][:, :],
                               H["ctlh"].ap()[:, 2048 * tau:2048 * tau + 2048]))
            tok["qgl"] = lt

        for pr in range(2):
            @self.sec("tensor")
            def _(Te):
                self.W(Te, "tensor", tok["qgl"])
                self.W(Te, "tensor", tok["tiny"])
                ps = PS["p2"]
                self.W(Te, "tensor", ("vs", self.cnt["vs"]))
                self.mm(Te, ps[0:64, :], S["tiny"][:, 256:320],
                        S["qsp"][:, 512 * (2 * pr):512 * (2 * pr) + 512],
                        start=True, stop=True, tp=(0, 0))
                tok["qrmm%d" % pr] = self.mm(
                    Te, ps[64:128, :], S["tiny"][:, 256:320],
                    S["qsp"][:, 512 * (2 * pr + 1):512 * (2 * pr + 1) + 512],
                    start=True, stop=True, tp=(0, 64), inc=True)

            @self.sec("vector")
            def _(V):
                self.W(V, "vector", tok["qrmm%d" % pr])
                tok["qrev%d" % pr] = self.op(V, "vector", "vs", "tensor_copy",
                                             S["qr"][:, 512 * pr:512 * pr + 512],
                                             PS["p2"][:, :])

        for pr in range(2):
            n = 32
            ts_p, ss_p = self.cnt["ts"], self.cnt["ss"]
            t_sc, t_av, t_exp = {}, {}, {}
            for k in range(n + 2):
                if k < n:
                    ts_p += 1
                    t_sc[k] = ts_p
                if 0 <= k - 2 < n:
                    ts_p += 1
                    t_av[k - 2] = ts_p
            for k in range(n):
                ss_p += 1
                t_exp[k] = ss_p

            @self.sec("tensor")
            def _(Te):
                self.W(Te, "tensor", tok["qrev0"])
                self.W(Te, "tensor", tok["qrev1"])
                self.W(Te, "tensor", ("vs", self.cnt["vs"]))
                def emit_sc(k):
                    psc = PS["pA" if k % 2 == 0 else "pB"]
                    if k >= 2:
                        self.W(Te, "tensor", ("ss", t_exp[k - 2]))
                    else:
                        self.W(Te, "tensor", ("ss", self.cnt["ss"]))
                    kA = S["kr"][0:64, T * pr + 128 * k:T * pr + 128 * k + 128]
                    kB = S["kr"][64:128, T * pr + 128 * k:T * pr + 128 * k + 128]
                    self.mm(Te, psc[:, 0:512], kA,
                            S["qr"][0:64, 512 * pr:512 * pr + 512],
                            start=True, stop=True)
                    v = self.mm(Te, psc[:, 512:1024], kB,
                                S["qr"][64:128, 512 * pr:512 * pr + 512],
                                start=True, stop=True, inc=True)
                    assert v[1] == t_sc[k]
                def emit_av(k):
                    self.W(Te, "tensor", ("ss", t_exp[k]))
                    base = 0 if k % 2 == 0 else 1024
                    hA, hB = 2 * pr, 2 * pr + 1
                    first, last = (k == 0), (k == 31)
                    self.mm(Te, PS["p2"][0:65, :],
                            S["vr"][:, 260 * k + 65 * hA:260 * k + 65 * hA + 65],
                            S["att"][:, base:base + 512],
                            start=first, stop=last)
                    v = self.mm(Te, PS["p3"][0:65, :],
                                S["vr"][:, 260 * k + 65 * hB:260 * k + 65 * hB + 65],
                                S["att"][:, base + 512:base + 1024],
                                start=first, stop=last, inc=True)
                    assert v[1] == t_av[k]
                for k in range(n + 2):
                    if k < n:
                        emit_sc(k)
                    if k - 2 >= 0:
                        emit_av(k - 2)
            tok["gav%d" % pr] = ("ts", self.cnt["ts"])

            @self.sec("scalar")
            def _(Sc):
                for k in range(n):
                    self.W(Sc, "scalar", ("ts", t_sc[k]))
                    if k >= 2:
                        self.W(Sc, "scalar", ("ts", t_av[k - 2]))
                    base = 0 if k % 2 == 0 else 1024
                    psc = PS["pA" if k % 2 == 0 else "pB"]
                    v = self.op(Sc, "scalar", "ss", "activation",
                                S["att"][:, base:base + 1024], psc[:, :], Act.Exp,
                                S["bias"][:, BC["mbg"]:BC["mbg"] + 1],
                                S["bias"][:, BC["scg"]:BC["scg"] + 1])
                    assert v[1] == t_exp[k]

            for hi in range(2):
                h = 2 * pr + hi
                pg = PS["p2"] if hi == 0 else PS["p3"]

                @self.sec("vector")
                def _(V):
                    self.W(V, "vector", tok["gav%d" % pr])
                    self.op(V, "vector", "vs", "reciprocal", S["svec"][:, 0:512],
                            pg[64:65, :])
                    tok["ng%d" % h] = self.op(
                        V, "vector", "vs", "tensor_tensor", S["svec"][:, 512:1024],
                        S["svec"][:, 0:512], S["gate"][0:1, t0:t0 + SP], Alu.mult)

                @self.sec("scalar")
                def _(Sc):
                    self.W(Sc, "scalar", tok["ng%d" % h])
                    tok["ngw%d" % h] = self.dma(Sc, "dt", H["nscr"].ap()[0:1, SP:2 * SP],
                                                S["svec"][:, 512:1024])

                @self.sec("gpsimd")
                def _(G):
                    self.W(G, "gpsimd", tok["ngw%d" % h])
                    tok["ngb%d" % h] = self.dma(
                        G, "dg", S["nbc"][:, 0:512],
                        H["nscr"].ap()[0:1, SP:2 * SP].partition_broadcast(128).squeeze(1))

                @self.sec("vector")
                def _(V):
                    self.W(V, "vector", tok["ngb%d" % h])
                    tok["cgs%d" % hi] = self.op(
                        V, "vector", "vs", "tensor_tensor",
                        S["ev"][0:64, 512 * hi:512 * hi + 512],
                        pg[0:64, :], S["nbc"][0:64, 0:512], Alu.mult)

            for hi in range(2):
                h = 2 * pr + hi

                @self.sec("tensor")
                def _(Te):
                    self.W(Te, "tensor", tok["cgs%d" % hi])
                    ps = PS["pA"][:, 512 * hi:512 * hi + 512]
                    self.W(Te, "tensor", ("ss", self.cnt["ss"]))
                    self.W(Te, "tensor", tok.get("ctxf%d" % (h - 2)))
                    tok["uomm%d" % h] = self.mm(
                        Te, ps, S["tiny"][0:64, 320:448],
                        S["ev"][0:64, 512 * hi:512 * hi + 512],
                        start=True, stop=True, inc=True)

                @self.sec("vector")
                def _(V):
                    self.W(V, "vector", tok["uomm%d" % h])
                    ps = PS["pA"][:, 512 * hi:512 * hi + 512]
                    self.W(V, "vector", tok.get("womm_t"))
                    tok["ctxf%d" % h] = self.op(
                        V, "vector", "vs", "tensor_tensor",
                        S["cx"][:, 512 * h:512 * h + 512],
                        S["ctl"][:, 512 * h:512 * h + 512], ps, Alu.add)

        for tt in range(4):
            for ns in range(4):
                @self.sec("tensor")
                def _(Te):
                    for h in range(4):
                        self.W(Te, "tensor", tok["ctxf%d" % h])
                    self.W(Te, "tensor", tok["wo"])
                    ps = PS["pB"][:, 512 * (ns % 2):512 * (ns % 2) + 512]
                    self.W(Te, "tensor", ("ss", self.cnt["ss"]))
                    pv = (ns - 2, tt) if ns >= 2 else ((ns + 2), tt - 1)
                    self.W(Te, "tensor", tok.get("woev%d_%d" % (tt, ns - 2) if ns >= 2
                                                 else "woev%d_%d" % (tt - 1, ns + 2)))
                    for cc in range(4):
                        self.mm(Te, ps,
                                S["cx"][:, 512 * cc + 128 * tt:512 * cc + 128 * tt + 128],
                                S["wC"][:, 2048 * cc + 512 * ns:2048 * cc + 512 * ns + 512],
                                start=(cc == 0), stop=(cc == 3), inc=(cc == 3))
                    tok["wops%d_%d" % (tt, ns)] = ("ts", self.cnt["ts"])

                @self.sec("vector")
                def _(V):
                    self.W(V, "vector", tok["wops%d_%d" % (tt, ns)])
                    self.W(V, "vector", tok.get("wow%d_%d" % (tt - 1, ns)))
                    tok["woev%d_%d" % (tt, ns)] = self.op(
                        V, "vector", "vs", "tensor_copy",
                        S["ev"][:, 512 * ns:512 * ns + 512],
                        PS["pB"][:, 512 * (ns % 2):512 * (ns % 2) + 512])

                @self.sec("gpsimd")
                def _(G):
                    self.W(G, "gpsimd", tok["woev%d_%d" % (tt, ns)])
                    if tt == 0 and ns == 0:
                        self.W(G, "gpsimd", self.qall("dt"))
                    tok["wow%d_%d" % (tt, ns)] = self.dma(
                        G, "dg",
                        H["rs_in"].ap()[t0 + 128 * tt:t0 + 128 * tt + 128,
                                        512 * ns:512 * ns + 512],
                        S["ev"][:, 512 * ns:512 * ns + 512], accum=True)
        tok["womm_t"] = ("ts", self.cnt["ts"])
        tok["aGdone%d" % tau] = ("ts", self.cnt["ts"])

    def attn1_span(self, tau):
        S, I, H, PS, tok = self.S, self.I, self.Hh, self.PS, self.tok
        t0 = SP * tau
        nsig = 4 * tau + 4

        @self.sec("sync")
        def _(SY):
            self.W(SY, "sync", tok.get("a1done%d" % (tau - 1)))
            if tau == 0:
                self.W(SY, "sync", tok.get("qkvdone%d" % (NSP - 1)))
                self.W(SY, "sync", tok.get("wo"))
            self.W(SY, "sync", self.qall("dt"))
            lt = []
            for i in range(4):
                lt.append(self.dma(SY, "dl", S["qsp"][:, 512 * i:512 * i + 512],
                                   H["q0h"].ap()[128 * i:128 * i + 128, t0:t0 + SP]))
            tok["ql"] = lt

        for grp in range(2):
            pcs = [PS["p0"], PS["p1"], PS["p2"], PS["p3"]]
            nblk = (nsig + 7) // 8
            for blkb in range(nblk):
                lo, hi = 8 * blkb, min(8 * blkb + 8, nsig)

                @self.sec("sync")
                def _(SY):
                    if blkb == 0:
                        self.W(SY, "sync", ("ts", self.cnt["ts"]))
                    self.W(SY, "sync", tok.get("l1b%d_%d" % (grp, blkb - 1)))
                    lt = []
                    for i, s in enumerate(range(lo, hi)):
                        lt.append(self.dma(SY, "dl", S["vl1"][:, 520 * i:520 * i + 520],
                                           H["vh"].ap()[:, 520 * s:520 * s + 520]))
                        sb_, st_ = s // 4, s % 4
                        for pri in range(2):
                            pr = 2 * grp + pri
                            lt.append(self.dma(
                                SY, "dl",
                                S["kl1"][:, 1024 * i + 128 * pri:1024 * i + 128 * pri + 128],
                                H["kh"].ap()[:, 2048 * sb_ + 512 * pr + 128 * st_:
                                             2048 * sb_ + 512 * pr + 128 * st_ + 128]))
                    tok["kvl"] = lt

                steps = [(s, pri) for s in range(lo, hi) for pri in range(2)]
                n = len(steps)
                ts_p, ss_p, vs_p = self.cnt["ts"], self.cnt["ss"], self.cnt["vs"]
                t_sc, t_av, t_exp, t_msk = {}, {}, {}, {}
                for k in range(n + 2):
                    if k < n:
                        ts_p += 1
                        t_sc[k] = ts_p
                    if 0 <= k - 2 < n:
                        ts_p += 1
                        t_av[k - 2] = ts_p
                for k in range(n):
                    ss_p += 1
                    t_exp[k] = ss_p
                for k in range(n):
                    s, pri = steps[k]
                    if s >= 4 * tau:
                        vs_p += 2
                        t_msk[k] = vs_p

                @self.sec("tensor")
                def _(Te):
                    self.W(Te, "tensor", tok["kvl"])
                    self.W(Te, "tensor", tok["ql"])
                    if blkb == 0 and grp == 0:
                        self.W(Te, "tensor", ("vs", self.cnt["vs"]))
                    if grp == 1 and blkb == 0:
                        self.W(Te, "tensor", [tok["c1f%d" % hh] for hh in range(4)])
                    def emit_sc(k):
                        s, pri = steps[k]
                        pr = 2 * grp + pri
                        i = s - lo
                        pnm = "pA" if k % 2 == 0 else "pB"
                        psc = PS[pnm]
                        if k >= 2:
                            self.W(Te, "tensor", ("ss", t_exp[k - 2]))
                        else:
                            self.W(Te, "tensor", ("ss", self.cnt["ss"]))
                        ks = S["kl1"][:, 1024 * i + 128 * pri:1024 * i + 128 * pri + 128]
                        self.mm(Te, psc[:, 0:512], ks[0:64, :],
                                S["qsp"][0:64, 512 * pr:512 * pr + 512],
                                start=True, stop=True)
                        v = self.mm(Te, psc[:, 512:1024], ks[64:128, :],
                                    S["qsp"][64:128, 512 * pr:512 * pr + 512],
                                    start=True, stop=True, inc=True)
                        assert v[1] == t_sc[k], (v, t_sc[k])
                    def emit_av(k):
                        s, pri = steps[k]
                        i = s - lo
                        diag = (s >= 4 * tau)
                        base = 0 if k % 2 == 0 else 1024
                        if diag:
                            self.W(Te, "tensor", ("vs", t_msk[k]))
                            asl = 2048 + base
                        else:
                            self.W(Te, "tensor", ("ss", t_exp[k]))
                            asl = base
                        pcA, pcB = pcs[2 * pri], pcs[2 * pri + 1]
                        pr = 2 * grp + pri
                        first, last = (s == 0), (s == nsig - 1)
                        self.mm(Te, pcA[0:65, :],
                                S["vl1"][:, 520 * i + 65 * (2 * pr):
                                        520 * i + 65 * (2 * pr) + 65],
                                S["att"][:, asl:asl + 512],
                                start=first, stop=last)
                        v = self.mm(Te, pcB[0:65, :],
                                    S["vl1"][:, 520 * i + 65 * (2 * pr + 1):
                                            520 * i + 65 * (2 * pr + 1) + 65],
                                    S["att"][:, asl + 512:asl + 1024],
                                    start=first, stop=last, inc=True)
                        assert v[1] == t_av[k], (v, t_av[k])
                    for k in range(n + 2):
                        if k < n:
                            emit_sc(k)
                        if k - 2 >= 0:
                            emit_av(k - 2)

                @self.sec("scalar")
                def _(Sc):
                    for k in range(n):
                        self.W(Sc, "scalar", ("ts", t_sc[k]))
                        if k >= 2:
                            self.W(Sc, "scalar", ("ts", t_av[k - 2]))
                        base = 0 if k % 2 == 0 else 1024
                        pnm = "pA" if k % 2 == 0 else "pB"
                        v = self.op(Sc, "scalar", "ss", "activation",
                                    S["att"][:, base:base + 1024], PS[pnm][:, :],
                                    Act.Exp,
                                    S["bias"][:, BC["mb1"]:BC["mb1"] + 1],
                                    S["bias"][:, BC["sc1"]:BC["sc1"] + 1])
                        assert v[1] == t_exp[k], (v, t_exp[k])

                @self.sec("vector")
                def _(V):
                    self.W(V, "vector", tok["msk"])
                    for k in range(n):
                        s, pri = steps[k]
                        if s < 4 * tau:
                            continue
                        self.W(V, "vector", ("ss", t_exp[k]))
                        if k >= 2:
                            self.W(V, "vector", ("ts", t_av[k - 2]))
                        base = 0 if k % 2 == 0 else 1024
                        qd = s - 4 * tau
                        ms = S["msk"][:, SP * qd:SP * qd + SP]
                        self.op(V, "vector", "vs", "tensor_tensor",
                                S["att"][:, 2048 + base:2048 + base + 512],
                                S["att"][:, base:base + 512], ms, Alu.mult)
                        v = self.op(V, "vector", "vs", "tensor_tensor",
                                    S["att"][:, 2048 + base + 512:2048 + base + 1024],
                                    S["att"][:, base + 512:base + 1024], ms, Alu.mult)
                        assert v[1] == t_msk[k], (v, t_msk[k])

                tok["l1b%d_%d" % (grp, blkb)] = ("ts", self.cnt["ts"])

            for pri in range(2):
                pr = 2 * grp + pri
                for hi2 in range(2):
                    h = 2 * pr + hi2
                    pg = pcs[2 * pri + hi2]

                    @self.sec("vector")
                    def _(V):
                        self.W(V, "vector", ("ts", self.cnt["ts"]))
                        tok["n1%d" % h] = self.op(V, "vector", "vs", "reciprocal",
                                                  S["svec"][:, 0:512], pg[64:65, :])

                    @self.sec("scalar")
                    def _(Sc):
                        self.W(Sc, "scalar", tok["n1%d" % h])
                        tok["n1w%d" % h] = self.dma(Sc, "dt", H["nscr"].ap()[0:1, 0:512],
                                                    S["svec"][:, 0:512])

                    @self.sec("gpsimd")
                    def _(G):
                        self.W(G, "gpsimd", tok["n1w%d" % h])
                        tok["n1b%d" % h] = self.dma(
                            G, "dg", S["nbc"][:, 0:512],
                            H["nscr"].ap()[0:1, 0:512].partition_broadcast(128).squeeze(1))

                    @self.sec("vector")
                    def _(V):
                        self.W(V, "vector", tok["n1b%d" % h])
                        self.W(V, "vector", tok.get("womm_t"))
                        tok["c1f%d" % h] = self.op(
                            V, "vector", "vs", "tensor_tensor",
                            S["cx"][64 * (h % 2):64 * (h % 2) + 64,
                                    512 * (h // 2):512 * (h // 2) + 512],
                            pg[0:64, :], S["nbc"][0:64, 0:512], Alu.mult)

        for tt in range(4):
            for ns in range(4):
                @self.sec("tensor")
                def _(Te):
                    for h in range(8):
                        self.W(Te, "tensor", tok["c1f%d" % h])
                    self.W(Te, "tensor", tok["wo"])
                    ps = PS["pB"][:, 512 * (ns % 2):512 * (ns % 2) + 512]
                    self.W(Te, "tensor", ("ss", self.cnt["ss"]))
                    self.W(Te, "tensor", tok.get("woev%d_%d" % (tt, ns - 2) if ns >= 2
                                                 else "woev%d_%d" % (tt - 1, ns + 2)))
                    for cc in range(4):
                        self.mm(Te, ps,
                                S["cx"][:, 512 * cc + 128 * tt:512 * cc + 128 * tt + 128],
                                S["hsp"][:, 2048 * cc + 512 * ns:2048 * cc + 512 * ns + 512],
                                start=(cc == 0), stop=(cc == 3), inc=(cc == 3))
                    tok["wops%d_%d" % (tt, ns)] = ("ts", self.cnt["ts"])

                @self.sec("vector")
                def _(V):
                    self.W(V, "vector", tok["wops%d_%d" % (tt, ns)])
                    self.W(V, "vector", tok.get("wow%d_%d" % (tt - 1, ns)))
                    tok["woev%d_%d" % (tt, ns)] = self.op(
                        V, "vector", "vs", "tensor_copy",
                        S["ctl"][:, 512 * ns:512 * ns + 512],
                        PS["pB"][:, 512 * (ns % 2):512 * (ns % 2) + 512])

                @self.sec("scalar")
                def _(Sc):
                    self.W(Sc, "scalar", tok["woev%d_%d" % (tt, ns)])
                    tok["wow%d_%d" % (tt, ns)] = self.dma(
                        Sc, "dt",
                        H["rs_in"].ap()[t0 + 128 * tt:t0 + 128 * tt + 128,
                                        512 * ns:512 * ns + 512],
                        S["ctl"][:, 512 * ns:512 * ns + 512])
        tok["womm_t"] = ("ts", self.cnt["ts"])
        tok["a1done%d" % tau] = ("ts", self.cnt["ts"])

    def p6(self):
        S, I, H, PS, tok = self.S, self.I, self.Hh, self.PS, self.tok

        @self.sec("gpsimd")
        def _(G):
            self.W(G, "gpsimd", ("vs", self.cnt["vs"]))
            tok["bdnb"] = self.dma(G, "dg", S["bcst"][:, :],
                                   I["bdn"].partition_broadcast(128).squeeze(1))

        for ti in range(8):
            @self.sec("sync")
            def _(SY):
                self.W(SY, "sync", ("vs", self.cnt["vs"]))
                self.W(SY, "sync", tok["rs0"])
                self.W(SY, "sync", tok.get("x1w%d" % (ti - 1)))
                tok["x0l"] = self.dma(SY, "dl", S["xt"][:, :],
                                      I["xin"][128 * ti:128 * ti + 128, :])
                tok["yfl"] = self.dma(SY, "dl", S["cx"][:, :],
                                      H["rs_out"].ap()[128 * ti:128 * ti + 128, :])

            @self.sec("vector")
            def _(V):
                self.W(V, "vector", tok["x0l"])
                self.W(V, "vector", tok["yfl"])
                self.W(V, "vector", tok["rs0"])
                self.W(V, "vector", tok.get("x1w%d" % (ti - 1)))
                self.op(V, "vector", "vs", "tensor_add", S["xt2"][:, :],
                        S["xt"][:, :], S["cx"][:, :])
                self.W(V, "vector", tok["bdnb"])
                tok["x1t"] = self.op(V, "vector", "vs", "tensor_add", S["xt"][:, :],
                                     S["xt2"][:, :], S["bcst"][:, :])

            @self.sec("scalar")
            def _(Sc):
                self.W(Sc, "scalar", tok["x1t"])
                tok["x1w%d" % ti] = self.dma(
                    Sc, "dt", H["x1h"].ap()[128 * ti:128 * ti + 128, :], S["xt"][:, :])

        @self.sec("sync")
        def _(SY):
            self.W(SY, "sync", self.qall("dt"))
        self.p1(layer=1)

    def p9_p10(self):
        S, I, H, PS, tok = self.S, self.I, self.Hh, self.PS, self.tok

        @self.sec("gpsimd")
        def _(G):
            self.W(G, "gpsimd", ("vs", self.cnt["vs"]))
            tok["bffb"] = self.dma(G, "dg", S["bcst"][:, :],
                                   I["bff1"].partition_broadcast(128).squeeze(1))

        @self.sec("gpsimd")
        def _(G):
            self.W(G, "gpsimd", tok["bffb"])
            tok["bcomb"] = self.dma(G, "dg", S["bcst"][:, :],
                                    I["bo1"].partition_broadcast(128).squeeze(1),
                                    accum=True)

        for osp in range(2):
            @self.sec("sync")
            def _(SY):
                self.W(SY, "sync", tok.get("p9done%d" % (osp - 1)))
                lt = []
                for cc in range(NCH):
                    lt.append(self.dma(SY, "dl",
                                       S["xnsp"][:, SP * cc:SP * cc + SP],
                                       H["xqh"].ap()[SP * osp:SP * osp + SP,
                                                     128 * cc:128 * cc + 128],
                                       transpose=True))
                tok["xql"] = lt

            for ns in range(4):
                @self.sec("sync")
                def _(SY):
                    self.W(SY, "sync", tok["tqw15"])
                    self.W(SY, "sync", tok.get("f1done%d" % (ns - 1)))
                    lt = []
                    for kc in range(NCH):
                        lt.append(self.dma(
                            SY, "dl", S["wA"][:, 512 * kc:512 * kc + 512],
                            H["wt1q"].ap()[512 * ns:512 * ns + 512,
                                           128 * kc:128 * kc + 128],
                            transpose=True))
                    tok["wtl"] = lt

                for tt in range(4):
                    ti = 4 * osp + tt

                    @self.sec("tensor")
                    def _(Te):
                        self.W(Te, "tensor", tok["wtl"])
                        self.W(Te, "tensor", tok["xql"])
                        ps = PS["p%d" % (tt % 2)]
                        pv = (ns, tt - 2) if tt >= 2 else ((ns - 1) % 4, tt + 2)
                        self.W(Te, "tensor", tok.get("x2ev%d_%d" % pv))
                        for kc in range(NCH):
                            self.mm(Te, ps[:, :],
                                    S["xnsp"][:, SP * kc + 128 * tt:SP * kc + 128 * tt + 128],
                                    S["wA"][:, 512 * kc:512 * kc + 512],
                                    start=(kc == 0), stop=(kc == NCH - 1),
                                    inc=(kc == NCH - 1))
                        tok["f1ps%d_%d" % (ns, tt)] = ("ts", self.cnt["ts"])

                    @self.sec("sync")
                    def _(SY):
                        self.W(SY, "sync", ("vs", self.cnt["vs"]))
                        self.W(SY, "sync", tok["rs1"])
                        tok["x1l"] = self.dma(
                            SY, "dl", S["xt"][:, 0:512],
                            H["x1h"].ap()[128 * ti:128 * ti + 128,
                                          512 * ns:512 * ns + 512])
                        tok["ya1"] = self.dma(
                            SY, "dl", S["cx"][:, 0:512],
                            H["rs_out"].ap()[128 * ti:128 * ti + 128,
                                             512 * ns:512 * ns + 512])

                    @self.sec("vector")
                    def _(V):
                        self.W(V, "vector", tok["f1ps%d_%d" % (ns, tt)])
                        self.W(V, "vector", self.qall("dt"))
                        self.W(V, "vector", tok["gbc"])
                        self.W(V, "vector", tok["bcomb"])
                        self.W(V, "vector", tok["x1l"])
                        self.W(V, "vector", tok["ya1"])
                        self.W(V, "vector", tok["rs1"])
                        ps = PS["p%d" % (tt % 2)]
                        xs = S["sm"][:, 40 + ti:41 + ti]
                        self.op(V, "vector", "vs", "tensor_scalar",
                                S["xt"][:, 512:1024], ps[:, :], xs, None, Alu.mult)
                        self.op(V, "vector", "vs", "tensor_tensor",
                                S["xt"][:, 1024:1536], S["xt"][:, 512:1024],
                                S["gbc"][:, 512 * ns:512 * ns + 512], Alu.mult)
                        self.op(V, "vector", "vs", "tensor_add",
                                S["xt"][:, 1536:2048], S["xt"][:, 0:512],
                                S["cx"][:, 0:512])
                        self.op(V, "vector", "vs", "tensor_add",
                                S["xt"][:, 1536:2048], S["xt"][:, 1536:2048],
                                S["xt"][:, 1024:1536])
                        self.op(V, "vector", "vs", "tensor_tensor",
                                S["xt2"][:, 0:512], S["xt"][:, 1536:2048],
                                S["bcst"][:, 512 * ns:512 * ns + 512],
                                Alu.add)
                        tok["x2ev%d_%d" % (ns, tt)] = ("vs", self.cnt["vs"])

                    @self.sec("scalar")
                    def _(Sc):
                        self.W(Sc, "scalar", tok["x2ev%d_%d" % (ns, tt)])
                        tok["youtw%d_%d" % (ns, ti)] = self.dma(
                            Sc, "dt",
                            self.yout[128 * ti:128 * ti + 128, 512 * ns:512 * ns + 512],
                            S["xt2"][:, 0:512])
                tok["f1done%d" % ns] = ("ts", self.cnt["ts"])
            tok["p9done%d" % osp] = ("ts", self.cnt["ts"])

    def finish(self):
        S, H = self.S, self.Hh

        @self.sec("gpsimd")
        def _(G):
            for s in ("ts", "vs", "ss", "cc"):
                self.W(G, "gpsimd", (s, self.cnt[s]))
            self.W(G, "gpsimd", self.allq())
            n = 0
            for name, ap in self.dbg_out.items():
                if name in H:
                    G.dma_start(out=ap, in_=H[name].ap()).then_inc(self.h["fin"], 16)
                    n += 1
                elif name in S:
                    G.dma_start(out=ap, in_=S[name][:, :]).then_inc(self.h["fin"], 16)
                    n += 1
            self.cnt["fin"] += 16 * n
            if self.cnt["fin"] > 0:
                G.wait_ge(self.h["fin"], self.cnt["fin"])


def sm_col(S, i):
    return S["sm"][:, i:i + 1]


def build_program(dbg=(), upto="full"):
    k = Kern(dbg, upto)
    nc = k.build()
    return nc, k


def kernel(**inputs):
    nc, k = build_program()
    in_maps = [prep_core_inputs(c, inputs) for c in range(NCORE)]
    res = run_bass_kernel_spmd(nc, in_maps, core_ids=list(range(NCORE)))
    out = np.zeros((B, T, C), np.float32)
    for c in range(NCORE):
        b, j = c // 4, c % 4
        out[b, OWN * j:OWN * j + OWN, :] = res.results[c]["yout"]
    return out


# version 05
# speedup vs baseline: 1.0341x; 1.0341x over previous
import numpy as np
import ml_dtypes

import concourse.bass as bass
import concourse.mybir as mybir
from concourse.bass_utils import run_bass_kernel_spmd

F32 = mybir.dt.float32
BF16 = mybir.dt.bfloat16
I32 = mybir.dt.int32
U16 = mybir.dt.uint16
Alu = mybir.AluOpType
Act = mybir.ActivationFunctionType
AxX = mybir.AxisListType.X

B, T, C = 2, 4096, 2048
NCORE = 8
OWN = 1024
SP = 512
NSP = T // SP
NCH = C // 128
FS = 2048
W_LOCAL = 512
SC0 = float(1.0 / np.sqrt(128.0))
SCG = float(SC0 * 128.0 / 64.0)
SC1 = float(1.0 / np.sqrt(64.0))
MB_L, MB_G, MB_1 = 20.0, 15.0, 25.0
GROUPS = [[0, 1, 2, 3], [4, 5, 6, 7]]

BC = dict(sc0=0, mbl=1, scg=2, mbg=3, sc1=4, mb1=5, bq=10, bk=14)


def _f32c(x):
    return np.ascontiguousarray(x, dtype=np.float32)


def _host_tables():
    t = {}
    inv = (1.0 / (10000.0 ** (np.arange(0, 128, 2, dtype=np.float32) / 128.0))).astype(np.float32)
    tt = np.arange(T, dtype=np.float32)
    fr = tt[:, None] * inv[None, :]
    emb = np.concatenate([fr, fr], axis=1)
    t["cos0"] = _f32c(np.cos(emb).T)
    t["sin0"] = _f32c(np.sin(emb).T)
    P = np.zeros((128, 128), np.float32)
    for i in range(64):
        P[2 * i, 2 * i + 1] = -1.0
        P[2 * i + 1, 2 * i] = 1.0
    t["rotP0T"] = np.ascontiguousarray(P.T, dtype=ml_dtypes.bfloat16)
    th = (10000.0 ** (-np.arange(0, 64, 2, dtype=np.float32) / 64.0)).astype(np.float32)
    tid = np.floor((np.arange(T) // 576).astype(np.float32) * 0.5).astype(np.float32)
    ang = (tt[:, None] + tid[:, None]) * th[None, :]
    ang2 = np.concatenate([ang, ang], axis=1)
    c1, s1 = np.cos(ang2).T, np.sin(ang2).T
    t["cos1"] = _f32c(np.concatenate([c1, c1], axis=0))
    t["sin1"] = _f32c(np.concatenate([s1, s1], axis=0))
    P1 = np.zeros((64, 64), np.float32)
    for i in range(32):
        P1[i, i + 32] = -1.0
        P1[i + 32, i] = 1.0
    z = np.zeros_like(P1)
    t["rotP1T"] = np.ascontiguousarray(np.block([[P1.T, z], [z, P1.T]]),
                                       dtype=ml_dtypes.bfloat16)
    m0 = np.zeros((128, 8 * SP), np.float32)
    for q in range(8):
        rho = -512 + 128 * q
        srel = np.arange(128)[:, None] + rho
        trel = np.arange(SP)[None, :]
        m0[:, q * SP:(q + 1) * SP] = (srel <= trel) & (srel >= trel - (W_LOCAL - 1))
    t["msk0"] = np.ascontiguousarray(m0, dtype=ml_dtypes.bfloat16)
    m1 = np.zeros((128, 4 * SP), np.float32)
    for q in range(4):
        m1[:, q * SP:(q + 1) * SP] = (np.arange(128)[:, None] + 128 * q) <= np.arange(SP)[None, :]
    t["msk1"] = np.ascontiguousarray(m1, dtype=ml_dtypes.bfloat16)
    t["idt32"] = np.eye(128, dtype=np.float32)
    return t


_TAB = None


def prep_core_inputs(core, inp):
    global _TAB
    if _TAB is None:
        _TAB = _host_tables()
    b, j = core // 4, core % 4
    d = {}
    x = np.asarray(inp["L0_attn_x"])
    d["xin"] = _f32c(x[b, OWN * j:OWN * j + OWN, :])
    wqkv = np.asarray(inp["L0_attn_w_qkv"])
    h0 = 4 * j
    d["wqT"] = _f32c(wqkv[128 * h0:128 * h0 + 512, :].T)
    d["wkT"] = _f32c(wqkv[2048 + 128 * h0:2048 + 128 * h0 + 512, :].T)
    d["wvT"] = _f32c(wqkv[4096 + 128 * h0:4096 + 128 * h0 + 512, :].T)
    d["wvN"] = _f32c(wqkv[4096 + 128 * h0:4096 + 128 * h0 + 512, :])
    d["pkT"] = _f32c(np.asarray(inp["L0_attn_pk"]).T)
    d["pvT"] = _f32c(np.asarray(inp["L0_attn_pv"]).T)
    d["uoT"] = _f32c(np.asarray(inp["L0_attn_uo"]).T)
    d["w1T"] = _f32c(np.asarray(inp["L0_attn_info_w1"]).T)
    d["w2c"] = _f32c(np.asarray(inp["L0_attn_info_w2"]).reshape(4, 128).T)
    d["woT0s"] = _f32c(np.asarray(inp["L0_attn_w_o"]).T[512 * j:512 * j + 512, :])
    d["wupT"] = _f32c(np.asarray(inp["L0_ffn_up_kernel"]).T[FS * j:FS * j + FS, :])
    d["wdnT"] = _f32c(np.asarray(inp["L0_ffn_down_kernel"]).T[:, FS * j:FS * j + FS])
    d["bupP"] = _f32c(np.asarray(inp["L0_ffn_up_bias"])[FS * j:FS * j + FS].reshape(16, 128).T)
    d["bdn"] = _f32c(np.asarray(inp["L0_ffn_down_bias"]).reshape(1, C))
    hb = 512 * j
    d["wq1T"] = _f32c(np.asarray(inp["L1_attn_Wq"]).T[:, hb:hb + 512])
    d["wk1T"] = _f32c(np.asarray(inp["L1_attn_Wk"]).T[:, hb:hb + 512])
    d["wv1T"] = _f32c(np.asarray(inp["L1_attn_Wv"]).T[:, hb:hb + 512])
    d["bq1"] = _f32c(np.asarray(inp["L1_attn_bq"])[hb:hb + 512].reshape(4, 128).T)
    d["bk1"] = _f32c(np.asarray(inp["L1_attn_bk"])[hb:hb + 512].reshape(4, 128).T)
    d["bv1"] = _f32c(np.asarray(inp["L1_attn_bv"])[hb:hb + 512].reshape(1, 512))
    d["bo1"] = _f32c(np.asarray(inp["L1_attn_bo"]).reshape(1, C))
    d["wo1Ts"] = _f32c(np.asarray(inp["L1_attn_Wo"]).T[hb:hb + 512, :])
    d["wt1"] = _f32c(np.asarray(inp["L1_ffn_weight"]))
    d["bff1"] = _f32c(np.asarray(inp["L1_ffn_bias"]).reshape(1, C))
    d.update(_TAB)
    return d


class Kern:
    def __init__(self, dbg=(), upto="full"):
        self.dbg = set(dbg)
        self.upto = upto
        self.dbg_out = {}
        self.nc = bass.Bass("TRN2", target_bir_lowering=False, debug=False)
        self.tok = {}

    def sem(self, name):
        self.h[name] = self.stk.enter_context(self.nc.semaphore(name))
        self.cnt[name] = 0

    def W(self, eng, ename, token):
        if token is None:
            return
        if isinstance(token, list):
            for t in token:
                self.W(eng, ename, t)
            return
        sname, val = token
        if sname[0] == "d" and sname[:2] in ("dl", "dg", "dt"):
            val = self.cnt[sname]
        key = (ename, sname)
        if val <= 0 or self.wm.get(key, 0) >= val:
            return
        eng.wait_ge(self.h[sname], val)
        self.wm[key] = val

    def _selfwait(self, eng, ename, sname):
        self.W(eng, ename, (sname, self.cnt[sname]))

    def op(self, eng, ename, sname, fname, *a, **kw):
        self._selfwait(eng, ename, sname)
        inst = getattr(eng, fname)(*a, **kw)
        inst.then_inc(self.h[sname], 1)
        self.cnt[sname] += 1
        return (sname, self.cnt[sname])

    def mm(self, Te, out, l, r, start, stop, tp=None, inc=False):
        kw = {}
        if tp is not None:
            kw["tile_position"] = tp
        if inc:
            self._selfwait(Te, "tensor", "ts")
        inst = Te.matmul(out, l, r, start=start, stop=stop, **kw)
        if inc:
            inst.then_inc(self.h["ts"], 1)
            self.cnt["ts"] += 1
            return ("ts", self.cnt["ts"])
        return None

    def tr(self, Te, out, in_, ident):
        self._selfwait(Te, "tensor", "ts")
        inst = Te.transpose(out, in_, ident)
        inst.then_inc(self.h["ts"], 1)
        self.cnt["ts"] += 1
        return ("ts", self.cnt["ts"])

    def dma(self, eng, qname, out, in_, transpose=False, accum=False):
        if self._dmasec.get(qname) != self._secn:
            self._dmasec[qname] = self._secn
            self.qi[qname] += 1
        i = self.qi[qname] % 4
        sname = "%s%d" % (qname, i)
        if transpose:
            inst = eng.dma_start_transpose(out=out, in_=in_)
        elif accum:
            inst = eng.dma_start(out=out, in_=in_, accum_op=Alu.add)
        else:
            inst = eng.dma_start(out=out, in_=in_)
        inst.then_inc(self.h[sname], 16)
        self.cnt[sname] += 16
        return (sname, self.cnt[sname])

    def qall(self, qname):
        return [("%s%d" % (qname, i), self.cnt["%s%d" % (qname, i)]) for i in range(4)]

    def allq(self):
        out = []
        for q in ("dl", "dg", "dt"):
            out += self.qall(q)
        return out

    def coll(self, kind, aop, ins, outs, name):
        tok = self.tok

        @self.sec("gpsimd")
        def _(G):
            self.W(G, "gpsimd", self.allq())
            for s in ("ts", "vs", "ss"):
                self.W(G, "gpsimd", (s, self.cnt[s]))
            self._selfwait(G, "gpsimd", "cc")
            inst = G.collective_compute(kind, aop, replica_groups=GROUPS,
                                        ins=[ins.ap().opt()], outs=[outs.ap().opt()])
            inst.then_inc(self.h["cc"], 1)
            self.cnt["cc"] += 1
            tok[name] = ("cc", self.cnt["cc"])

    def declare(self):
        nc = self.nc
        A = {}
        def ein(name, shape, dt=F32):
            A[name] = nc.dram_tensor(name, list(shape), dt, kind="ExternalInput").ap()
        ein("xin", [OWN, C])
        ein("wqT", [C, 512]); ein("wkT", [C, 512]); ein("wvT", [C, 512])
        ein("wvN", [512, C])
        ein("pkT", [128, 64]); ein("pvT", [128, 64]); ein("uoT", [64, 128])
        ein("w1T", [C, 512]); ein("w2c", [128, 4])
        ein("woT0s", [512, C])
        ein("wupT", [FS, C]); ein("wdnT", [C, FS])
        ein("bupP", [128, 16]); ein("bdn", [1, C])
        ein("wq1T", [C, 512]); ein("wk1T", [C, 512]); ein("wv1T", [C, 512])
        ein("bq1", [128, 4]); ein("bk1", [128, 4]); ein("bv1", [1, 512])
        ein("bo1", [1, C]); ein("wo1Ts", [512, C]); ein("wt1", [C, C])
        ein("bff1", [1, C])
        ein("cos0", [128, T]); ein("sin0", [128, T])
        ein("cos1", [128, T]); ein("sin1", [128, T])
        ein("rotP0T", [128, 128], BF16); ein("rotP1T", [128, 128], BF16)
        ein("msk0", [128, 8 * SP], BF16); ein("msk1", [128, 4 * SP], BF16)
        ein("idt32", [128, 128])
        self.I = A
        self.yout = nc.dram_tensor("yout", [OWN, C], F32, kind="ExternalOutput").ap()
        Hh = {}
        def hin(name, shape, dt=BF16):
            Hh[name] = nc.dram_tensor(name, list(shape), dt, kind="Internal")
        hin("ag_in", [OWN, C]); hin("ag_out", [T, C])
        hin("rs_in", [T, C]); hin("rs_out", [OWN, C])
        hin("x1h", [OWN, C], F32)
        hin("q0h", [512, T])
        hin("kh", [128, NSP * 4 * 512])
        hin("vh", [128, 32 * 520])
        hin("wqupT", [FS, C]); hin("wqdnT", [C, FS]); hin("wt1q", [C, C])
        hin("xqh", [OWN, C])
        hin("gscr", [1, C], F32)
        hin("nscr", [1, 2 * SP], F32)
        hin("ctlh", [128, NSP * 2048])
        hin("wvrh", [256, C])
        self.Hh = Hh

    def dump(self, name, shape, dt=F32):
        if name in self.dbg:
            self.dbg_out[name] = self.nc.dram_tensor(
                "dbg_" + name, list(shape), dt, kind="ExternalOutput").ap()

    def build(self):
        from contextlib import ExitStack
        nc = self.nc
        self.declare()
        self.dump("ag_in", [OWN, C], BF16); self.dump("ag_out", [T, C], BF16)
        self.dump("q0h", [512, T], BF16); self.dump("kh", [128, NSP * 4 * 512], BF16)
        self.dump("vh", [128, 32 * 520], BF16)
        self.dump("kr", [128, 2 * T], BF16); self.dump("vr", [128, 32 * 260], BF16)
        self.dump("gate", [1, T]); self.dump("wqupT", [FS, C], BF16)
        self.dump("wqdnT", [C, FS], BF16); self.dump("wt1q", [C, C], BF16)
        self.dump("gscr", [1, C]); self.dump("rs_out", [OWN, C], BF16)
        self.dump("x1h", [OWN, C]); self.dump("xqh", [OWN, C], BF16)
        self.dump("rs_in", [T, C], BF16); self.dump("wvrh", [256, C], BF16)
        self.dump("gbc", [128, 2048])
        with ExitStack() as stk:
            self.stk = stk
            self.h, self.cnt, self.wm = {}, {}, {}
            self.qi = {"dl": 0, "dg": 0, "dt": 0}
            self._secn = 0
            self._dmasec = {}
            S = {}
            def sb(name, shape, dt=BF16):
                S[name] = stk.enter_context(nc.sbuf_tensor(name, shape, dt))
            sb("xnsp", [128, NCH * SP])
            sb("wA", [128, NCH * 512])
            sb("wB", [128, NCH * 512])
            sb("wC", [128, NCH * 512])
            sb("wD", [128, NCH * 256])
            sb("w1b", [128, 2 * 512])
            sb("kr", [128, 2 * T])
            sb("vr", [128, 32 * 260])
            sb("qsp", [128, 4 * SP])
            sb("qr", [128, 2 * SP])
            sb("tab", [128, 2 * SP], F32)
            sb("msk", [128, 8 * SP])
            sb("att", [128, 4 * 1024])
            sb("hsp", [128, NCH * SP])
            sb("ev", [128, 8 * 512])
            sb("ctl", [128, 4 * 512])
            sb("xt", [128, 2048], F32)
            sb("xt2", [128, 2048], F32)
            sb("cx", [128, 2048])
            sb("sm", [128, 256], F32)
            sb("smb", [128, 128])
            sb("nbc", [128, 2 * SP], F32)
            sb("bias", [128, 40], F32)
            sb("bcst", [128, 2048])
            sb("gbc", [128, 2048])
            sb("gcol", [128, 16], F32)
            sb("tiny", [128, 640])
            sb("id32", [128, 128], F32)
            sb("gate", [1, T])
            S["kl"] = S["wC"]
            S["vl"] = S["wA"]
            S["qr"] = S["wD"]
            S["svec"] = S["xt2"][0:1, 0:2048]
            S["kl1"] = S["kr"]
            S["vl1"] = S["vr"]
            self.S = S
            PS = {}
            def psm(name, shape):
                PS[name] = stk.enter_context(nc.psum_tensor(name, shape, F32))
            psm("qA", [128, 1024]); psm("qB", [128, 1024])
            psm("qC", [128, 1024]); psm("qD", [128, 1024])
            PS["pA"] = PS["qA"][:, :]
            PS["pB"] = PS["qB"][:, :]
            PS["p0"] = PS["qC"][:, 0:512]
            PS["p1"] = PS["qC"][:, 512:1024]
            PS["p2"] = PS["qD"][:, 0:512]
            PS["p3"] = PS["qD"][:, 512:1024]
            self.PS = PS
            for q in ("dl", "dg", "dt"):
                for i in range(4):
                    self.sem("%s%d" % (q, i))
            for s in ("ts", "vs", "ss", "cc", "fin"):
                self.sem(s)
            self.blk = stk.enter_context(nc.Block())
            self.emit()
        return nc

    def sec(self, name):
        self._secn += 1
        return getattr(self.blk, name)

    def emit(self):
        upto = self.upto
        tok = self.tok
        self.p0()
        self.p1(layer=0)
        self.coll("AllGather", Alu.bypass, self.Hh["ag_in"], self.Hh["ag_out"], "ag0")
        if upto == "p1":
            return self.finish()
        self.quant_mx("up")
        self.quant_mx("dn")
        self.quant_ternary()
        if upto == "p2":
            return self.finish()
        self.l0_wload()
        self.wvr_build()
        for sp in range(NSP):
            self.qkv_span(0, sp)
        if upto == "p3":
            return self.finish()
        for tau in range(NSP):
            self.attn0_local_span(tau)
        self.l0_wo_load()
        for sp in range(NSP):
            self.ffn0_span(sp)
            self.attn0_global_span(sp)
        self.coll("ReduceScatter", Alu.add, self.Hh["rs_in"], self.Hh["rs_out"], "rs0")
        if upto in ("p4", "p5"):
            return self.finish()
        self.p6()
        self.coll("AllGather", Alu.bypass, self.Hh["ag_in"], self.Hh["ag_out"], "ag1")
        if upto == "p6":
            return self.finish()
        self.l1_wload()
        self.l1_wo_load()
        if upto == "p7":
            for sp in range(NSP):
                self.qkv_span(1, sp)
            return self.finish()
        for sp in range(NSP):
            self.qkv_span(1, sp)
            self.attn1_span(sp)
        self.coll("ReduceScatter", Alu.add, self.Hh["rs_in"], self.Hh["rs_out"], "rs1")
        if upto == "p8":
            return self.finish()
        self.p9_p10()
        self.finish()

    def p0(self):
        S, I, tok = self.S, self.I, self.tok

        @self.sec("vector")
        def _(V):
            for name, val in [("sc0", SC0), ("mbl", -MB_L), ("scg", SCG),
                              ("mbg", -MB_G), ("sc1", SC1), ("mb1", -MB_1)]:
                self.op(V, "vector", "vs", "memset", S["bias"][:, BC[name]:BC[name] + 1], val)
            tok["consts"] = self.op(V, "vector", "vs", "memset",
                                    S["tiny"][:, 516:517], 1.0)

        @self.sec("sync")
        def _(SY):
            tok["msk"] = self.dma(SY, "dl", S["msk"][:, 0:8 * SP], I["msk0"])
            tok["id32"] = self.dma(SY, "dl", S["id32"][:, :], I["idt32"])
            tok["biasq"] = self.dma(SY, "dl", S["bias"][:, BC["bq"]:BC["bq"] + 4], I["bq1"])
            tok["biask"] = self.dma(SY, "dl", S["bias"][:, BC["bk"]:BC["bk"] + 4], I["bk1"])

        @self.sec("gpsimd")
        def _(G):
            t = S["tiny"]
            self.dma(G, "dg", t[:, 0:128], I["rotP0T"])
            self.dma(G, "dg", t[:, 128:256], I["rotP1T"])
            self.dma(G, "dg", t[:, 256:320], I["pkT"])
            self.dma(G, "dg", t[0:64, 320:448], I["uoT"])
            self.dma(G, "dg", t[:, 448:512], I["pvT"])
            self.dma(G, "dg", t[:, 520:524], I["w2c"])
            self.dma(G, "dg", S["nbc"][:, SP:2 * SP],
                     I["bv1"].partition_broadcast(128).squeeze(1))
            tok["bupP"] = self.dma(G, "dg", S["bias"][:, 20:36], I["bupP"])
            tok["tiny"] = self.qall("dg")

    def p1(self, layer):
        S, I, H, tok = self.S, self.I, self.Hh, self.tok
        src = I["xin"] if layer == 0 else H["x1h"].ap()
        for ti in range(8):
            self.ln_tile(ti, src[128 * ti:128 * ti + 128, :], layer)

            @self.sec("scalar")
            def _(Sc):
                self.W(Sc, "scalar", tok["lnxn"])
                tok["agw%d" % ti] = self.dma(
                    Sc, "dt", H["ag_in"].ap()[128 * ti:128 * ti + 128, :], S["cx"][:, :])

            if layer == 1:
                self.xq_tile(ti)

    def ln_tile(self, ti, src_ap, layer):
        S, tok = self.S, self.tok
        xt, xt2, cx, sm = S["xt"], S["xt2"], S["cx"], S["sm"]
        c = 192 + (ti % 2) * 16

        @self.sec("sync")
        def _(SY):
            self.W(SY, "sync", ("vs", self.cnt["vs"]))
            self.W(SY, "sync", ("ss", self.cnt["ss"]))
            tok["lnx"] = self.dma(SY, "dl", xt[:, :], src_ap)

        @self.sec("scalar")
        def _(Sc):
            self.W(Sc, "scalar", tok["lnx"])
            tok["lnsq"] = self.op(Sc, "scalar", "ss", "activation",
                                  xt2[:, :], xt[:, :], Act.Square)

        @self.sec("vector")
        def _(V):
            self.W(V, "vector", tok["lnx"])
            self.W(V, "vector", tok["lnsq"])
            self.op(V, "vector", "vs", "reduce_sum", sm[:, c:c + 1], xt[:, :], AxX)
            self.op(V, "vector", "vs", "reduce_sum", sm[:, c + 1:c + 2], xt2[:, :], AxX)
            self.op(V, "vector", "vs", "tensor_scalar", sm[:, c:c + 1], sm[:, c:c + 1],
                    1.0 / C, None, Alu.mult)
            self.op(V, "vector", "vs", "tensor_scalar", sm[:, c + 1:c + 2],
                    sm[:, c + 1:c + 2], 1.0 / C, None, Alu.mult)
            self.op(V, "vector", "vs", "tensor_scalar", sm[:, c + 2:c + 3],
                    sm[:, c:c + 1], sm[:, c:c + 1], None, Alu.mult)
            self.op(V, "vector", "vs", "tensor_sub", sm[:, c + 3:c + 4],
                    sm[:, c + 1:c + 2], sm[:, c + 2:c + 3])
            self.op(V, "vector", "vs", "tensor_scalar", sm[:, c + 4:c + 5],
                    sm[:, c + 3:c + 4], 1e-5, None, Alu.add)
            tok["lnv"] = self.op(V, "vector", "vs", "reciprocal",
                                 sm[:, c + 5:c + 6], sm[:, c + 4:c + 5])

        @self.sec("scalar")
        def _(Sc):
            self.W(Sc, "scalar", tok["lnv"])
            tok["lnrs"] = self.op(Sc, "scalar", "ss", "activation",
                                  sm[:, c + 6:c + 7], sm[:, c + 5:c + 6], Act.Sqrt)

        @self.sec("vector")
        def _(V):
            self.W(V, "vector", tok["lnrs"])
            self.W(V, "vector", tok.get("agw%d" % (ti - 1)))
            tok["lnxn"] = self.op(V, "vector", "vs", "tensor_scalar", cx[:, :], xt[:, :],
                                  sm[:, c:c + 1], sm[:, c + 6:c + 7],
                                  Alu.subtract, Alu.mult)
            if layer == 1:
                self.op(V, "vector", "vs", "reduce_max", sm[:, c + 7:c + 8], cx[:, :],
                        AxX, apply_absolute_value=True)
                self.op(V, "vector", "vs", "tensor_scalar", sm[:, 40 + ti:41 + ti],
                        sm[:, c + 7:c + 8], 1.0 / 127.0, None, Alu.mult)
                self.op(V, "vector", "vs", "reciprocal", sm[:, c + 8:c + 9],
                        sm[:, c + 4:c + 5])
                self.op(V, "vector", "vs", "tensor_mul", sm[:, c + 9:c + 10],
                        sm[:, c + 8:c + 9], sm[:, c + 3:c + 4])
                tok["rms2"] = self.op(V, "vector", "vs", "tensor_scalar",
                                      sm[:, c + 9:c + 10], sm[:, c + 9:c + 10],
                                      1e-8, None, Alu.add)

        if layer == 1:
            @self.sec("scalar")
            def _(Sc):
                self.W(Sc, "scalar", tok["rms2"])
                tok["rms"] = self.op(Sc, "scalar", "ss", "activation",
                                     sm[:, c + 10:c + 11], sm[:, c + 9:c + 10], Act.Sqrt)

            @self.sec("vector")
            def _(V):
                self.W(V, "vector", tok["rms"])
                self.op(V, "vector", "vs", "tensor_scalar", sm[:, c + 11:c + 12],
                        sm[:, 40 + ti:41 + ti], 1e-8, None, Alu.add)
                self.op(V, "vector", "vs", "tensor_mul", sm[:, c + 12:c + 13],
                        sm[:, c + 11:c + 12], sm[:, c + 10:c + 11])
                tok["cb"] = self.op(V, "vector", "vs", "reciprocal",
                                    sm[:, 48 + ti:49 + ti], sm[:, c + 12:c + 13])

    def xq_tile(self, ti):
        S, H, tok = self.S, self.Hh, self.tok
        xt, xt2, cx, sm = S["xt"], S["xt2"], S["cx"], S["sm"]

        @self.sec("vector")
        def _(V):
            self.W(V, "vector", tok["cb"])
            self.W(V, "vector", tok.get("xqw"))
            self.op(V, "vector", "vs", "tensor_scalar", xt2[:, :], cx[:, :],
                    sm[:, 48 + ti:49 + ti], None, Alu.mult)
            self.op(V, "vector", "vs", "tensor_scalar", xt[:, :], xt2[:, :],
                    12582912.0, None, Alu.add)
            self.op(V, "vector", "vs", "tensor_scalar", xt2[:, :], xt[:, :],
                    12582912.0, None, Alu.subtract)
            self.op(V, "vector", "vs", "tensor_scalar", xt[:, :], xt2[:, :],
                    127.0, -127.0, Alu.min, Alu.max)
            tok["xq"] = self.op(V, "vector", "vs", "tensor_copy",
                                S["ev"][:, 0:2048], xt[:, :])

        @self.sec("scalar")
        def _(Sc):
            self.W(Sc, "scalar", tok["xq"])
            tok["xqw"] = self.dma(Sc, "dt", H["xqh"].ap()[128 * ti:128 * ti + 128, :],
                                  S["ev"][:, 0:2048])

    def quant_mx(self, which):
        S, I, H, tok = self.S, self.I, self.Hh, self.tok
        src = I["wupT" if which == "up" else "wdnT"]
        dsth = H["wqupT" if which == "up" else "wqdnT"]
        xt, xt2, hs, sm, smb = S["xt"], S["xt2"], S["hsp"], S["sm"], S["smb"]
        for ti in range(16):
            @self.sec("sync")
            def _(SY):
                self.W(SY, "sync", ("vs", self.cnt["vs"]))
                tok["qzl"] = self.dma(SY, "dl", xt[:, :],
                                      src[128 * ti:128 * ti + 128, :])

            @self.sec("vector")
            def _(V):
                self.W(V, "vector", tok["qzl"])
                if ti == 0:
                    self.W(V, "vector", self.qall("dt"))
                self.W(V, "vector", tok.get("qzw%s%d" % (which, ti - 1)))
                w3 = xt[:, :].rearrange("p (nb k) -> p nb k", k=32)
                self.op(V, "vector", "vs", "tensor_reduce", sm[:, 0:64], w3, AxX,
                        Alu.max, apply_absolute_value=True)
                a_i = sm[:, 0:64].bitcast(I32)
                s_i = sm[:, 64:128].bitcast(I32)
                r_i = sm[:, 128:192].bitcast(I32)
                p_i = sm[:, 192:256].bitcast(I32)
                self.op(V, "vector", "vs", "tensor_scalar", p_i, a_i,
                        0x7F800000, None, Alu.bitwise_and)
                self.op(V, "vector", "vs", "tensor_scalar", s_i, p_i,
                        0x01000000, None, Alu.subtract)
                self.op(V, "vector", "vs", "tensor_scalar", r_i, s_i,
                        0x7F000000, -1, Alu.subtract, Alu.mult)
                self.op(V, "vector", "vs", "tensor_copy", smb[:, 0:64], sm[:, 128:192])
                self.op(V, "vector", "vs", "tensor_copy", smb[:, 64:128], sm[:, 64:128])
                acc = hs[:, 0:2048]; tmp = hs[:, 2048:4096]
                sgn = hs[:, 4096:6144]; qs = hs[:, 6144:8192]
                rb = smb[:, 0:64].unsqueeze(2).broadcast_to([128, 64, 32])
                sbb = smb[:, 64:128].unsqueeze(2).broadcast_to([128, 64, 32])
                self.op(V, "vector", "vs", "tensor_tensor",
                        xt2[:, :].rearrange("p (nb k) -> p nb k", k=32), w3, rb,
                        Alu.mult)
                self.op(V, "vector", "vs", "tensor_scalar", sgn, xt2[:, :],
                        0.0, 2.0, Alu.is_ge, Alu.mult)
                self.op(V, "vector", "vs", "tensor_scalar", xt2[:, :].bitcast(I32),
                        xt2[:, :].bitcast(I32), 0x7FFFFFFF, None, Alu.bitwise_and)
                self.op(V, "vector", "vs", "tensor_scalar", acc, xt2[:, :],
                        0.25, 0.5, Alu.is_ge, Alu.mult)
                for th, dl_ in [(0.75, 0.5), (1.25, 0.5), (1.75, 0.5),
                                (2.5, 1.0), (3.5, 1.0), (5.0, 2.0)]:
                    self.op(V, "vector", "vs", "tensor_scalar", tmp, xt2[:, :],
                            th, dl_, Alu.is_ge, Alu.mult)
                    self.op(V, "vector", "vs", "tensor_add", acc, acc, tmp)
                self.op(V, "vector", "vs", "tensor_tensor",
                        qs.rearrange("p (nb k) -> p nb k", k=32),
                        acc.rearrange("p (nb k) -> p nb k", k=32), sbb, Alu.mult)
                self.op(V, "vector", "vs", "tensor_scalar", sgn, sgn,
                        1.0, None, Alu.subtract)
                tok["qz%s%d" % (which, ti)] = self.op(
                    V, "vector", "vs", "tensor_tensor", acc, qs, sgn, Alu.mult)

            @self.sec("scalar")
            def _(Sc):
                self.W(Sc, "scalar", tok["qz%s%d" % (which, ti)])
                tok["qzw%s%d" % (which, ti)] = self.dma(
                    Sc, "dt", dsth.ap()[128 * ti:128 * ti + 128, :], hs[:, 0:2048])

    def quant_ternary(self):
        S, I, H, tok = self.S, self.I, self.Hh, self.tok
        xt, xt2, sm = S["xt"], S["xt2"], S["sm"]
        for ti in range(16):
            @self.sec("sync")
            def _(SY):
                self.W(SY, "sync", ("vs", self.cnt["vs"]))
                tok["tql"] = self.dma(SY, "dl", xt[:, :],
                                      I["wt1"][128 * ti:128 * ti + 128, :])

            @self.sec("vector")
            def _(V):
                self.W(V, "vector", tok["tql"])
                if ti == 0:
                    self.W(V, "vector", self.qall("dt"))
                self.W(V, "vector", tok.get("tqw%d" % (ti - 1)))
                self.op(V, "vector", "vs", "tensor_reduce", sm[:, 0:1], xt[:, :],
                        AxX, Alu.add, apply_absolute_value=True)
                self.op(V, "vector", "vs", "tensor_scalar", S["gcol"][:, ti:ti + 1],
                        sm[:, 0:1], 1.0 / 2048.0, None, Alu.mult)
                self.op(V, "vector", "vs", "tensor_scalar", sm[:, 2:3],
                        S["gcol"][:, ti:ti + 1], 1e-8, None, Alu.add)
                self.op(V, "vector", "vs", "reciprocal", sm[:, 3:4], sm[:, 2:3])
                self.op(V, "vector", "vs", "tensor_scalar", xt2[:, :], xt[:, :],
                        sm[:, 3:4], None, Alu.mult)
                self.op(V, "vector", "vs", "tensor_scalar", xt[:, :], xt2[:, :],
                        12582912.0, None, Alu.add)
                self.op(V, "vector", "vs", "tensor_scalar", xt2[:, :], xt[:, :],
                        12582912.0, None, Alu.subtract)
                self.op(V, "vector", "vs", "tensor_scalar", xt[:, :], xt2[:, :],
                        1.0, -1.0, Alu.min, Alu.max)
                tok["tq%d" % ti] = self.op(V, "vector", "vs", "tensor_copy",
                                           S["ev"][:, 0:2048], xt[:, :])

            @self.sec("scalar")
            def _(Sc):
                self.W(Sc, "scalar", tok["tq%d" % ti])
                tok["tqw%d" % ti] = self.dma(
                    Sc, "dt", H["wt1q"].ap()[128 * ti:128 * ti + 128, :],
                    S["ev"][:, 0:2048])

        @self.sec("tensor")
        def _(Te):
            self.W(Te, "tensor", ("vs", self.cnt["vs"]))
            self.W(Te, "tensor", self.tok["id32"])
            tok["gtr"] = self.tr(Te, self.PS["p0"][0:16, 0:128], S["gcol"][:, :],
                                 S["id32"][:, :])

        @self.sec("vector")
        def _(V):
            self.W(V, "vector", tok["gtr"])
            tok["gev"] = self.op(V, "vector", "vs", "tensor_copy",
                                 sm[0:16, 128:256], self.PS["p0"][0:16, 0:128])

        @self.sec("scalar")
        def _(Sc):
            self.W(Sc, "scalar", tok["gev"])
            tok["gw"] = self.dma(Sc, "dt", H["gscr"].ap(), sm[0:16, 128:256])

        @self.sec("gpsimd")
        def _(G):
            self.W(G, "gpsimd", tok["gw"])
            tok["gbc"] = self.dma(G, "dg", S["gbc"][:, :],
                                  H["gscr"].ap().partition_broadcast(128).squeeze(1))

    def l0_wload(self):
        S, I, tok = self.S, self.I, self.tok

        @self.sec("gpsimd")
        def _(G):
            self.W(G, "gpsimd", ("vs", self.cnt["vs"]))
            tk = None
            for cc in range(NCH):
                self.dma(G, "dg", S["wA"][:, 512 * cc:512 * cc + 512],
                         I["wqT"][128 * cc:128 * cc + 128, :])
                self.dma(G, "dg", S["wB"][:, 512 * cc:512 * cc + 512],
                         I["wkT"][128 * cc:128 * cc + 128, :])
                self.dma(G, "dg", S["wC"][:, 512 * cc:512 * cc + 512],
                         I["wvT"][128 * cc:128 * cc + 128, :])
            tok["lw"] = self.qall("dg")

    def l1_wload(self):
        S, I, tok = self.S, self.I, self.tok

        @self.sec("gpsimd")
        def _(G):
            self.W(G, "gpsimd", ("ts", self.cnt["ts"]))
            for cc in range(NCH):
                self.dma(G, "dg", S["wA"][:, 512 * cc:512 * cc + 512],
                         I["wq1T"][128 * cc:128 * cc + 128, :])
                self.dma(G, "dg", S["wB"][:, 512 * cc:512 * cc + 512],
                         I["wk1T"][128 * cc:128 * cc + 128, :])
                self.dma(G, "dg", S["wC"][:, 512 * cc:512 * cc + 512],
                         I["wv1T"][128 * cc:128 * cc + 128, :])
            tok["lw"] = self.qall("dg")

        @self.sec("sync")
        def _(SY):
            self.W(SY, "sync", ("vs", self.cnt["vs"]))
            tok["msk"] = self.dma(SY, "dl", S["msk"][:, 0:4 * SP], I["msk1"])

    def l0_wo_load(self):
        S, I, tok = self.S, self.I, self.tok

        @self.sec("gpsimd")
        def _(G):
            self.W(G, "gpsimd", ("ts", self.cnt["ts"]))
            self.W(G, "gpsimd", ("vs", self.cnt["vs"]))
            for cc in range(4):
                self.dma(G, "dg", S["wC"][:, 2048 * cc:2048 * cc + 2048],
                         I["woT0s"][128 * cc:128 * cc + 128, :])
            tok["wo"] = self.qall("dg")

    def l1_wo_load(self):
        S, I, tok = self.S, self.I, self.tok

        @self.sec("gpsimd")
        def _(G):
            self.W(G, "gpsimd", ("ts", self.cnt["ts"]))
            self.W(G, "gpsimd", ("ss", self.cnt["ss"]))
            for cc in range(4):
                self.dma(G, "dg", S["hsp"][:, 2048 * cc:2048 * cc + 2048],
                         I["wo1Ts"][128 * cc:128 * cc + 128, :])
            tok["wo"] = self.qall("dg")

    def wvr_build(self):
        S, I, H, PS, tok = self.S, self.I, self.Hh, self.PS, self.tok
        for h in range(4):
            buf = (h % 2) * 2048

            @self.sec("gpsimd")
            def _(G):
                self.W(G, "gpsimd", tok.get("wvrw%d" % (h - 2)))
                tok["wvrl%d" % h] = self.dma(G, "dg", S["att"][:, buf:buf + 2048],
                                             I["wvN"][128 * h:128 * h + 128, :])

            @self.sec("tensor")
            def _(Te):
                self.W(Te, "tensor", tok["wvrl%d" % h])
                self.W(Te, "tensor", tok["tiny"])
                for csp in range(4):
                    self.W(Te, "tensor", tok.get("wvrev%d_%d" % (h - 1, csp)))
                    tok["wvrp%d_%d" % (h, csp)] = self.mm(
                        Te, PS["p%d" % csp][0:64, :], S["tiny"][:, 448:512],
                        S["att"][:, buf + 512 * csp:buf + 512 * csp + 512],
                        start=True, stop=True, inc=True)

            @self.sec("vector")
            def _(V):
                if h == 0:
                    self.W(V, "vector", self.qall("dt"))
                self.W(V, "vector", tok.get("wvrw%d" % (h - 1)))
                for csp in range(4):
                    self.W(V, "vector", tok["wvrp%d_%d" % (h, csp)])
                    tok["wvrev%d_%d" % (h, csp)] = self.op(
                        V, "vector", "vs", "tensor_copy",
                        S["ev"][0:64, 512 * csp:512 * csp + 512],
                        PS["p%d" % csp][0:64, :])

            @self.sec("scalar")
            def _(Sc):
                self.W(Sc, "scalar", tok["wvrev%d_3" % h])
                tok["wvrw%d" % h] = self.dma(
                    Sc, "dt", H["wvrh"].ap()[64 * h:64 * h + 64, :],
                    S["ev"][0:64, 0:2048])

        @self.sec("sync")
        def _(SY):
            self.W(SY, "sync", [tok["wvrw%d" % h] for h in range(4)])
            tk = None
            for cc in range(NCH):
                tk = self.dma(SY, "dl", S["wD"][:, 256 * cc:256 * cc + 256],
                              H["wvrh"].ap()[0:256, 128 * cc:128 * cc + 128],
                              transpose=True)
            tok["wDl"] = self.qall("dl")

    def qkv_span(self, layer, sp):
        S, I, H, PS, tok = self.S, self.I, self.Hh, self.PS, self.tok
        xb = 0
        agt = tok["ag%d" % layer]

        @self.sec("sync")
        def _(SY):
            self.W(SY, "sync", agt)
            self.W(SY, "sync", tok.get("qkvdone%d" % (sp - 1)))
            lt = []
            for cc in range(NCH):
                lt.append(self.dma(SY, "dl",
                                   S["xnsp"][:, xb + SP * cc:xb + SP * cc + SP],
                                   H["ag_out"].ap()[SP * sp:SP * sp + SP,
                                                    128 * cc:128 * cc + 128],
                                   transpose=True))
            self.W(SY, "sync", tok.get("ropedone%d" % (sp - 1)))
            lt.append(self.dma(SY, "dl", S["tab"][:, 0:SP],
                               I["cos%d" % layer][:, SP * sp:SP * sp + SP]))
            lt.append(self.dma(SY, "dl", S["tab"][:, SP:2 * SP],
                               I["sin%d" % layer][:, SP * sp:SP * sp + SP]))
            tok["xnl"] = lt

        for oc in range(8):
            isq = oc < 4
            wsb = S["wA"] if isq else S["wB"]
            hcol = oc % 4

            @self.sec("tensor")
            def _(Te):
                self.W(Te, "tensor", tok["xnl"])
                self.W(Te, "tensor", tok["lw"])
                if layer == 0:
                    ps = PS["p%d" % (oc % 2)]
                    self.W(Te, "tensor", tok.get("qkev%d" % ((oc - 2) % 8)))
                else:
                    ps = PS["p0"]
                    if oc == 0:
                        self.W(Te, "tensor", ("vs", self.cnt["vs"]))
                    self.W(Te, "tensor", tok.get("qkev%d" % ((oc - 1) % 8)))
                for cc in range(NCH):
                    self.mm(Te, ps[:, :],
                            wsb[:, 512 * cc + 128 * hcol:512 * cc + 128 * hcol + 128],
                            S["xnsp"][:, xb + SP * cc:xb + SP * cc + SP],
                            start=(cc == 0), stop=(cc == NCH - 1),
                            inc=(cc == NCH - 1))
                tok["qkmm%d" % oc] = ("ts", self.cnt["ts"])

            if layer == 1:
                @self.sec("vector")
                def _(V):
                    self.W(V, "vector", tok["qkmm%d" % oc])
                    ps = PS["p0"]
                    slot = S["ev"][:, 512 * oc:512 * oc + 512]
                    bcol = (BC["bq"] if isq else BC["bk"]) + hcol
                    self.W(V, "vector", tok["biasq"])
                    self.W(V, "vector", tok["biask"])
                    tok["qkev%d" % oc] = self.op(
                        V, "vector", "vs", "tensor_scalar", slot, ps[:, :],
                        S["bias"][:, bcol:bcol + 1], None, Alu.add)
            else:
                @self.sec("scalar")
                def _(Sc):
                    self.W(Sc, "scalar", tok["qkmm%d" % oc])
                    ps = PS["p%d" % (oc % 2)]
                    slot = S["ev"][:, 512 * oc:512 * oc + 512]
                    self.W(Sc, "scalar", tok.get("rotmm%d" % ((oc - 2) % 8)))
                    tok["qkev%d" % oc] = self.op(Sc, "scalar", "ss", "copy",
                                                 slot, ps[:, :])

            @self.sec("tensor")
            def _(Te):
                self.W(Te, "tensor", tok["qkev%d" % oc])
                self.W(Te, "tensor", tok["tiny"])
                if layer == 0:
                    prot = PS["p%d" % (2 + oc % 2)]
                    if oc < 2:
                        self.W(Te, "tensor", ("vs", self.cnt["vs"]))
                    self.W(Te, "tensor", tok.get("rope%d" % ((oc - 2) % 8)))
                else:
                    prot = PS["p1"]
                    self.W(Te, "tensor", tok.get("rope%d" % ((oc - 1) % 8)))
                rp = S["tiny"][:, 0:128] if layer == 0 else S["tiny"][:, 128:256]
                tok["rotmm%d" % oc] = self.mm(
                    Te, prot[:, :], rp, S["ev"][:, 512 * oc:512 * oc + 512],
                    start=True, stop=True, inc=True)

            @self.sec("vector")
            def _(V):
                self.W(V, "vector", tok["rotmm%d" % oc])
                self.W(V, "vector", tok["xnl"])
                prot = PS["p%d" % (2 + oc % 2)] if layer == 0 else PS["p1"]
                slot = S["ev"][:, 512 * oc:512 * oc + 512]
                t1 = S["xt"][:, 0:512]
                t2 = S["xt"][:, 512:1024]
                self.W(V, "vector", tok.get("qkw%d" % oc))
                self.op(V, "vector", "vs", "tensor_tensor", t1, slot,
                        S["tab"][:, 0:SP], Alu.mult)
                self.op(V, "vector", "vs", "tensor_tensor", t2, prot[:, :],
                        S["tab"][:, SP:2 * SP], Alu.mult)
                dst = (S["att"] if layer == 0 else S["wD"])[:, 512 * oc:512 * oc + 512]
                tok["rope%d" % oc] = self.op(V, "vector", "vs", "tensor_add",
                                             dst, t1, t2)

            @self.sec("scalar")
            def _(Sc):
                self.W(Sc, "scalar", tok["rope%d" % oc])
                if layer == 0 and sp == 0:
                    self.W(Sc, "scalar", tok.get("wDl"))
                ssrc = (S["att"] if layer == 0 else S["wD"])[:, 512 * oc:512 * oc + 512]
                if isq:
                    tok["qkw%d" % oc] = self.dma(
                        Sc, "dt", H["q0h"].ap()[128 * oc:128 * oc + 128,
                                                SP * sp:SP * sp + SP], ssrc)
                else:
                    kt = oc - 4
                    tok["qkw%d" % oc] = self.dma(
                        Sc, "dt", H["kh"].ap()[:, 2048 * sp + 512 * kt:
                                               2048 * sp + 512 * kt + 512], ssrc)

        if layer == 0:
            for pr in range(2):
                @self.sec("tensor")
                def _(Te):
                    self.W(Te, "tensor", tok["rope%d" % (4 + 2 * pr)])
                    self.W(Te, "tensor", tok["rope%d" % (5 + 2 * pr)])
                    ps = PS["pA"]
                    self.W(Te, "tensor", tok.get("krev%d" % ((pr - 1) % 2)))
                    if pr == 0:
                        self.W(Te, "tensor", tok.get("infog0"))
                        self.W(Te, "tensor", tok.get("infog1"))
                    self.mm(Te, ps[0:64, 0:512], S["tiny"][:, 256:320],
                            S["att"][:, 512 * (4 + 2 * pr):512 * (4 + 2 * pr) + 512],
                            start=True, stop=True, tp=(0, 0))
                    tok["krmm%d" % pr] = self.mm(
                        Te, ps[64:128, 0:512], S["tiny"][:, 256:320],
                        S["att"][:, 512 * (5 + 2 * pr):512 * (5 + 2 * pr) + 512],
                        start=True, stop=True, tp=(0, 64), inc=True)

                @self.sec("vector")
                def _(V):
                    self.W(V, "vector", tok["krmm%d" % pr])
                    tok["krev%d" % pr] = self.op(
                        V, "vector", "vs", "tensor_copy",
                        S["kr"][:, T * pr + SP * sp:T * pr + SP * sp + SP],
                        PS["pA"][:, 0:512])

            self.info_span(sp)

        for st in range(4):
            sig = 4 * sp + st

            @self.sec("tensor")
            def _(Te):
                self.W(Te, "tensor", tok["xnl"])
                self.W(Te, "tensor", tok["lw"])
                ps = PS["p%d" % (st % 2)]
                self.W(Te, "tensor", tok.get("vev%d" % ((st - 2) % 4)))
                if layer == 1 and st < 2:
                    self.W(Te, "tensor", ("vs", self.cnt["vs"]))
                for cc in range(NCH):
                    self.mm(Te, ps[:, :],
                            S["xnsp"][:, xb + SP * cc + 128 * st:
                                      xb + SP * cc + 128 * st + 128],
                            S["wC"][:, 512 * cc:512 * cc + 512],
                            start=(cc == 0), stop=(cc == NCH - 1),
                            inc=(cc == NCH - 1))
                tok["vmm%d" % st] = ("ts", self.cnt["ts"])
                if layer == 0:
                    pvr = PS["p%d" % (2 + st % 2)]
                    self.W(Te, "tensor", tok.get("vrev%d" % ((st - 2) % 4)))
                    if st < 2:
                        self.W(Te, "tensor", tok.get("infos"))
                    self.W(Te, "tensor", tok.get("wDl"))
                    for cc in range(NCH):
                        self.mm(Te, pvr[:, 0:256],
                                S["xnsp"][:, xb + SP * cc + 128 * st:
                                          xb + SP * cc + 128 * st + 128],
                                S["wD"][:, 256 * cc:256 * cc + 256],
                                start=(cc == 0), stop=(cc == NCH - 1),
                                inc=(cc == NCH - 1))
                    tok["vrmm%d" % st] = ("ts", self.cnt["ts"])

            @self.sec("vector")
            def _(V):
                self.W(V, "vector", tok["vmm%d" % st])
                ps = PS["p%d" % (st % 2)]
                varena = S["cx"] if layer == 0 else S["wD"]
                vbuf = varena[:, 1040 * (st % 2):1040 * (st % 2) + 520]
                if layer == 1 and st < 2:
                    self.W(V, "vector", self.qall("dt"))
                self.W(V, "vector", tok.get("vw%d" % ((st - 2) % 4)))
                dst = vbuf.rearrange("p (h e) -> p h e", e=65)[:, :, 0:64]
                srcv = ps[:, :].rearrange("p (h e) -> p h e", e=64)
                if layer == 0:
                    self.op(V, "vector", "vs", "tensor_copy", dst, srcv)
                else:
                    bv = S["nbc"][:, SP:2 * SP].rearrange("p (h e) -> p h e", e=64)
                    self.op(V, "vector", "vs", "tensor_tensor", dst, srcv, bv, Alu.add)
                tok["vev%d" % st] = self.op(
                    V, "vector", "vs", "memset",
                    vbuf.rearrange("p (h e) -> p h e", e=65)[:, :, 64:65], 1.0)
                if layer == 0:
                    self.W(V, "vector", tok["vrmm%d" % st])
                    pvr = PS["p%d" % (2 + st % 2)]
                    dvr = S["vr"][:, 260 * sig:260 * sig + 260].rearrange(
                        "p (h e) -> p h e", e=65)[:, :, 0:64]
                    svr = pvr[:, 0:256].rearrange("p (h e) -> p h e", e=64)
                    self.op(V, "vector", "vs", "tensor_copy", dvr, svr)
                    tok["vrev%d" % st] = self.op(
                        V, "vector", "vs", "memset",
                        S["vr"][:, 260 * sig:260 * sig + 260].rearrange(
                            "p (h e) -> p h e", e=65)[:, :, 64:65], 1.0)

            @self.sec("scalar")
            def _(Sc):
                self.W(Sc, "scalar", tok["vev%d" % st])
                varena = S["cx"] if layer == 0 else S["wD"]
                vbuf = varena[:, 1040 * (st % 2):1040 * (st % 2) + 520]
                tok["vw%d" % st] = self.dma(
                    Sc, "dt", H["vh"].ap()[:, 520 * sig:520 * sig + 520], vbuf)

        tok["qkvdone%d" % sp] = [("ts", self.cnt["ts"]), ("vs", self.cnt["vs"]),
                                 ("ss", self.cnt["ss"])]
        tok["ropedone%d" % sp] = ("vs", self.cnt["vs"])

    def info_span(self, sp):
        S, I, H, PS, tok = self.S, self.I, self.Hh, self.PS, self.tok
        xb = 0

        for cc in range(NCH):
            @self.sec("gpsimd")
            def _(G):
                self.W(G, "gpsimd", tok.get("infomm_c%d" % ((cc - 2) % NCH)))
                tok["w1l%d" % cc] = self.dma(
                    G, "dg", S["w1b"][:, 512 * (cc % 2):512 * (cc % 2) + 512],
                    I["w1T"][128 * cc:128 * cc + 128, :])

            @self.sec("tensor")
            def _(Te):
                self.W(Te, "tensor", tok["w1l%d" % cc])
                self.W(Te, "tensor", tok["xnl"])
                if cc == 0:
                    self.W(Te, "tensor", tok.get("infog3"))
                    self.W(Te, "tensor", tok.get("krev0"))
                    self.W(Te, "tensor", tok.get("krev1"))
                wchunk = S["w1b"][:, 512 * (cc % 2):512 * (cc % 2) + 512]
                for fc in range(4):
                    ps = PS["pA" if fc < 2 else "pB"]
                    self.mm(Te, ps[:, 512 * (fc % 2):512 * (fc % 2) + 512],
                            wchunk[:, 128 * fc:128 * fc + 128],
                            S["xnsp"][:, xb + SP * cc:xb + SP * cc + SP],
                            start=(cc == 0), stop=(cc == NCH - 1),
                            inc=(fc == 3))
                tok["infomm_c%d" % cc] = ("ts", self.cnt["ts"])
        tok["infomm"] = ("ts", self.cnt["ts"])

        for fc in range(4):
            @self.sec("scalar")
            def _(Sc):
                self.W(Sc, "scalar", tok["infomm"])
                ps = PS["pA" if fc < 2 else "pB"]
                tok["infog%d" % fc] = self.op(
                    Sc, "scalar", "ss", "activation",
                    S["hsp"][:, 512 * fc:512 * fc + 512],
                    ps[:, 512 * (fc % 2):512 * (fc % 2) + 512], Act.Gelu)

        @self.sec("tensor")
        def _(Te):
            for fc in range(4):
                self.W(Te, "tensor", tok["infog%d" % fc])
            self.W(Te, "tensor", tok["consts"])
            self.W(Te, "tensor", tok["tiny"])
            ps = PS["p2"]
            self.W(Te, "tensor", tok.get("infoev"))
            self.W(Te, "tensor", tok.get("rope6"))
            for fc in range(4):
                self.mm(Te, ps[32 * fc:32 * fc + 1, :],
                        S["tiny"][:, 520 + fc:521 + fc],
                        S["hsp"][:, 512 * fc:512 * fc + 512],
                        start=True, stop=True, tp=(0, 32 * fc),
                        inc=(fc == 3))
            tok["infomm2"] = ("ts", self.cnt["ts"])

        @self.sec("vector")
        def _(V):
            self.W(V, "vector", tok["infomm2"])
            ps = PS["p2"]
            for fc in range(4):
                self.op(V, "vector", "vs", "tensor_copy",
                        S["svec"][:, 512 * fc:512 * fc + 512], ps[32 * fc:32 * fc + 1, :])
            self.op(V, "vector", "vs", "tensor_add", S["svec"][:, 0:512],
                    S["svec"][:, 0:512], S["svec"][:, 512:1024])
            self.op(V, "vector", "vs", "tensor_add", S["svec"][:, 1024:1536],
                    S["svec"][:, 1024:1536], S["svec"][:, 1536:2048])
            tok["infos"] = self.op(V, "vector", "vs", "tensor_add",
                                   S["svec"][:, 0:512], S["svec"][:, 0:512],
                                   S["svec"][:, 1024:1536])

        @self.sec("scalar")
        def _(Sc):
            self.W(Sc, "scalar", tok["infos"])
            tok["infosg"] = self.op(Sc, "scalar", "ss", "activation",
                                    S["svec"][:, 512:1024], S["svec"][:, 0:512],
                                    Act.Sigmoid)

        @self.sec("vector")
        def _(V):
            self.W(V, "vector", tok["infosg"])
            tok["infoev"] = self.op(V, "vector", "vs", "tensor_scalar",
                                    S["gate"][0:1, SP * sp:SP * sp + SP],
                                    S["svec"][:, 512:1024], 0.75, 1.0,
                                    Alu.is_gt, Alu.mult)

    def ffn0_span(self, sp):
        S, I, H, PS, tok = self.S, self.I, self.Hh, self.PS, self.tok
        xb = 0
        agt = self.tok["ag0"]

        @self.sec("sync")
        def _(SY):
            self.W(SY, "sync", agt)
            self.W(SY, "sync", tok.get("ffdone%d" % (sp - 1)))
            if sp == 0:
                self.W(SY, "sync", tok.get("qkvdone%d" % (NSP - 1)))
                self.W(SY, "sync", tok.get("aLdone%d" % (NSP - 1)))
            lt = []
            for cc in range(NCH):
                lt.append(self.dma(SY, "dl",
                                   S["xnsp"][:, xb + SP * cc:xb + SP * cc + SP],
                                   H["ag_out"].ap()[SP * sp:SP * sp + SP,
                                                    128 * cc:128 * cc + 128],
                                   transpose=True))
            tok["fxnl"] = lt

        for fs4 in range(4):
            wup = S["wA"] if fs4 % 2 == 0 else S["wB"]

            @self.sec("sync")
            def _(SY):
                self.W(SY, "sync", tok["qzwup15"])
                self.W(SY, "sync", tok.get("upmm%d" % (fs4 - 2)))
                lt = []
                for cc in range(NCH):
                    lt.append(self.dma(
                        SY, "dl", wup[:, 512 * cc:512 * cc + 512],
                        H["wqupT"].ap()[512 * fs4:512 * fs4 + 512,
                                        128 * cc:128 * cc + 128],
                        transpose=True))
                tok["upwl"] = lt

            ss_p = self.cnt["ss"]
            t_gel = {fi: ss_p + fi + 1 for fi in range(4)}
            ts_p = self.cnt["ts"]
            t_up = {fi: ts_p + 16 * (fi + 1) for fi in range(4)}

            @self.sec("tensor")
            def _(Te):
                self.W(Te, "tensor", tok["upwl"])
                self.W(Te, "tensor", tok["fxnl"])
                if sp == 0 and fs4 == 0:
                    self.W(Te, "tensor", ("vs", self.cnt["vs"]))
                for fi in range(4):
                    fcg = 4 * fs4 + fi
                    ps = PS["p%d" % (fi % 2)]
                    if fi >= 2:
                        self.W(Te, "tensor", ("ss", t_gel[fi - 2]))
                    elif fs4 > 0:
                        self.W(Te, "tensor", ("ss", tok["gelt%d" % (4 * fs4 - 2 + fi)][1]))
                    for cc in range(NCH):
                        v = self.mm(Te, ps[:, :],
                                    wup[:, 512 * cc + 128 * fi:512 * cc + 128 * fi + 128],
                                    S["xnsp"][:, xb + SP * cc:xb + SP * cc + SP],
                                    start=(cc == 0), stop=(cc == NCH - 1), inc=True)
                    assert v[1] == t_up[fi]

            @self.sec("scalar")
            def _(Sc):
                self.W(Sc, "scalar", tok["bupP"])
                for fi in range(4):
                    fcg = 4 * fs4 + fi
                    self.W(Sc, "scalar", ("ts", t_up[fi]))
                    ps = PS["p%d" % (fi % 2)]
                    v = self.op(Sc, "scalar", "ss", "activation",
                                S["hsp"][:, 512 * fcg:512 * fcg + 512], ps[:, :],
                                Act.Gelu_apprx_tanh, S["bias"][:, 20 + fcg:21 + fcg])
                    assert v[1] == t_gel[fi]
                    tok["gelt%d" % fcg] = ("ss", v[1])
            tok["upmm%d" % fs4] = ("ts", self.cnt["ts"])
        tok["gel15"] = tok["gelt15"]

        for ns in range(4):
            wdn = S["wB"] if ns % 2 == 0 else S["wA"]

            @self.sec("sync")
            def _(SY):
                self.W(SY, "sync", tok["qzwdn15"])
                self.W(SY, "sync",
                       tok["upmm3"] if ns < 2 else tok.get("dnmm%d" % (ns - 2)))
                lt = []
                for fc in range(NCH):
                    lt.append(self.dma(
                        SY, "dl", wdn[:, 512 * fc:512 * fc + 512],
                        H["wqdnT"].ap()[512 * ns:512 * ns + 512,
                                        128 * fc:128 * fc + 128],
                        transpose=True))
                tok["dnwl"] = lt

            for tt in range(4):
                @self.sec("tensor")
                def _(Te):
                    self.W(Te, "tensor", tok["dnwl"])
                    self.W(Te, "tensor", tok["gel15"])
                    ps = PS["p%d" % (tt % 2)]
                    pv = (ns, tt - 2) if tt >= 2 else ((ns - 1) % 4, tt + 2)
                    self.W(Te, "tensor", tok.get("yev%d_%d" % pv))
                    if ns == 0 and tt < 2:
                        self.W(Te, "tensor", ("ss", self.cnt["ss"]))
                    for fc in range(NCH):
                        self.mm(Te, ps[:, :],
                                S["hsp"][:, 512 * fc + 128 * tt:512 * fc + 128 * tt + 128],
                                wdn[:, 512 * fc:512 * fc + 512],
                                start=(fc == 0), stop=(fc == NCH - 1),
                                inc=(fc == NCH - 1))
                    tok["dnps"] = ("ts", self.cnt["ts"])

                @self.sec("vector")
                def _(V):
                    self.W(V, "vector", tok["dnps"])
                    ps = PS["p%d" % (tt % 2)]
                    slot = S["ev"][:, 2048 + 512 * tt:2048 + 512 * tt + 512]
                    pv = (ns, tt - 2) if tt >= 2 else ((ns - 1) % 4, tt + 2)
                    self.W(V, "vector", tok.get("yw%d_%d" % pv, self.qall("dt")))
                    tok["yev%d_%d" % (ns, tt)] = self.op(
                        V, "vector", "vs", "tensor_copy", slot, ps[:, :])

                @self.sec("scalar")
                def _(Sc):
                    self.W(Sc, "scalar", tok["yev%d_%d" % (ns, tt)])
                    tok["yw%d_%d" % (ns, tt)] = self.dma(
                        Sc, "dt",
                        H["rs_in"].ap()[SP * sp + 128 * tt:SP * sp + 128 * tt + 128,
                                        512 * ns:512 * ns + 512],
                        S["ev"][:, 2048 + 512 * tt:2048 + 512 * tt + 512])
            tok["dnmm%d" % ns] = ("ts", self.cnt["ts"])

        tok["ffdone%d" % sp] = [("ts", self.cnt["ts"]), ("vs", self.cnt["vs"]),
                                ("ss", self.cnt["ss"])]

    def attn0_local_span(self, tau):
        S, I, H, PS, tok = self.S, self.I, self.Hh, self.PS, self.tok
        t0 = SP * tau
        sigs = list(range(max(0, 4 * tau - 4), 4 * tau + 4))
        spb = sorted(set(s // 4 for s in sigs))
        spb_ix = {s: i for i, s in enumerate(spb)}

        @self.sec("sync")
        def _(SY):
            self.W(SY, "sync", tok.get("aLdone%d" % (tau - 1)))
            if tau == 0:
                self.W(SY, "sync", tok.get("qkvdone%d" % (NSP - 1)))
            self.W(SY, "sync", self.qall("dt"))
            lt = []
            for i in range(4):
                lt.append(self.dma(SY, "dl", S["qsp"][:, 512 * i:512 * i + 512],
                                   H["q0h"].ap()[128 * i:128 * i + 128, t0:t0 + SP]))
            for bi, sb_ in enumerate(spb):
                for kt in range(4):
                    lt.append(self.dma(
                        SY, "dl",
                        S["kl"][:, 2048 * bi + 512 * kt:2048 * bi + 512 * kt + 512],
                        H["kh"].ap()[:, 2048 * sb_ + 512 * kt:2048 * sb_ + 512 * kt + 512]))
            for i, s in enumerate(sigs):
                lt.append(self.dma(SY, "dl", S["vl"][:, 520 * i:520 * i + 520],
                                   H["vh"].ap()[:, 520 * s:520 * s + 520]))
            tok["ql"] = lt

        for h in range(4):
            pairs = [sigs[2 * i:2 * i + 2] for i in range((len(sigs) + 1) // 2)]
            n = len(pairs)
            ts_p, ss_p, vs_p = self.cnt["ts"], self.cnt["ss"], self.cnt["vs"]
            t_sc, t_av, t_exp, t_msk = {}, {}, {}, {}
            for k in range(n + 2):
                if k < n:
                    ts_p += 1
                    t_sc[k] = ts_p
                if 0 <= k - 2 < n:
                    ts_p += 1
                    t_av[k - 2] = ts_p
            for k in range(n):
                ss_p += 1
                t_exp[k] = ss_p
            for k in range(n):
                vs_p += len(pairs[k])
                t_msk[k] = vs_p

            @self.sec("tensor")
            def _(Te):
                self.W(Te, "tensor", tok["ql"])
                if h == 0:
                    self.W(Te, "tensor", ("vs", self.cnt["vs"]))
                else:
                    self.W(Te, "tensor", tok["ctxl%d" % (h - 1)])
                def emit_sc(k):
                    psc = PS["pA" if k % 2 == 0 else "pB"]
                    if k >= 2:
                        self.W(Te, "tensor", ("ss", t_exp[k - 2]))
                    for k_, s in enumerate(pairs[k]):
                        sb_, st_ = s // 4, s % 4
                        ksl = S["kl"][:, 2048 * spb_ix[sb_] + 512 * h + 128 * st_:
                                      2048 * spb_ix[sb_] + 512 * h + 128 * st_ + 128]
                        v = self.mm(Te, psc[:, 512 * k_:512 * k_ + 512], ksl,
                                    S["qsp"][:, 512 * h:512 * h + 512],
                                    start=True, stop=True,
                                    inc=(k_ == len(pairs[k]) - 1))
                    assert v[1] == t_sc[k]
                def emit_av(k):
                    self.W(Te, "tensor", ("vs", t_msk[k]))
                    base = 2048 if k % 2 == 0 else 3072
                    for k_, s in enumerate(pairs[k]):
                        vi = sigs.index(s)
                        va = S["vl"][:, 520 * vi + 130 * h:520 * vi + 130 * h + 65]
                        vb = S["vl"][:, 520 * vi + 130 * h + 65:520 * vi + 130 * h + 130]
                        first = (k == 0 and k_ == 0)
                        last = (k == n - 1 and k_ == len(pairs[k]) - 1)
                        self.mm(Te, PS["p0"][0:65, :], va,
                                S["att"][:, base + 512 * k_:base + 512 * k_ + 512],
                                start=first, stop=last)
                        v = self.mm(Te, PS["p1"][0:65, :], vb,
                                    S["att"][:, base + 512 * k_:base + 512 * k_ + 512],
                                    start=first, stop=last,
                                    inc=(k_ == len(pairs[k]) - 1))
                    assert v[1] == t_av[k]
                for k in range(n + 2):
                    if k < n:
                        emit_sc(k)
                    if k - 2 >= 0:
                        emit_av(k - 2)
            tok["lav%d" % h] = ("ts", self.cnt["ts"])

            @self.sec("scalar")
            def _(Sc):
                self.W(Sc, "scalar", tok["consts"])
                for k in range(n):
                    self.W(Sc, "scalar", ("ts", t_sc[k]))
                    if k >= 2:
                        self.W(Sc, "scalar", ("ts", t_av[k - 2]))
                    w = 512 * len(pairs[k])
                    base = 0 if k % 2 == 0 else 1024
                    psc = PS["pA" if k % 2 == 0 else "pB"]
                    v = self.op(Sc, "scalar", "ss", "activation",
                                S["att"][:, base:base + w], psc[:, 0:w], Act.Exp,
                                S["bias"][:, BC["mbl"]:BC["mbl"] + 1],
                                S["bias"][:, BC["sc0"]:BC["sc0"] + 1])
                    assert v[1] == t_exp[k]

            @self.sec("vector")
            def _(V):
                self.W(V, "vector", tok["msk"])
                for k in range(n):
                    self.W(V, "vector", ("ss", t_exp[k]))
                    if k >= 2:
                        self.W(V, "vector", ("ts", t_av[k - 2]))
                    base = 0 if k % 2 == 0 else 1024
                    mb = 2048 if k % 2 == 0 else 3072
                    for k_, s in enumerate(pairs[k]):
                        qd = s - 4 * tau + 4
                        v = self.op(V, "vector", "vs", "tensor_tensor",
                                    S["att"][:, mb + 512 * k_:mb + 512 * k_ + 512],
                                    S["att"][:, base + 512 * k_:base + 512 * k_ + 512],
                                    S["msk"][:, SP * qd:SP * qd + SP], Alu.mult)
                    assert v[1] == t_msk[k]

            @self.sec("vector")
            def _(V):
                self.W(V, "vector", tok["lav%d" % h])
                tok["nl%d" % h] = self.op(V, "vector", "vs", "reciprocal",
                                          S["svec"][:, 0:512], PS["p0"][64:65, :])

            @self.sec("scalar")
            def _(Sc):
                self.W(Sc, "scalar", tok["nl%d" % h])
                tok["nlw%d" % h] = self.dma(Sc, "dt", H["nscr"].ap()[0:1, 0:512],
                                            S["svec"][:, 0:512])

            @self.sec("gpsimd")
            def _(G):
                self.W(G, "gpsimd", tok["nlw%d" % h])
                tok["nlb%d" % h] = self.dma(
                    G, "dg", S["nbc"][:, 0:512],
                    H["nscr"].ap()[0:1, 0:512].partition_broadcast(128).squeeze(1))

            @self.sec("vector")
            def _(V):
                self.W(V, "vector", tok["nlb%d" % h])
                self.W(V, "vector", tok.get("ctlw%d" % (tau - 1)))
                self.op(V, "vector", "vs", "tensor_tensor",
                        S["ctl"][0:64, 512 * h:512 * h + 512], PS["p0"][0:64, :],
                        S["nbc"][0:64, 0:512], Alu.mult)
                tok["ctxl%d" % h] = self.op(
                    V, "vector", "vs", "tensor_tensor",
                    S["ctl"][64:128, 512 * h:512 * h + 512], PS["p1"][0:64, :],
                    S["nbc"][64:128, 0:512], Alu.mult)

        @self.sec("scalar")
        def _(Sc):
            self.W(Sc, "scalar", tok["ctxl3"])
            tok["ctlw%d" % tau] = self.dma(
                Sc, "dt", H["ctlh"].ap()[:, 2048 * tau:2048 * tau + 2048],
                S["ctl"][:, :])
        tok["aLdone%d" % tau] = ("ts", self.cnt["ts"])

    def attn0_global_span(self, tau):
        S, I, H, PS, tok = self.S, self.I, self.Hh, self.PS, self.tok
        t0 = SP * tau

        @self.sec("sync")
        def _(SY):
            self.W(SY, "sync", tok.get("aGdone%d" % (tau - 1)))
            if tau == 0:
                self.W(SY, "sync", tok.get("aLdone%d" % (NSP - 1)))
                self.W(SY, "sync", tok.get("wo"))
            self.W(SY, "sync", self.qall("dt"))
            lt = []
            for i in range(4):
                lt.append(self.dma(SY, "dl", S["qsp"][:, 512 * i:512 * i + 512],
                                   H["q0h"].ap()[128 * i:128 * i + 128, t0:t0 + SP]))
            lt.append(self.dma(SY, "dl", S["ctl"][:, :],
                               H["ctlh"].ap()[:, 2048 * tau:2048 * tau + 2048]))
            tok["qgl"] = lt

        for pr in range(2):
            @self.sec("tensor")
            def _(Te):
                self.W(Te, "tensor", tok["qgl"])
                self.W(Te, "tensor", tok["tiny"])
                ps = PS["p2"]
                self.W(Te, "tensor", ("vs", self.cnt["vs"]))
                self.mm(Te, ps[0:64, :], S["tiny"][:, 256:320],
                        S["qsp"][:, 512 * (2 * pr):512 * (2 * pr) + 512],
                        start=True, stop=True, tp=(0, 0))
                tok["qrmm%d" % pr] = self.mm(
                    Te, ps[64:128, :], S["tiny"][:, 256:320],
                    S["qsp"][:, 512 * (2 * pr + 1):512 * (2 * pr + 1) + 512],
                    start=True, stop=True, tp=(0, 64), inc=True)

            @self.sec("vector")
            def _(V):
                self.W(V, "vector", tok["qrmm%d" % pr])
                tok["qrev%d" % pr] = self.op(V, "vector", "vs", "tensor_copy",
                                             S["qr"][:, 512 * pr:512 * pr + 512],
                                             PS["p2"][:, :])

        for pr in range(2):
            n = 32
            ts_p, ss_p = self.cnt["ts"], self.cnt["ss"]
            t_sc, t_av, t_exp = {}, {}, {}
            for k in range(n + 2):
                if k < n:
                    ts_p += 1
                    t_sc[k] = ts_p
                if 0 <= k - 2 < n:
                    ts_p += 1
                    t_av[k - 2] = ts_p
            for k in range(n):
                ss_p += 1
                t_exp[k] = ss_p

            @self.sec("tensor")
            def _(Te):
                self.W(Te, "tensor", tok["qrev0"])
                self.W(Te, "tensor", tok["qrev1"])
                self.W(Te, "tensor", ("vs", self.cnt["vs"]))
                def emit_sc(k):
                    psc = PS["pA" if k % 2 == 0 else "pB"]
                    if k >= 2:
                        self.W(Te, "tensor", ("ss", t_exp[k - 2]))
                    else:
                        self.W(Te, "tensor", ("ss", self.cnt["ss"]))
                    kA = S["kr"][0:64, T * pr + 128 * k:T * pr + 128 * k + 128]
                    kB = S["kr"][64:128, T * pr + 128 * k:T * pr + 128 * k + 128]
                    self.mm(Te, psc[:, 0:512], kA,
                            S["qr"][0:64, 512 * pr:512 * pr + 512],
                            start=True, stop=True)
                    v = self.mm(Te, psc[:, 512:1024], kB,
                                S["qr"][64:128, 512 * pr:512 * pr + 512],
                                start=True, stop=True, inc=True)
                    assert v[1] == t_sc[k]
                def emit_av(k):
                    self.W(Te, "tensor", ("ss", t_exp[k]))
                    base = 0 if k % 2 == 0 else 1024
                    hA, hB = 2 * pr, 2 * pr + 1
                    first, last = (k == 0), (k == 31)
                    self.mm(Te, PS["p2"][0:65, :],
                            S["vr"][:, 260 * k + 65 * hA:260 * k + 65 * hA + 65],
                            S["att"][:, base:base + 512],
                            start=first, stop=last)
                    v = self.mm(Te, PS["p3"][0:65, :],
                                S["vr"][:, 260 * k + 65 * hB:260 * k + 65 * hB + 65],
                                S["att"][:, base + 512:base + 1024],
                                start=first, stop=last, inc=True)
                    assert v[1] == t_av[k]
                for k in range(n + 2):
                    if k < n:
                        emit_sc(k)
                    if k - 2 >= 0:
                        emit_av(k - 2)
            tok["gav%d" % pr] = ("ts", self.cnt["ts"])

            @self.sec("scalar")
            def _(Sc):
                for k in range(n):
                    self.W(Sc, "scalar", ("ts", t_sc[k]))
                    if k >= 2:
                        self.W(Sc, "scalar", ("ts", t_av[k - 2]))
                    base = 0 if k % 2 == 0 else 1024
                    psc = PS["pA" if k % 2 == 0 else "pB"]
                    v = self.op(Sc, "scalar", "ss", "activation",
                                S["att"][:, base:base + 1024], psc[:, :], Act.Exp,
                                S["bias"][:, BC["mbg"]:BC["mbg"] + 1],
                                S["bias"][:, BC["scg"]:BC["scg"] + 1])
                    assert v[1] == t_exp[k]

            for hi in range(2):
                h = 2 * pr + hi
                pg = PS["p2"] if hi == 0 else PS["p3"]

                @self.sec("vector")
                def _(V):
                    self.W(V, "vector", tok["gav%d" % pr])
                    self.op(V, "vector", "vs", "reciprocal", S["svec"][:, 0:512],
                            pg[64:65, :])
                    tok["ng%d" % h] = self.op(
                        V, "vector", "vs", "tensor_tensor", S["svec"][:, 512:1024],
                        S["svec"][:, 0:512], S["gate"][0:1, t0:t0 + SP], Alu.mult)

                @self.sec("scalar")
                def _(Sc):
                    self.W(Sc, "scalar", tok["ng%d" % h])
                    tok["ngw%d" % h] = self.dma(Sc, "dt", H["nscr"].ap()[0:1, SP:2 * SP],
                                                S["svec"][:, 512:1024])

                @self.sec("gpsimd")
                def _(G):
                    self.W(G, "gpsimd", tok["ngw%d" % h])
                    tok["ngb%d" % h] = self.dma(
                        G, "dg", S["nbc"][:, 0:512],
                        H["nscr"].ap()[0:1, SP:2 * SP].partition_broadcast(128).squeeze(1))

                @self.sec("vector")
                def _(V):
                    self.W(V, "vector", tok["ngb%d" % h])
                    tok["cgs%d" % hi] = self.op(
                        V, "vector", "vs", "tensor_tensor",
                        S["ev"][0:64, 512 * hi:512 * hi + 512],
                        pg[0:64, :], S["nbc"][0:64, 0:512], Alu.mult)

            for hi in range(2):
                h = 2 * pr + hi

                @self.sec("tensor")
                def _(Te):
                    self.W(Te, "tensor", tok["cgs%d" % hi])
                    ps = PS["pA"][:, 512 * hi:512 * hi + 512]
                    self.W(Te, "tensor", ("ss", self.cnt["ss"]))
                    self.W(Te, "tensor", tok.get("ctxf%d" % (h - 2)))
                    tok["uomm%d" % h] = self.mm(
                        Te, ps, S["tiny"][0:64, 320:448],
                        S["ev"][0:64, 512 * hi:512 * hi + 512],
                        start=True, stop=True, inc=True)

                @self.sec("vector")
                def _(V):
                    self.W(V, "vector", tok["uomm%d" % h])
                    ps = PS["pA"][:, 512 * hi:512 * hi + 512]
                    self.W(V, "vector", tok.get("womm_t"))
                    tok["ctxf%d" % h] = self.op(
                        V, "vector", "vs", "tensor_tensor",
                        S["cx"][:, 512 * h:512 * h + 512],
                        S["ctl"][:, 512 * h:512 * h + 512], ps, Alu.add)

        for tt in range(4):
            for ns in range(4):
                @self.sec("tensor")
                def _(Te):
                    for h in range(4):
                        self.W(Te, "tensor", tok["ctxf%d" % h])
                    self.W(Te, "tensor", tok["wo"])
                    ps = PS["pB"][:, 512 * (ns % 2):512 * (ns % 2) + 512]
                    self.W(Te, "tensor", ("ss", self.cnt["ss"]))
                    pv = (ns - 2, tt) if ns >= 2 else ((ns + 2), tt - 1)
                    self.W(Te, "tensor", tok.get("woev%d_%d" % (tt, ns - 2) if ns >= 2
                                                 else "woev%d_%d" % (tt - 1, ns + 2)))
                    for cc in range(4):
                        self.mm(Te, ps,
                                S["cx"][:, 512 * cc + 128 * tt:512 * cc + 128 * tt + 128],
                                S["wC"][:, 2048 * cc + 512 * ns:2048 * cc + 512 * ns + 512],
                                start=(cc == 0), stop=(cc == 3), inc=(cc == 3))
                    tok["wops%d_%d" % (tt, ns)] = ("ts", self.cnt["ts"])

                @self.sec("vector")
                def _(V):
                    self.W(V, "vector", tok["wops%d_%d" % (tt, ns)])
                    self.W(V, "vector", tok.get("wow%d_%d" % (tt - 1, ns)))
                    tok["woev%d_%d" % (tt, ns)] = self.op(
                        V, "vector", "vs", "tensor_copy",
                        S["ev"][:, 512 * ns:512 * ns + 512],
                        PS["pB"][:, 512 * (ns % 2):512 * (ns % 2) + 512])

                @self.sec("gpsimd")
                def _(G):
                    self.W(G, "gpsimd", tok["woev%d_%d" % (tt, ns)])
                    if tt == 0 and ns == 0:
                        self.W(G, "gpsimd", self.qall("dt"))
                    tok["wow%d_%d" % (tt, ns)] = self.dma(
                        G, "dg",
                        H["rs_in"].ap()[t0 + 128 * tt:t0 + 128 * tt + 128,
                                        512 * ns:512 * ns + 512],
                        S["ev"][:, 512 * ns:512 * ns + 512], accum=True)
        tok["womm_t"] = ("ts", self.cnt["ts"])
        tok["aGdone%d" % tau] = ("ts", self.cnt["ts"])

    def attn1_span(self, tau):
        S, I, H, PS, tok = self.S, self.I, self.Hh, self.PS, self.tok
        t0 = SP * tau
        nsig = 4 * tau + 4

        @self.sec("sync")
        def _(SY):
            self.W(SY, "sync", tok.get("a1done%d" % (tau - 1)))
            if tau == 0:
                self.W(SY, "sync", tok.get("qkvdone%d" % (NSP - 1)))
                self.W(SY, "sync", tok.get("wo"))
            self.W(SY, "sync", self.qall("dt"))
            lt = []
            for i in range(4):
                lt.append(self.dma(SY, "dl", S["qsp"][:, 512 * i:512 * i + 512],
                                   H["q0h"].ap()[128 * i:128 * i + 128, t0:t0 + SP]))
            tok["ql"] = lt

        for grp in range(2):
            pcs = [PS["p0"], PS["p1"], PS["p2"], PS["p3"]]
            nblk = (nsig + 7) // 8
            for blkb in range(nblk):
                lo, hi = 8 * blkb, min(8 * blkb + 8, nsig)

                @self.sec("sync")
                def _(SY):
                    if blkb == 0:
                        self.W(SY, "sync", ("ts", self.cnt["ts"]))
                    self.W(SY, "sync", tok.get("l1b%d_%d" % (grp, blkb - 1)))
                    lt = []
                    for i, s in enumerate(range(lo, hi)):
                        lt.append(self.dma(SY, "dl", S["vl1"][:, 520 * i:520 * i + 520],
                                           H["vh"].ap()[:, 520 * s:520 * s + 520]))
                        sb_, st_ = s // 4, s % 4
                        for pri in range(2):
                            pr = 2 * grp + pri
                            lt.append(self.dma(
                                SY, "dl",
                                S["kl1"][:, 1024 * i + 128 * pri:1024 * i + 128 * pri + 128],
                                H["kh"].ap()[:, 2048 * sb_ + 512 * pr + 128 * st_:
                                             2048 * sb_ + 512 * pr + 128 * st_ + 128]))
                    tok["kvl"] = lt

                steps = [(s, pri) for s in range(lo, hi) for pri in range(2)]
                n = len(steps)
                ts_p, ss_p, vs_p = self.cnt["ts"], self.cnt["ss"], self.cnt["vs"]
                t_sc, t_av, t_exp, t_msk = {}, {}, {}, {}
                for k in range(n + 2):
                    if k < n:
                        ts_p += 1
                        t_sc[k] = ts_p
                    if 0 <= k - 2 < n:
                        ts_p += 1
                        t_av[k - 2] = ts_p
                for k in range(n):
                    ss_p += 1
                    t_exp[k] = ss_p
                for k in range(n):
                    s, pri = steps[k]
                    if s >= 4 * tau:
                        vs_p += 2
                        t_msk[k] = vs_p

                @self.sec("tensor")
                def _(Te):
                    self.W(Te, "tensor", tok["kvl"])
                    self.W(Te, "tensor", tok["ql"])
                    if blkb == 0 and grp == 0:
                        self.W(Te, "tensor", ("vs", self.cnt["vs"]))
                    if grp == 1 and blkb == 0:
                        self.W(Te, "tensor", [tok["c1f%d" % hh] for hh in range(4)])
                    def emit_sc(k):
                        s, pri = steps[k]
                        pr = 2 * grp + pri
                        i = s - lo
                        pnm = "pA" if k % 2 == 0 else "pB"
                        psc = PS[pnm]
                        if k >= 2:
                            self.W(Te, "tensor", ("ss", t_exp[k - 2]))
                        else:
                            self.W(Te, "tensor", ("ss", self.cnt["ss"]))
                        ks = S["kl1"][:, 1024 * i + 128 * pri:1024 * i + 128 * pri + 128]
                        self.mm(Te, psc[:, 0:512], ks[0:64, :],
                                S["qsp"][0:64, 512 * pr:512 * pr + 512],
                                start=True, stop=True)
                        v = self.mm(Te, psc[:, 512:1024], ks[64:128, :],
                                    S["qsp"][64:128, 512 * pr:512 * pr + 512],
                                    start=True, stop=True, inc=True)
                        assert v[1] == t_sc[k], (v, t_sc[k])
                    def emit_av(k):
                        s, pri = steps[k]
                        i = s - lo
                        diag = (s >= 4 * tau)
                        base = 0 if k % 2 == 0 else 1024
                        if diag:
                            self.W(Te, "tensor", ("vs", t_msk[k]))
                            asl = 2048 + base
                        else:
                            self.W(Te, "tensor", ("ss", t_exp[k]))
                            asl = base
                        pcA, pcB = pcs[2 * pri], pcs[2 * pri + 1]
                        pr = 2 * grp + pri
                        first, last = (s == 0), (s == nsig - 1)
                        self.mm(Te, pcA[0:65, :],
                                S["vl1"][:, 520 * i + 65 * (2 * pr):
                                        520 * i + 65 * (2 * pr) + 65],
                                S["att"][:, asl:asl + 512],
                                start=first, stop=last)
                        v = self.mm(Te, pcB[0:65, :],
                                    S["vl1"][:, 520 * i + 65 * (2 * pr + 1):
                                            520 * i + 65 * (2 * pr + 1) + 65],
                                    S["att"][:, asl + 512:asl + 1024],
                                    start=first, stop=last, inc=True)
                        assert v[1] == t_av[k], (v, t_av[k])
                    for k in range(n + 2):
                        if k < n:
                            emit_sc(k)
                        if k - 2 >= 0:
                            emit_av(k - 2)

                @self.sec("scalar")
                def _(Sc):
                    for k in range(n):
                        self.W(Sc, "scalar", ("ts", t_sc[k]))
                        if k >= 2:
                            self.W(Sc, "scalar", ("ts", t_av[k - 2]))
                        base = 0 if k % 2 == 0 else 1024
                        pnm = "pA" if k % 2 == 0 else "pB"
                        v = self.op(Sc, "scalar", "ss", "activation",
                                    S["att"][:, base:base + 1024], PS[pnm][:, :],
                                    Act.Exp,
                                    S["bias"][:, BC["mb1"]:BC["mb1"] + 1],
                                    S["bias"][:, BC["sc1"]:BC["sc1"] + 1])
                        assert v[1] == t_exp[k], (v, t_exp[k])

                @self.sec("vector")
                def _(V):
                    self.W(V, "vector", tok["msk"])
                    for k in range(n):
                        s, pri = steps[k]
                        if s < 4 * tau:
                            continue
                        self.W(V, "vector", ("ss", t_exp[k]))
                        if k >= 2:
                            self.W(V, "vector", ("ts", t_av[k - 2]))
                        base = 0 if k % 2 == 0 else 1024
                        qd = s - 4 * tau
                        ms = S["msk"][:, SP * qd:SP * qd + SP]
                        self.op(V, "vector", "vs", "tensor_tensor",
                                S["att"][:, 2048 + base:2048 + base + 512],
                                S["att"][:, base:base + 512], ms, Alu.mult)
                        v = self.op(V, "vector", "vs", "tensor_tensor",
                                    S["att"][:, 2048 + base + 512:2048 + base + 1024],
                                    S["att"][:, base + 512:base + 1024], ms, Alu.mult)
                        assert v[1] == t_msk[k], (v, t_msk[k])

                tok["l1b%d_%d" % (grp, blkb)] = ("ts", self.cnt["ts"])

            for pri in range(2):
                pr = 2 * grp + pri
                for hi2 in range(2):
                    h = 2 * pr + hi2
                    pg = pcs[2 * pri + hi2]

                    @self.sec("vector")
                    def _(V):
                        self.W(V, "vector", ("ts", self.cnt["ts"]))
                        tok["n1%d" % h] = self.op(V, "vector", "vs", "reciprocal",
                                                  S["svec"][:, 0:512], pg[64:65, :])

                    @self.sec("scalar")
                    def _(Sc):
                        self.W(Sc, "scalar", tok["n1%d" % h])
                        tok["n1w%d" % h] = self.dma(Sc, "dt", H["nscr"].ap()[0:1, 0:512],
                                                    S["svec"][:, 0:512])

                    @self.sec("gpsimd")
                    def _(G):
                        self.W(G, "gpsimd", tok["n1w%d" % h])
                        tok["n1b%d" % h] = self.dma(
                            G, "dg", S["nbc"][:, 0:512],
                            H["nscr"].ap()[0:1, 0:512].partition_broadcast(128).squeeze(1))

                    @self.sec("vector")
                    def _(V):
                        self.W(V, "vector", tok["n1b%d" % h])
                        self.W(V, "vector", tok.get("womm_t"))
                        tok["c1f%d" % h] = self.op(
                            V, "vector", "vs", "tensor_tensor",
                            S["cx"][64 * (h % 2):64 * (h % 2) + 64,
                                    512 * (h // 2):512 * (h // 2) + 512],
                            pg[0:64, :], S["nbc"][0:64, 0:512], Alu.mult)

        for tt in range(4):
            for ns in range(4):
                @self.sec("tensor")
                def _(Te):
                    for h in range(8):
                        self.W(Te, "tensor", tok["c1f%d" % h])
                    self.W(Te, "tensor", tok["wo"])
                    ps = PS["pB"][:, 512 * (ns % 2):512 * (ns % 2) + 512]
                    self.W(Te, "tensor", ("ss", self.cnt["ss"]))
                    self.W(Te, "tensor", tok.get("woev%d_%d" % (tt, ns - 2) if ns >= 2
                                                 else "woev%d_%d" % (tt - 1, ns + 2)))
                    for cc in range(4):
                        self.mm(Te, ps,
                                S["cx"][:, 512 * cc + 128 * tt:512 * cc + 128 * tt + 128],
                                S["hsp"][:, 2048 * cc + 512 * ns:2048 * cc + 512 * ns + 512],
                                start=(cc == 0), stop=(cc == 3), inc=(cc == 3))
                    tok["wops%d_%d" % (tt, ns)] = ("ts", self.cnt["ts"])

                @self.sec("vector")
                def _(V):
                    self.W(V, "vector", tok["wops%d_%d" % (tt, ns)])
                    self.W(V, "vector", tok.get("wow%d_%d" % (tt - 1, ns)))
                    tok["woev%d_%d" % (tt, ns)] = self.op(
                        V, "vector", "vs", "tensor_copy",
                        S["ctl"][:, 512 * ns:512 * ns + 512],
                        PS["pB"][:, 512 * (ns % 2):512 * (ns % 2) + 512])

                @self.sec("scalar")
                def _(Sc):
                    self.W(Sc, "scalar", tok["woev%d_%d" % (tt, ns)])
                    tok["wow%d_%d" % (tt, ns)] = self.dma(
                        Sc, "dt",
                        H["rs_in"].ap()[t0 + 128 * tt:t0 + 128 * tt + 128,
                                        512 * ns:512 * ns + 512],
                        S["ctl"][:, 512 * ns:512 * ns + 512])
        tok["womm_t"] = ("ts", self.cnt["ts"])
        tok["a1done%d" % tau] = ("ts", self.cnt["ts"])

    def p6(self):
        S, I, H, PS, tok = self.S, self.I, self.Hh, self.PS, self.tok

        @self.sec("gpsimd")
        def _(G):
            self.W(G, "gpsimd", ("vs", self.cnt["vs"]))
            tok["bdnb"] = self.dma(G, "dg", S["bcst"][:, :],
                                   I["bdn"].partition_broadcast(128).squeeze(1))

        for ti in range(8):
            @self.sec("sync")
            def _(SY):
                self.W(SY, "sync", ("vs", self.cnt["vs"]))
                self.W(SY, "sync", tok["rs0"])
                self.W(SY, "sync", tok.get("x1w%d" % (ti - 1)))
                tok["x0l"] = self.dma(SY, "dl", S["xt"][:, :],
                                      I["xin"][128 * ti:128 * ti + 128, :])
                tok["yfl"] = self.dma(SY, "dl", S["cx"][:, :],
                                      H["rs_out"].ap()[128 * ti:128 * ti + 128, :])

            @self.sec("vector")
            def _(V):
                self.W(V, "vector", tok["x0l"])
                self.W(V, "vector", tok["yfl"])
                self.W(V, "vector", tok["rs0"])
                self.W(V, "vector", tok.get("x1w%d" % (ti - 1)))
                self.op(V, "vector", "vs", "tensor_add", S["xt2"][:, :],
                        S["xt"][:, :], S["cx"][:, :])
                self.W(V, "vector", tok["bdnb"])
                tok["x1t"] = self.op(V, "vector", "vs", "tensor_add", S["xt"][:, :],
                                     S["xt2"][:, :], S["bcst"][:, :])

            @self.sec("scalar")
            def _(Sc):
                self.W(Sc, "scalar", tok["x1t"])
                tok["x1w%d" % ti] = self.dma(
                    Sc, "dt", H["x1h"].ap()[128 * ti:128 * ti + 128, :], S["xt"][:, :])

        @self.sec("sync")
        def _(SY):
            self.W(SY, "sync", self.qall("dt"))
        self.p1(layer=1)

    def p9_p10(self):
        S, I, H, PS, tok = self.S, self.I, self.Hh, self.PS, self.tok

        @self.sec("gpsimd")
        def _(G):
            self.W(G, "gpsimd", ("vs", self.cnt["vs"]))
            tok["bffb"] = self.dma(G, "dg", S["bcst"][:, :],
                                   I["bff1"].partition_broadcast(128).squeeze(1))

        @self.sec("gpsimd")
        def _(G):
            self.W(G, "gpsimd", tok["bffb"])
            tok["bcomb"] = self.dma(G, "dg", S["bcst"][:, :],
                                    I["bo1"].partition_broadcast(128).squeeze(1),
                                    accum=True)

        for osp in range(2):
            @self.sec("sync")
            def _(SY):
                self.W(SY, "sync", tok.get("p9done%d" % (osp - 1)))
                lt = []
                for cc in range(NCH):
                    lt.append(self.dma(SY, "dl",
                                       S["xnsp"][:, SP * cc:SP * cc + SP],
                                       H["xqh"].ap()[SP * osp:SP * osp + SP,
                                                     128 * cc:128 * cc + 128],
                                       transpose=True))
                tok["xql"] = lt

            for ns in range(4):
                @self.sec("sync")
                def _(SY):
                    self.W(SY, "sync", tok["tqw15"])
                    self.W(SY, "sync", tok.get("f1done%d" % (ns - 1)))
                    lt = []
                    for kc in range(NCH):
                        lt.append(self.dma(
                            SY, "dl", S["wA"][:, 512 * kc:512 * kc + 512],
                            H["wt1q"].ap()[512 * ns:512 * ns + 512,
                                           128 * kc:128 * kc + 128],
                            transpose=True))
                    tok["wtl"] = lt

                for tt in range(4):
                    ti = 4 * osp + tt

                    @self.sec("tensor")
                    def _(Te):
                        self.W(Te, "tensor", tok["wtl"])
                        self.W(Te, "tensor", tok["xql"])
                        ps = PS["p%d" % (tt % 2)]
                        pv = (ns, tt - 2) if tt >= 2 else ((ns - 1) % 4, tt + 2)
                        self.W(Te, "tensor", tok.get("x2ev%d_%d" % pv))
                        for kc in range(NCH):
                            self.mm(Te, ps[:, :],
                                    S["xnsp"][:, SP * kc + 128 * tt:SP * kc + 128 * tt + 128],
                                    S["wA"][:, 512 * kc:512 * kc + 512],
                                    start=(kc == 0), stop=(kc == NCH - 1),
                                    inc=(kc == NCH - 1))
                        tok["f1ps%d_%d" % (ns, tt)] = ("ts", self.cnt["ts"])

                    @self.sec("sync")
                    def _(SY):
                        self.W(SY, "sync", ("vs", self.cnt["vs"]))
                        self.W(SY, "sync", tok["rs1"])
                        tok["x1l"] = self.dma(
                            SY, "dl", S["xt"][:, 0:512],
                            H["x1h"].ap()[128 * ti:128 * ti + 128,
                                          512 * ns:512 * ns + 512])
                        tok["ya1"] = self.dma(
                            SY, "dl", S["cx"][:, 0:512],
                            H["rs_out"].ap()[128 * ti:128 * ti + 128,
                                             512 * ns:512 * ns + 512])

                    @self.sec("vector")
                    def _(V):
                        self.W(V, "vector", tok["f1ps%d_%d" % (ns, tt)])
                        self.W(V, "vector", self.qall("dt"))
                        self.W(V, "vector", tok["gbc"])
                        self.W(V, "vector", tok["bcomb"])
                        self.W(V, "vector", tok["x1l"])
                        self.W(V, "vector", tok["ya1"])
                        self.W(V, "vector", tok["rs1"])
                        ps = PS["p%d" % (tt % 2)]
                        xs = S["sm"][:, 40 + ti:41 + ti]
                        self.op(V, "vector", "vs", "tensor_scalar",
                                S["xt"][:, 512:1024], ps[:, :], xs, None, Alu.mult)
                        self.op(V, "vector", "vs", "tensor_tensor",
                                S["xt"][:, 1024:1536], S["xt"][:, 512:1024],
                                S["gbc"][:, 512 * ns:512 * ns + 512], Alu.mult)
                        self.op(V, "vector", "vs", "tensor_add",
                                S["xt"][:, 1536:2048], S["xt"][:, 0:512],
                                S["cx"][:, 0:512])
                        self.op(V, "vector", "vs", "tensor_add",
                                S["xt"][:, 1536:2048], S["xt"][:, 1536:2048],
                                S["xt"][:, 1024:1536])
                        self.op(V, "vector", "vs", "tensor_tensor",
                                S["xt2"][:, 0:512], S["xt"][:, 1536:2048],
                                S["bcst"][:, 512 * ns:512 * ns + 512],
                                Alu.add)
                        tok["x2ev%d_%d" % (ns, tt)] = ("vs", self.cnt["vs"])

                    @self.sec("scalar")
                    def _(Sc):
                        self.W(Sc, "scalar", tok["x2ev%d_%d" % (ns, tt)])
                        tok["youtw%d_%d" % (ns, ti)] = self.dma(
                            Sc, "dt",
                            self.yout[128 * ti:128 * ti + 128, 512 * ns:512 * ns + 512],
                            S["xt2"][:, 0:512])
                tok["f1done%d" % ns] = ("ts", self.cnt["ts"])
            tok["p9done%d" % osp] = ("ts", self.cnt["ts"])

    def finish(self):
        S, H = self.S, self.Hh

        @self.sec("gpsimd")
        def _(G):
            for s in ("ts", "vs", "ss", "cc"):
                self.W(G, "gpsimd", (s, self.cnt[s]))
            self.W(G, "gpsimd", self.allq())
            n = 0
            for name, ap in self.dbg_out.items():
                if name in H:
                    G.dma_start(out=ap, in_=H[name].ap()).then_inc(self.h["fin"], 16)
                    n += 1
                elif name in S:
                    G.dma_start(out=ap, in_=S[name][:, :]).then_inc(self.h["fin"], 16)
                    n += 1
            self.cnt["fin"] += 16 * n
            if self.cnt["fin"] > 0:
                G.wait_ge(self.h["fin"], self.cnt["fin"])


def sm_col(S, i):
    return S["sm"][:, i:i + 1]


def build_program(dbg=(), upto="full"):
    k = Kern(dbg, upto)
    nc = k.build()
    return nc, k


def kernel(**inputs):
    nc, k = build_program()
    in_maps = [prep_core_inputs(c, inputs) for c in range(NCORE)]
    res = run_bass_kernel_spmd(nc, in_maps, core_ids=list(range(NCORE)))
    out = np.zeros((B, T, C), np.float32)
    for c in range(NCORE):
        b, j = c // 4, c % 4
        out[b, OWN * j:OWN * j + OWN, :] = res.results[c]["yout"]
    return out


# version 06 (final)
# speedup vs baseline: 1.0950x; 1.0588x over previous
import numpy as np
import ml_dtypes

import concourse.bass as bass
import concourse.mybir as mybir
from concourse.bass_utils import run_bass_kernel_spmd

F32 = mybir.dt.float32
BF16 = mybir.dt.bfloat16
I32 = mybir.dt.int32
U16 = mybir.dt.uint16
Alu = mybir.AluOpType
Act = mybir.ActivationFunctionType
AxX = mybir.AxisListType.X

B, T, C = 2, 4096, 2048
NCORE = 8
OWN = 1024
SP = 512
NSP = T // SP
NCH = C // 128
FS = 2048
W_LOCAL = 512
SC0 = float(1.0 / np.sqrt(128.0))
SCG = float(SC0 * 128.0 / 64.0)
SC1 = float(1.0 / np.sqrt(64.0))
MB_L, MB_G, MB_1 = 20.0, 15.0, 25.0
GROUPS = [[0, 1, 2, 3], [4, 5, 6, 7]]

BC = dict(sc0=0, mbl=1, scg=2, mbg=3, sc1=4, mb1=5, bq=10, bk=14)


def _f32c(x):
    return np.ascontiguousarray(x, dtype=np.float32)


def _host_tables():
    t = {}
    inv = (1.0 / (10000.0 ** (np.arange(0, 128, 2, dtype=np.float32) / 128.0))).astype(np.float32)
    tt = np.arange(T, dtype=np.float32)
    fr = tt[:, None] * inv[None, :]
    emb = np.concatenate([fr, fr], axis=1)
    t["cos0"] = _f32c(np.cos(emb).T)
    t["sin0"] = _f32c(np.sin(emb).T)
    P = np.zeros((128, 128), np.float32)
    for i in range(64):
        P[2 * i, 2 * i + 1] = -1.0
        P[2 * i + 1, 2 * i] = 1.0
    t["rotP0T"] = np.ascontiguousarray(P.T, dtype=ml_dtypes.bfloat16)
    th = (10000.0 ** (-np.arange(0, 64, 2, dtype=np.float32) / 64.0)).astype(np.float32)
    tid = np.floor((np.arange(T) // 576).astype(np.float32) * 0.5).astype(np.float32)
    ang = (tt[:, None] + tid[:, None]) * th[None, :]
    ang2 = np.concatenate([ang, ang], axis=1)
    c1, s1 = np.cos(ang2).T, np.sin(ang2).T
    t["cos1"] = _f32c(np.concatenate([c1, c1], axis=0))
    t["sin1"] = _f32c(np.concatenate([s1, s1], axis=0))
    P1 = np.zeros((64, 64), np.float32)
    for i in range(32):
        P1[i, i + 32] = -1.0
        P1[i + 32, i] = 1.0
    z = np.zeros_like(P1)
    t["rotP1T"] = np.ascontiguousarray(np.block([[P1.T, z], [z, P1.T]]),
                                       dtype=ml_dtypes.bfloat16)
    m0 = np.zeros((128, 8 * SP), np.float32)
    for q in range(8):
        rho = -512 + 128 * q
        srel = np.arange(128)[:, None] + rho
        trel = np.arange(SP)[None, :]
        m0[:, q * SP:(q + 1) * SP] = (srel <= trel) & (srel >= trel - (W_LOCAL - 1))
    t["msk0"] = np.ascontiguousarray(m0, dtype=ml_dtypes.bfloat16)
    m1 = np.zeros((128, 4 * SP), np.float32)
    for q in range(4):
        m1[:, q * SP:(q + 1) * SP] = (np.arange(128)[:, None] + 128 * q) <= np.arange(SP)[None, :]
    t["msk1"] = np.ascontiguousarray(m1, dtype=ml_dtypes.bfloat16)
    t["idt32"] = np.eye(128, dtype=np.float32)
    return t


_TAB = None


def prep_core_inputs(core, inp):
    global _TAB
    if _TAB is None:
        _TAB = _host_tables()
    b, j = core // 4, core % 4
    d = {}
    x = np.asarray(inp["L0_attn_x"])
    d["xin"] = _f32c(x[b, OWN * j:OWN * j + OWN, :])
    wqkv = np.asarray(inp["L0_attn_w_qkv"])
    h0 = 4 * j
    d["wqT"] = _f32c(wqkv[128 * h0:128 * h0 + 512, :].T)
    d["wkT"] = _f32c(wqkv[2048 + 128 * h0:2048 + 128 * h0 + 512, :].T)
    d["wvT"] = _f32c(wqkv[4096 + 128 * h0:4096 + 128 * h0 + 512, :].T)
    d["wvN"] = _f32c(wqkv[4096 + 128 * h0:4096 + 128 * h0 + 512, :])
    d["pkT"] = _f32c(np.asarray(inp["L0_attn_pk"]).T)
    d["pvT"] = _f32c(np.asarray(inp["L0_attn_pv"]).T)
    d["uoT"] = _f32c(np.asarray(inp["L0_attn_uo"]).T)
    d["w1T"] = _f32c(np.asarray(inp["L0_attn_info_w1"]).T)
    d["w2c"] = _f32c(np.asarray(inp["L0_attn_info_w2"]).reshape(4, 128).T)
    d["woT0s"] = _f32c(np.asarray(inp["L0_attn_w_o"]).T[512 * j:512 * j + 512, :])
    d["wupT"] = _f32c(np.asarray(inp["L0_ffn_up_kernel"]).T[FS * j:FS * j + FS, :])
    d["wdnT"] = _f32c(np.asarray(inp["L0_ffn_down_kernel"]).T[:, FS * j:FS * j + FS])
    d["bupP"] = _f32c(np.asarray(inp["L0_ffn_up_bias"])[FS * j:FS * j + FS].reshape(16, 128).T)
    d["bdn"] = _f32c(np.asarray(inp["L0_ffn_down_bias"]).reshape(1, C))
    hb = 512 * j
    d["wq1T"] = _f32c(np.asarray(inp["L1_attn_Wq"]).T[:, hb:hb + 512])
    d["wk1T"] = _f32c(np.asarray(inp["L1_attn_Wk"]).T[:, hb:hb + 512])
    d["wv1T"] = _f32c(np.asarray(inp["L1_attn_Wv"]).T[:, hb:hb + 512])
    d["bq1"] = _f32c(np.asarray(inp["L1_attn_bq"])[hb:hb + 512].reshape(4, 128).T)
    d["bk1"] = _f32c(np.asarray(inp["L1_attn_bk"])[hb:hb + 512].reshape(4, 128).T)
    d["bv1"] = _f32c(np.asarray(inp["L1_attn_bv"])[hb:hb + 512].reshape(1, 512))
    d["bo1"] = _f32c(np.asarray(inp["L1_attn_bo"]).reshape(1, C))
    d["wo1Ts"] = _f32c(np.asarray(inp["L1_attn_Wo"]).T[hb:hb + 512, :])
    d["wt1"] = _f32c(np.asarray(inp["L1_ffn_weight"]))
    d["bff1"] = _f32c(np.asarray(inp["L1_ffn_bias"]).reshape(1, C))
    d.update(_TAB)
    return d


class Kern:
    def __init__(self, dbg=(), upto="full"):
        self.dbg = set(dbg)
        self.upto = upto
        self.dbg_out = {}
        self.nc = bass.Bass("TRN2", target_bir_lowering=False, debug=False)
        self.tok = {}

    def sem(self, name):
        self.h[name] = self.stk.enter_context(self.nc.semaphore(name))
        self.cnt[name] = 0

    def W(self, eng, ename, token):
        if token is None:
            return
        if isinstance(token, list):
            for t in token:
                self.W(eng, ename, t)
            return
        sname, val = token
        if sname[0] == "d" and sname[:2] in ("dl", "dg", "dt"):
            val = self.cnt[sname]
        key = (ename, sname)
        if val <= 0 or self.wm.get(key, 0) >= val:
            return
        eng.wait_ge(self.h[sname], val)
        self.wm[key] = val

    def _selfwait(self, eng, ename, sname):
        self.W(eng, ename, (sname, self.cnt[sname]))

    def op(self, eng, ename, sname, fname, *a, **kw):
        self._selfwait(eng, ename, sname)
        inst = getattr(eng, fname)(*a, **kw)
        inst.then_inc(self.h[sname], 1)
        self.cnt[sname] += 1
        return (sname, self.cnt[sname])

    def mm(self, Te, out, l, r, start, stop, tp=None, inc=False):
        kw = {}
        if tp is not None:
            kw["tile_position"] = tp
        if inc:
            self._selfwait(Te, "tensor", "ts")
        inst = Te.matmul(out, l, r, start=start, stop=stop, **kw)
        if inc:
            inst.then_inc(self.h["ts"], 1)
            self.cnt["ts"] += 1
            return ("ts", self.cnt["ts"])
        return None

    def tr(self, Te, out, in_, ident):
        self._selfwait(Te, "tensor", "ts")
        inst = Te.transpose(out, in_, ident)
        inst.then_inc(self.h["ts"], 1)
        self.cnt["ts"] += 1
        return ("ts", self.cnt["ts"])

    def dma(self, eng, qname, out, in_, transpose=False, accum=False):
        if self._dmasec.get(qname) != self._secn:
            self._dmasec[qname] = self._secn
            self.qi[qname] += 1
        i = self.qi[qname] % 4
        sname = "%s%d" % (qname, i)
        if transpose:
            inst = eng.dma_start_transpose(out=out, in_=in_)
        elif accum:
            inst = eng.dma_start(out=out, in_=in_, accum_op=Alu.add)
        else:
            inst = eng.dma_start(out=out, in_=in_)
        inst.then_inc(self.h[sname], 16)
        self.cnt[sname] += 16
        return (sname, self.cnt[sname])

    def qall(self, qname):
        return [("%s%d" % (qname, i), self.cnt["%s%d" % (qname, i)]) for i in range(4)]

    def allq(self):
        out = []
        for q in ("dl", "dg", "dt"):
            out += self.qall(q)
        return out

    def coll(self, kind, aop, ins, outs, name):
        tok = self.tok

        @self.sec("gpsimd")
        def _(G):
            self.W(G, "gpsimd", self.allq())
            for s in ("ts", "vs", "ss"):
                self.W(G, "gpsimd", (s, self.cnt[s]))
            self._selfwait(G, "gpsimd", "cc")
            inst = G.collective_compute(kind, aop, replica_groups=GROUPS,
                                        ins=[ins.ap().opt()], outs=[outs.ap().opt()])
            inst.then_inc(self.h["cc"], 1)
            self.cnt["cc"] += 1
            tok[name] = ("cc", self.cnt["cc"])

    def declare(self):
        nc = self.nc
        A = {}
        def ein(name, shape, dt=F32):
            A[name] = nc.dram_tensor(name, list(shape), dt, kind="ExternalInput").ap()
        ein("xin", [OWN, C])
        ein("wqT", [C, 512]); ein("wkT", [C, 512]); ein("wvT", [C, 512])
        ein("wvN", [512, C])
        ein("pkT", [128, 64]); ein("pvT", [128, 64]); ein("uoT", [64, 128])
        ein("w1T", [C, 512]); ein("w2c", [128, 4])
        ein("woT0s", [512, C])
        ein("wupT", [FS, C]); ein("wdnT", [C, FS])
        ein("bupP", [128, 16]); ein("bdn", [1, C])
        ein("wq1T", [C, 512]); ein("wk1T", [C, 512]); ein("wv1T", [C, 512])
        ein("bq1", [128, 4]); ein("bk1", [128, 4]); ein("bv1", [1, 512])
        ein("bo1", [1, C]); ein("wo1Ts", [512, C]); ein("wt1", [C, C])
        ein("bff1", [1, C])
        ein("cos0", [128, T]); ein("sin0", [128, T])
        ein("cos1", [128, T]); ein("sin1", [128, T])
        ein("rotP0T", [128, 128], BF16); ein("rotP1T", [128, 128], BF16)
        ein("msk0", [128, 8 * SP], BF16); ein("msk1", [128, 4 * SP], BF16)
        ein("idt32", [128, 128])
        self.I = A
        self.yout = nc.dram_tensor("yout", [OWN, C], F32, kind="ExternalOutput").ap()
        Hh = {}
        def hin(name, shape, dt=BF16):
            Hh[name] = nc.dram_tensor(name, list(shape), dt, kind="Internal")
        hin("ag_in", [OWN, C]); hin("ag_out", [T, C])
        hin("rs_in", [T, C]); hin("rs_out", [OWN, C])
        hin("x1h", [OWN, C], F32)
        hin("q0h", [512, T])
        hin("kh", [128, NSP * 4 * 512])
        hin("vh", [128, 32 * 520])
        hin("wqupT", [FS, C]); hin("wqdnT", [C, FS]); hin("wt1q", [C, C])
        hin("xqh", [OWN, C])
        hin("gscr", [1, C], F32)
        hin("nscr", [1, 2 * SP], F32)
        hin("ctlh", [128, NSP * 2048])
        hin("wvrh", [256, C])
        self.Hh = Hh

    def dump(self, name, shape, dt=F32):
        if name in self.dbg:
            self.dbg_out[name] = self.nc.dram_tensor(
                "dbg_" + name, list(shape), dt, kind="ExternalOutput").ap()

    def build(self):
        from contextlib import ExitStack
        nc = self.nc
        self.declare()
        self.dump("ag_in", [OWN, C], BF16); self.dump("ag_out", [T, C], BF16)
        self.dump("q0h", [512, T], BF16); self.dump("kh", [128, NSP * 4 * 512], BF16)
        self.dump("vh", [128, 32 * 520], BF16)
        self.dump("kr", [128, 2 * T], BF16); self.dump("vr", [128, 32 * 260], BF16)
        self.dump("gate", [1, T]); self.dump("wqupT", [FS, C], BF16)
        self.dump("wqdnT", [C, FS], BF16); self.dump("wt1q", [C, C], BF16)
        self.dump("gscr", [1, C]); self.dump("rs_out", [OWN, C], BF16)
        self.dump("x1h", [OWN, C]); self.dump("xqh", [OWN, C], BF16)
        self.dump("rs_in", [T, C], BF16); self.dump("wvrh", [256, C], BF16)
        self.dump("gbc", [128, 2048])
        with ExitStack() as stk:
            self.stk = stk
            self.h, self.cnt, self.wm = {}, {}, {}
            self.qi = {"dl": 0, "dg": 0, "dt": 0}
            self._secn = 0
            self._dmasec = {}
            S = {}
            def sb(name, shape, dt=BF16):
                S[name] = stk.enter_context(nc.sbuf_tensor(name, shape, dt))
            sb("xnsp", [128, NCH * SP])
            sb("wA", [128, NCH * 512])
            sb("wB", [128, NCH * 512])
            sb("wC", [128, NCH * 512])
            sb("wD", [128, NCH * 256])
            sb("w1b", [128, 2 * 512])
            sb("kr", [128, 2 * T])
            sb("vr", [128, 32 * 260])
            sb("qsp", [128, 4 * SP])
            sb("qr", [128, 2 * SP])
            sb("tab", [128, 2 * SP], F32)
            sb("msk", [128, 8 * SP])
            sb("att", [128, 4 * 1024])
            sb("hsp", [128, NCH * SP])
            sb("ev", [128, 8 * 512])
            sb("ctl", [128, 4 * 512])
            sb("xt", [128, 2048], F32)
            sb("xt2", [128, 2048], F32)
            sb("cx", [128, 2048])
            sb("sm", [128, 256], F32)
            sb("smb", [128, 128])
            sb("nbc", [128, 2 * SP], F32)
            sb("bias", [128, 40], F32)
            sb("bcst", [128, 2048])
            sb("gbc", [128, 2048])
            sb("gcol", [128, 16], F32)
            sb("tiny", [128, 640])
            sb("id32", [128, 128], F32)
            sb("gate", [1, T])
            S["kl"] = S["wC"]
            S["vl"] = S["wA"]
            S["qr"] = S["wD"]
            S["svec"] = S["xt2"][0:1, 0:2048]
            S["kl1"] = S["kr"]
            S["vl1"] = S["vr"]
            self.S = S
            PS = {}
            def psm(name, shape):
                PS[name] = stk.enter_context(nc.psum_tensor(name, shape, F32))
            psm("qA", [128, 1024]); psm("qB", [128, 1024])
            psm("qC", [128, 1024]); psm("qD", [128, 1024])
            PS["pA"] = PS["qA"][:, :]
            PS["pB"] = PS["qB"][:, :]
            PS["p0"] = PS["qC"][:, 0:512]
            PS["p1"] = PS["qC"][:, 512:1024]
            PS["p2"] = PS["qD"][:, 0:512]
            PS["p3"] = PS["qD"][:, 512:1024]
            self.PS = PS
            for q in ("dl", "dg", "dt"):
                for i in range(4):
                    self.sem("%s%d" % (q, i))
            for s in ("ts", "vs", "ss", "cc", "fin"):
                self.sem(s)
            self.blk = stk.enter_context(nc.Block())
            self.emit()
        return nc

    def sec(self, name):
        self._secn += 1
        return getattr(self.blk, name)

    def emit(self):
        upto = self.upto
        tok = self.tok
        self.p0()
        self.p1(layer=0)
        self.coll("AllGather", Alu.bypass, self.Hh["ag_in"], self.Hh["ag_out"], "ag0")
        if upto == "p1":
            return self.finish()
        self.quant_mx("up")
        self.quant_mx("dn")
        self.quant_ternary()
        if upto == "p2":
            return self.finish()
        self.l0_wload()
        self.wvr_build()
        for sp in range(NSP):
            self.qkv_span(0, sp)
        if upto == "p3":
            return self.finish()
        for tau in range(NSP):
            self.attn0_local_span(tau)
        self.l0_wo_load()
        for sp in range(NSP):
            self.ffn0_span(sp)
            self.attn0_global_span(sp)
        self.coll("ReduceScatter", Alu.add, self.Hh["rs_in"], self.Hh["rs_out"], "rs0")
        if upto in ("p4", "p5"):
            return self.finish()
        self.p6()
        self.coll("AllGather", Alu.bypass, self.Hh["ag_in"], self.Hh["ag_out"], "ag1")
        if upto == "p6":
            return self.finish()
        self.l1_wload()
        self.l1_wo_load()
        if upto == "p7":
            for sp in range(NSP):
                self.qkv_span(1, sp)
            return self.finish()
        for sp in range(NSP):
            self.qkv_span(1, sp)
            self.attn1_span(sp)
        self.coll("ReduceScatter", Alu.add, self.Hh["rs_in"], self.Hh["rs_out"], "rs1")
        if upto == "p8":
            return self.finish()
        self.p9_p10()
        self.finish()

    def p0(self):
        S, I, tok = self.S, self.I, self.tok

        @self.sec("vector")
        def _(V):
            for name, val in [("sc0", SC0), ("mbl", -MB_L), ("scg", SCG),
                              ("mbg", -MB_G), ("sc1", SC1), ("mb1", -MB_1)]:
                self.op(V, "vector", "vs", "memset", S["bias"][:, BC[name]:BC[name] + 1], val)
            tok["consts"] = self.op(V, "vector", "vs", "memset",
                                    S["tiny"][:, 516:517], 1.0)

        @self.sec("sync")
        def _(SY):
            tok["msk"] = self.dma(SY, "dl", S["msk"][:, 0:8 * SP], I["msk0"])
            tok["id32"] = self.dma(SY, "dl", S["id32"][:, :], I["idt32"])
            tok["biasq"] = self.dma(SY, "dl", S["bias"][:, BC["bq"]:BC["bq"] + 4], I["bq1"])
            tok["biask"] = self.dma(SY, "dl", S["bias"][:, BC["bk"]:BC["bk"] + 4], I["bk1"])

        @self.sec("gpsimd")
        def _(G):
            t = S["tiny"]
            self.dma(G, "dg", t[:, 0:128], I["rotP0T"])
            self.dma(G, "dg", t[:, 128:256], I["rotP1T"])
            self.dma(G, "dg", t[:, 256:320], I["pkT"])
            self.dma(G, "dg", t[0:64, 320:448], I["uoT"])
            self.dma(G, "dg", t[:, 448:512], I["pvT"])
            self.dma(G, "dg", t[:, 520:524], I["w2c"])
            self.dma(G, "dg", S["nbc"][:, SP:2 * SP],
                     I["bv1"].partition_broadcast(128).squeeze(1))
            tok["bupP"] = self.dma(G, "dg", S["bias"][:, 20:36], I["bupP"])
            tok["tiny"] = self.qall("dg")

    def p1(self, layer):
        S, I, H, tok = self.S, self.I, self.Hh, self.tok
        src = I["xin"] if layer == 0 else H["x1h"].ap()
        for ti in range(8):
            self.ln_tile(ti, src[128 * ti:128 * ti + 128, :], layer)

            @self.sec("scalar")
            def _(Sc):
                self.W(Sc, "scalar", tok["lnxn"])
                tok["agw%d" % ti] = self.dma(
                    Sc, "dt", H["ag_in"].ap()[128 * ti:128 * ti + 128, :], S["cx"][:, :])

            if layer == 1:
                self.xq_tile(ti)

    def ln_tile(self, ti, src_ap, layer):
        S, tok = self.S, self.tok
        xt, xt2, cx, sm = S["xt"], S["xt2"], S["cx"], S["sm"]
        c = 192 + (ti % 2) * 16

        @self.sec("sync")
        def _(SY):
            self.W(SY, "sync", ("vs", self.cnt["vs"]))
            self.W(SY, "sync", ("ss", self.cnt["ss"]))
            tok["lnx"] = self.dma(SY, "dl", xt[:, :], src_ap)

        @self.sec("scalar")
        def _(Sc):
            self.W(Sc, "scalar", tok["lnx"])
            tok["lnsq"] = self.op(Sc, "scalar", "ss", "activation",
                                  xt2[:, :], xt[:, :], Act.Square)

        @self.sec("vector")
        def _(V):
            self.W(V, "vector", tok["lnx"])
            self.W(V, "vector", tok["lnsq"])
            self.op(V, "vector", "vs", "reduce_sum", sm[:, c:c + 1], xt[:, :], AxX)
            self.op(V, "vector", "vs", "reduce_sum", sm[:, c + 1:c + 2], xt2[:, :], AxX)
            self.op(V, "vector", "vs", "tensor_scalar", sm[:, c:c + 1], sm[:, c:c + 1],
                    1.0 / C, None, Alu.mult)
            self.op(V, "vector", "vs", "tensor_scalar", sm[:, c + 1:c + 2],
                    sm[:, c + 1:c + 2], 1.0 / C, None, Alu.mult)
            self.op(V, "vector", "vs", "tensor_scalar", sm[:, c + 2:c + 3],
                    sm[:, c:c + 1], sm[:, c:c + 1], None, Alu.mult)
            self.op(V, "vector", "vs", "tensor_sub", sm[:, c + 3:c + 4],
                    sm[:, c + 1:c + 2], sm[:, c + 2:c + 3])
            self.op(V, "vector", "vs", "tensor_scalar", sm[:, c + 4:c + 5],
                    sm[:, c + 3:c + 4], 1e-5, None, Alu.add)
            tok["lnv"] = self.op(V, "vector", "vs", "reciprocal",
                                 sm[:, c + 5:c + 6], sm[:, c + 4:c + 5])

        @self.sec("scalar")
        def _(Sc):
            self.W(Sc, "scalar", tok["lnv"])
            tok["lnrs"] = self.op(Sc, "scalar", "ss", "activation",
                                  sm[:, c + 6:c + 7], sm[:, c + 5:c + 6], Act.Sqrt)

        @self.sec("vector")
        def _(V):
            self.W(V, "vector", tok["lnrs"])
            self.W(V, "vector", tok.get("agw%d" % (ti - 1)))
            tok["lnxn"] = self.op(V, "vector", "vs", "tensor_scalar", cx[:, :], xt[:, :],
                                  sm[:, c:c + 1], sm[:, c + 6:c + 7],
                                  Alu.subtract, Alu.mult)
            if layer == 1:
                self.op(V, "vector", "vs", "reduce_max", sm[:, c + 7:c + 8], cx[:, :],
                        AxX, apply_absolute_value=True)
                self.op(V, "vector", "vs", "tensor_scalar", sm[:, 40 + ti:41 + ti],
                        sm[:, c + 7:c + 8], 1.0 / 127.0, None, Alu.mult)
                self.op(V, "vector", "vs", "reciprocal", sm[:, c + 8:c + 9],
                        sm[:, c + 4:c + 5])
                self.op(V, "vector", "vs", "tensor_mul", sm[:, c + 9:c + 10],
                        sm[:, c + 8:c + 9], sm[:, c + 3:c + 4])
                tok["rms2"] = self.op(V, "vector", "vs", "tensor_scalar",
                                      sm[:, c + 9:c + 10], sm[:, c + 9:c + 10],
                                      1e-8, None, Alu.add)

        if layer == 1:
            @self.sec("scalar")
            def _(Sc):
                self.W(Sc, "scalar", tok["rms2"])
                tok["rms"] = self.op(Sc, "scalar", "ss", "activation",
                                     sm[:, c + 10:c + 11], sm[:, c + 9:c + 10], Act.Sqrt)

            @self.sec("vector")
            def _(V):
                self.W(V, "vector", tok["rms"])
                self.op(V, "vector", "vs", "tensor_scalar", sm[:, c + 11:c + 12],
                        sm[:, 40 + ti:41 + ti], 1e-8, None, Alu.add)
                self.op(V, "vector", "vs", "tensor_mul", sm[:, c + 12:c + 13],
                        sm[:, c + 11:c + 12], sm[:, c + 10:c + 11])
                tok["cb"] = self.op(V, "vector", "vs", "reciprocal",
                                    sm[:, 48 + ti:49 + ti], sm[:, c + 12:c + 13])

    def xq_tile(self, ti):
        S, H, tok = self.S, self.Hh, self.tok
        xt, xt2, cx, sm = S["xt"], S["xt2"], S["cx"], S["sm"]

        @self.sec("vector")
        def _(V):
            self.W(V, "vector", tok["cb"])
            self.W(V, "vector", tok.get("xqw"))
            self.op(V, "vector", "vs", "tensor_scalar", xt2[:, :], cx[:, :],
                    sm[:, 48 + ti:49 + ti], None, Alu.mult)
            self.op(V, "vector", "vs", "tensor_scalar", xt[:, :], xt2[:, :],
                    12582912.0, None, Alu.add)
            self.op(V, "vector", "vs", "tensor_scalar", xt2[:, :], xt[:, :],
                    12582912.0, None, Alu.subtract)
            self.op(V, "vector", "vs", "tensor_scalar", xt[:, :], xt2[:, :],
                    127.0, -127.0, Alu.min, Alu.max)
            tok["xq"] = self.op(V, "vector", "vs", "tensor_copy",
                                S["ev"][:, 0:2048], xt[:, :])

        @self.sec("scalar")
        def _(Sc):
            self.W(Sc, "scalar", tok["xq"])
            tok["xqw"] = self.dma(Sc, "dt", H["xqh"].ap()[128 * ti:128 * ti + 128, :],
                                  S["ev"][:, 0:2048])

    def quant_mx(self, which):
        S, I, H, tok = self.S, self.I, self.Hh, self.tok
        src = I["wupT" if which == "up" else "wdnT"]
        dsth = H["wqupT" if which == "up" else "wqdnT"]
        xt, xt2, hs, sm, smb = S["xt"], S["xt2"], S["hsp"], S["sm"], S["smb"]
        for ti in range(16):
            @self.sec("sync")
            def _(SY):
                self.W(SY, "sync", ("vs", self.cnt["vs"]))
                tok["qzl"] = self.dma(SY, "dl", xt[:, :],
                                      src[128 * ti:128 * ti + 128, :])

            @self.sec("vector")
            def _(V):
                self.W(V, "vector", tok["qzl"])
                if ti == 0:
                    self.W(V, "vector", self.qall("dt"))
                self.W(V, "vector", tok.get("qzw%s%d" % (which, ti - 1)))
                w3 = xt[:, :].rearrange("p (nb k) -> p nb k", k=32)
                self.op(V, "vector", "vs", "tensor_reduce", sm[:, 0:64], w3, AxX,
                        Alu.max, apply_absolute_value=True)
                a_i = sm[:, 0:64].bitcast(I32)
                s_i = sm[:, 64:128].bitcast(I32)
                r_i = sm[:, 128:192].bitcast(I32)
                p_i = sm[:, 192:256].bitcast(I32)
                self.op(V, "vector", "vs", "tensor_scalar", p_i, a_i,
                        0x7F800000, None, Alu.bitwise_and)
                self.op(V, "vector", "vs", "tensor_scalar", s_i, p_i,
                        0x01000000, None, Alu.subtract)
                self.op(V, "vector", "vs", "tensor_scalar", r_i, s_i,
                        0x7F000000, -1, Alu.subtract, Alu.mult)
                self.op(V, "vector", "vs", "tensor_copy", smb[:, 0:64], sm[:, 128:192])
                self.op(V, "vector", "vs", "tensor_copy", smb[:, 64:128], sm[:, 64:128])
                acc = hs[:, 0:2048]; tmp = hs[:, 2048:4096]
                sgn = hs[:, 4096:6144]; qs = hs[:, 6144:8192]
                rb = smb[:, 0:64].unsqueeze(2).broadcast_to([128, 64, 32])
                sbb = smb[:, 64:128].unsqueeze(2).broadcast_to([128, 64, 32])
                self.op(V, "vector", "vs", "tensor_tensor",
                        xt2[:, :].rearrange("p (nb k) -> p nb k", k=32), w3, rb,
                        Alu.mult)
                self.op(V, "vector", "vs", "tensor_scalar", sgn, xt2[:, :],
                        0.0, 2.0, Alu.is_ge, Alu.mult)
                self.op(V, "vector", "vs", "tensor_scalar", xt2[:, :].bitcast(I32),
                        xt2[:, :].bitcast(I32), 0x7FFFFFFF, None, Alu.bitwise_and)
                self.op(V, "vector", "vs", "tensor_scalar", acc, xt2[:, :],
                        0.25, 0.5, Alu.is_ge, Alu.mult)
                for th, dl_ in [(0.75, 0.5), (1.25, 0.5), (1.75, 0.5),
                                (2.5, 1.0), (3.5, 1.0), (5.0, 2.0)]:
                    self.op(V, "vector", "vs", "tensor_scalar", tmp, xt2[:, :],
                            th, dl_, Alu.is_ge, Alu.mult)
                    self.op(V, "vector", "vs", "tensor_add", acc, acc, tmp)
                self.op(V, "vector", "vs", "tensor_tensor",
                        qs.rearrange("p (nb k) -> p nb k", k=32),
                        acc.rearrange("p (nb k) -> p nb k", k=32), sbb, Alu.mult)
                self.op(V, "vector", "vs", "tensor_scalar", sgn, sgn,
                        1.0, None, Alu.subtract)
                tok["qz%s%d" % (which, ti)] = self.op(
                    V, "vector", "vs", "tensor_tensor", acc, qs, sgn, Alu.mult)

            @self.sec("scalar")
            def _(Sc):
                self.W(Sc, "scalar", tok["qz%s%d" % (which, ti)])
                tok["qzw%s%d" % (which, ti)] = self.dma(
                    Sc, "dt", dsth.ap()[128 * ti:128 * ti + 128, :], hs[:, 0:2048])

    def quant_ternary(self):
        S, I, H, tok = self.S, self.I, self.Hh, self.tok
        xt, xt2, sm = S["xt"], S["xt2"], S["sm"]
        for ti in range(16):
            @self.sec("sync")
            def _(SY):
                self.W(SY, "sync", ("vs", self.cnt["vs"]))
                tok["tql"] = self.dma(SY, "dl", xt[:, :],
                                      I["wt1"][128 * ti:128 * ti + 128, :])

            @self.sec("vector")
            def _(V):
                self.W(V, "vector", tok["tql"])
                if ti == 0:
                    self.W(V, "vector", self.qall("dt"))
                self.W(V, "vector", tok.get("tqw%d" % (ti - 1)))
                self.op(V, "vector", "vs", "tensor_reduce", sm[:, 0:1], xt[:, :],
                        AxX, Alu.add, apply_absolute_value=True)
                self.op(V, "vector", "vs", "tensor_scalar", S["gcol"][:, ti:ti + 1],
                        sm[:, 0:1], 1.0 / 2048.0, None, Alu.mult)
                self.op(V, "vector", "vs", "tensor_scalar", sm[:, 2:3],
                        S["gcol"][:, ti:ti + 1], 1e-8, None, Alu.add)
                self.op(V, "vector", "vs", "reciprocal", sm[:, 3:4], sm[:, 2:3])
                self.op(V, "vector", "vs", "tensor_scalar", xt2[:, :], xt[:, :],
                        sm[:, 3:4], None, Alu.mult)
                self.op(V, "vector", "vs", "tensor_scalar", xt[:, :], xt2[:, :],
                        12582912.0, None, Alu.add)
                self.op(V, "vector", "vs", "tensor_scalar", xt2[:, :], xt[:, :],
                        12582912.0, None, Alu.subtract)
                self.op(V, "vector", "vs", "tensor_scalar", xt[:, :], xt2[:, :],
                        1.0, -1.0, Alu.min, Alu.max)
                tok["tq%d" % ti] = self.op(V, "vector", "vs", "tensor_copy",
                                           S["ev"][:, 0:2048], xt[:, :])

            @self.sec("scalar")
            def _(Sc):
                self.W(Sc, "scalar", tok["tq%d" % ti])
                tok["tqw%d" % ti] = self.dma(
                    Sc, "dt", H["wt1q"].ap()[128 * ti:128 * ti + 128, :],
                    S["ev"][:, 0:2048])

        @self.sec("tensor")
        def _(Te):
            self.W(Te, "tensor", ("vs", self.cnt["vs"]))
            self.W(Te, "tensor", self.tok["id32"])
            tok["gtr"] = self.tr(Te, self.PS["p0"][0:16, 0:128], S["gcol"][:, :],
                                 S["id32"][:, :])

        @self.sec("vector")
        def _(V):
            self.W(V, "vector", tok["gtr"])
            tok["gev"] = self.op(V, "vector", "vs", "tensor_copy",
                                 sm[0:16, 128:256], self.PS["p0"][0:16, 0:128])

        @self.sec("scalar")
        def _(Sc):
            self.W(Sc, "scalar", tok["gev"])
            tok["gw"] = self.dma(Sc, "dt", H["gscr"].ap(), sm[0:16, 128:256])

        @self.sec("gpsimd")
        def _(G):
            self.W(G, "gpsimd", tok["gw"])
            tok["gbc"] = self.dma(G, "dg", S["gbc"][:, :],
                                  H["gscr"].ap().partition_broadcast(128).squeeze(1))

    def l0_wload(self):
        S, I, tok = self.S, self.I, self.tok

        @self.sec("gpsimd")
        def _(G):
            self.W(G, "gpsimd", ("vs", self.cnt["vs"]))
            tk = None
            for cc in range(NCH):
                self.dma(G, "dg", S["wA"][:, 512 * cc:512 * cc + 512],
                         I["wqT"][128 * cc:128 * cc + 128, :])
                self.dma(G, "dg", S["wB"][:, 512 * cc:512 * cc + 512],
                         I["wkT"][128 * cc:128 * cc + 128, :])
                self.dma(G, "dg", S["wC"][:, 512 * cc:512 * cc + 512],
                         I["wvT"][128 * cc:128 * cc + 128, :])
            tok["lw"] = self.qall("dg")

    def l1_wload(self):
        S, I, tok = self.S, self.I, self.tok

        @self.sec("gpsimd")
        def _(G):
            self.W(G, "gpsimd", ("ts", self.cnt["ts"]))
            for cc in range(NCH):
                self.dma(G, "dg", S["wA"][:, 512 * cc:512 * cc + 512],
                         I["wq1T"][128 * cc:128 * cc + 128, :])
                self.dma(G, "dg", S["wB"][:, 512 * cc:512 * cc + 512],
                         I["wk1T"][128 * cc:128 * cc + 128, :])
                self.dma(G, "dg", S["wC"][:, 512 * cc:512 * cc + 512],
                         I["wv1T"][128 * cc:128 * cc + 128, :])
            tok["lw"] = self.qall("dg")

        @self.sec("sync")
        def _(SY):
            self.W(SY, "sync", ("vs", self.cnt["vs"]))
            tok["msk"] = self.dma(SY, "dl", S["msk"][:, 0:4 * SP], I["msk1"])

    def l0_wo_load(self):
        S, I, tok = self.S, self.I, self.tok

        @self.sec("gpsimd")
        def _(G):
            self.W(G, "gpsimd", ("ts", self.cnt["ts"]))
            self.W(G, "gpsimd", ("vs", self.cnt["vs"]))
            for cc in range(4):
                self.dma(G, "dg", S["wC"][:, 2048 * cc:2048 * cc + 2048],
                         I["woT0s"][128 * cc:128 * cc + 128, :])
            tok["wo"] = self.qall("dg")

    def l1_wo_load(self):
        S, I, tok = self.S, self.I, self.tok

        @self.sec("gpsimd")
        def _(G):
            self.W(G, "gpsimd", ("ts", self.cnt["ts"]))
            self.W(G, "gpsimd", ("ss", self.cnt["ss"]))
            for cc in range(4):
                self.dma(G, "dg", S["hsp"][:, 2048 * cc:2048 * cc + 2048],
                         I["wo1Ts"][128 * cc:128 * cc + 128, :])
            tok["wo"] = self.qall("dg")

    def wvr_build(self):
        S, I, H, PS, tok = self.S, self.I, self.Hh, self.PS, self.tok
        for h in range(4):
            buf = (h % 2) * 2048

            @self.sec("gpsimd")
            def _(G):
                self.W(G, "gpsimd", tok.get("wvrw%d" % (h - 2)))
                tok["wvrl%d" % h] = self.dma(G, "dg", S["att"][:, buf:buf + 2048],
                                             I["wvN"][128 * h:128 * h + 128, :])

            @self.sec("tensor")
            def _(Te):
                self.W(Te, "tensor", tok["wvrl%d" % h])
                self.W(Te, "tensor", tok["tiny"])
                for csp in range(4):
                    self.W(Te, "tensor", tok.get("wvrev%d_%d" % (h - 1, csp)))
                    tok["wvrp%d_%d" % (h, csp)] = self.mm(
                        Te, PS["p%d" % csp][0:64, :], S["tiny"][:, 448:512],
                        S["att"][:, buf + 512 * csp:buf + 512 * csp + 512],
                        start=True, stop=True, inc=True)

            @self.sec("vector")
            def _(V):
                if h == 0:
                    self.W(V, "vector", self.qall("dt"))
                self.W(V, "vector", tok.get("wvrw%d" % (h - 1)))
                for csp in range(4):
                    self.W(V, "vector", tok["wvrp%d_%d" % (h, csp)])
                    tok["wvrev%d_%d" % (h, csp)] = self.op(
                        V, "vector", "vs", "tensor_copy",
                        S["ev"][0:64, 512 * csp:512 * csp + 512],
                        PS["p%d" % csp][0:64, :])

            @self.sec("scalar")
            def _(Sc):
                self.W(Sc, "scalar", tok["wvrev%d_3" % h])
                tok["wvrw%d" % h] = self.dma(
                    Sc, "dt", H["wvrh"].ap()[64 * h:64 * h + 64, :],
                    S["ev"][0:64, 0:2048])

        @self.sec("sync")
        def _(SY):
            self.W(SY, "sync", [tok["wvrw%d" % h] for h in range(4)])
            tk = None
            for cc in range(NCH):
                tk = self.dma(SY, "dl", S["wD"][:, 256 * cc:256 * cc + 256],
                              H["wvrh"].ap()[0:256, 128 * cc:128 * cc + 128],
                              transpose=True)
            tok["wDl"] = self.qall("dl")

    def qkv_span(self, layer, sp):
        S, I, H, PS, tok = self.S, self.I, self.Hh, self.PS, self.tok
        xb = 0
        agt = tok["ag%d" % layer]

        @self.sec("sync")
        def _(SY):
            self.W(SY, "sync", agt)
            self.W(SY, "sync", tok.get("qkvdone%d" % (sp - 1)))
            lt = []
            for cc in range(NCH):
                lt.append(self.dma(SY, "dl",
                                   S["xnsp"][:, xb + SP * cc:xb + SP * cc + SP],
                                   H["ag_out"].ap()[SP * sp:SP * sp + SP,
                                                    128 * cc:128 * cc + 128],
                                   transpose=True))
            self.W(SY, "sync", tok.get("ropedone%d" % (sp - 1)))
            lt.append(self.dma(SY, "dl", S["tab"][:, 0:SP],
                               I["cos%d" % layer][:, SP * sp:SP * sp + SP]))
            lt.append(self.dma(SY, "dl", S["tab"][:, SP:2 * SP],
                               I["sin%d" % layer][:, SP * sp:SP * sp + SP]))
            tok["xnl"] = lt

        for oc in range(8):
            isq = oc < 4
            wsb = S["wA"] if isq else S["wB"]
            hcol = oc % 4

            @self.sec("tensor")
            def _(Te):
                self.W(Te, "tensor", tok["xnl"])
                self.W(Te, "tensor", tok["lw"])
                if layer == 0:
                    ps = PS["p%d" % (oc % 2)]
                    self.W(Te, "tensor", tok.get("qkev%d" % ((oc - 2) % 8)))
                else:
                    ps = PS["p0"]
                    if oc == 0:
                        self.W(Te, "tensor", ("vs", self.cnt["vs"]))
                    self.W(Te, "tensor", tok.get("qkev%d" % ((oc - 1) % 8)))
                for cc in range(NCH):
                    self.mm(Te, ps[:, :],
                            wsb[:, 512 * cc + 128 * hcol:512 * cc + 128 * hcol + 128],
                            S["xnsp"][:, xb + SP * cc:xb + SP * cc + SP],
                            start=(cc == 0), stop=(cc == NCH - 1),
                            inc=(cc == NCH - 1))
                tok["qkmm%d" % oc] = ("ts", self.cnt["ts"])

            if layer == 1:
                @self.sec("vector")
                def _(V):
                    self.W(V, "vector", tok["qkmm%d" % oc])
                    ps = PS["p0"]
                    slot = S["ev"][:, 512 * oc:512 * oc + 512]
                    bcol = (BC["bq"] if isq else BC["bk"]) + hcol
                    self.W(V, "vector", tok["biasq"])
                    self.W(V, "vector", tok["biask"])
                    tok["qkev%d" % oc] = self.op(
                        V, "vector", "vs", "tensor_scalar", slot, ps[:, :],
                        S["bias"][:, bcol:bcol + 1], None, Alu.add)
            else:
                @self.sec("scalar")
                def _(Sc):
                    self.W(Sc, "scalar", tok["qkmm%d" % oc])
                    ps = PS["p%d" % (oc % 2)]
                    slot = S["ev"][:, 512 * oc:512 * oc + 512]
                    self.W(Sc, "scalar", tok.get("rotmm%d" % ((oc - 2) % 8)))
                    tok["qkev%d" % oc] = self.op(Sc, "scalar", "ss", "copy",
                                                 slot, ps[:, :])

            @self.sec("tensor")
            def _(Te):
                self.W(Te, "tensor", tok["qkev%d" % oc])
                self.W(Te, "tensor", tok["tiny"])
                if layer == 0:
                    prot = PS["p%d" % (2 + oc % 2)]
                    if oc < 2:
                        self.W(Te, "tensor", ("vs", self.cnt["vs"]))
                    self.W(Te, "tensor", tok.get("rope%d" % ((oc - 2) % 8)))
                else:
                    prot = PS["p1"]
                    self.W(Te, "tensor", tok.get("rope%d" % ((oc - 1) % 8)))
                rp = S["tiny"][:, 0:128] if layer == 0 else S["tiny"][:, 128:256]
                tok["rotmm%d" % oc] = self.mm(
                    Te, prot[:, :], rp, S["ev"][:, 512 * oc:512 * oc + 512],
                    start=True, stop=True, inc=True)

            @self.sec("vector")
            def _(V):
                self.W(V, "vector", tok["rotmm%d" % oc])
                self.W(V, "vector", tok["xnl"])
                prot = PS["p%d" % (2 + oc % 2)] if layer == 0 else PS["p1"]
                slot = S["ev"][:, 512 * oc:512 * oc + 512]
                t1 = S["xt"][:, 0:512]
                t2 = S["xt"][:, 512:1024]
                self.W(V, "vector", tok.get("qkw%d" % oc))
                self.op(V, "vector", "vs", "tensor_tensor", t1, slot,
                        S["tab"][:, 0:SP], Alu.mult)
                self.op(V, "vector", "vs", "tensor_tensor", t2, prot[:, :],
                        S["tab"][:, SP:2 * SP], Alu.mult)
                dst = (S["att"] if layer == 0 else S["wD"])[:, 512 * oc:512 * oc + 512]
                tok["rope%d" % oc] = self.op(V, "vector", "vs", "tensor_add",
                                             dst, t1, t2)

            @self.sec("scalar")
            def _(Sc):
                self.W(Sc, "scalar", tok["rope%d" % oc])
                if layer == 0 and sp == 0:
                    self.W(Sc, "scalar", tok.get("wDl"))
                ssrc = (S["att"] if layer == 0 else S["wD"])[:, 512 * oc:512 * oc + 512]
                if isq:
                    tok["qkw%d" % oc] = self.dma(
                        Sc, "dt", H["q0h"].ap()[128 * oc:128 * oc + 128,
                                                SP * sp:SP * sp + SP], ssrc)
                else:
                    kt = oc - 4
                    tok["qkw%d" % oc] = self.dma(
                        Sc, "dt", H["kh"].ap()[:, 2048 * sp + 512 * kt:
                                               2048 * sp + 512 * kt + 512], ssrc)

        if layer == 0:
            for pr in range(2):
                @self.sec("tensor")
                def _(Te):
                    self.W(Te, "tensor", tok["rope%d" % (4 + 2 * pr)])
                    self.W(Te, "tensor", tok["rope%d" % (5 + 2 * pr)])
                    ps = PS["pA"]
                    self.W(Te, "tensor", tok.get("krev%d" % ((pr - 1) % 2)))
                    if pr == 0:
                        self.W(Te, "tensor", tok.get("infog0"))
                        self.W(Te, "tensor", tok.get("infog1"))
                    self.mm(Te, ps[0:64, 0:512], S["tiny"][:, 256:320],
                            S["att"][:, 512 * (4 + 2 * pr):512 * (4 + 2 * pr) + 512],
                            start=True, stop=True, tp=(0, 0))
                    tok["krmm%d" % pr] = self.mm(
                        Te, ps[64:128, 0:512], S["tiny"][:, 256:320],
                        S["att"][:, 512 * (5 + 2 * pr):512 * (5 + 2 * pr) + 512],
                        start=True, stop=True, tp=(0, 64), inc=True)

                @self.sec("vector")
                def _(V):
                    self.W(V, "vector", tok["krmm%d" % pr])
                    tok["krev%d" % pr] = self.op(
                        V, "vector", "vs", "tensor_copy",
                        S["kr"][:, T * pr + SP * sp:T * pr + SP * sp + SP],
                        PS["pA"][:, 0:512])

            self.info_span(sp)

        for st in range(4):
            sig = 4 * sp + st

            @self.sec("tensor")
            def _(Te):
                self.W(Te, "tensor", tok["xnl"])
                self.W(Te, "tensor", tok["lw"])
                ps = PS["p%d" % (st % 2)]
                self.W(Te, "tensor", tok.get("vev%d" % ((st - 2) % 4)))
                if layer == 1 and st < 2:
                    self.W(Te, "tensor", ("vs", self.cnt["vs"]))
                for cc in range(NCH):
                    self.mm(Te, ps[:, :],
                            S["xnsp"][:, xb + SP * cc + 128 * st:
                                      xb + SP * cc + 128 * st + 128],
                            S["wC"][:, 512 * cc:512 * cc + 512],
                            start=(cc == 0), stop=(cc == NCH - 1),
                            inc=(cc == NCH - 1))
                tok["vmm%d" % st] = ("ts", self.cnt["ts"])
                if layer == 0:
                    pvr = PS["p%d" % (2 + st % 2)]
                    self.W(Te, "tensor", tok.get("vrev%d" % ((st - 2) % 4)))
                    if st < 2:
                        self.W(Te, "tensor", tok.get("infos"))
                    self.W(Te, "tensor", tok.get("wDl"))
                    for cc in range(NCH):
                        self.mm(Te, pvr[:, 0:256],
                                S["xnsp"][:, xb + SP * cc + 128 * st:
                                          xb + SP * cc + 128 * st + 128],
                                S["wD"][:, 256 * cc:256 * cc + 256],
                                start=(cc == 0), stop=(cc == NCH - 1),
                                inc=(cc == NCH - 1))
                    tok["vrmm%d" % st] = ("ts", self.cnt["ts"])

            @self.sec("vector")
            def _(V):
                self.W(V, "vector", tok["vmm%d" % st])
                ps = PS["p%d" % (st % 2)]
                varena = S["cx"] if layer == 0 else S["wD"]
                vbuf = varena[:, 1040 * (st % 2):1040 * (st % 2) + 520]
                if layer == 1 and st < 2:
                    self.W(V, "vector", self.qall("dt"))
                self.W(V, "vector", tok.get("vw%d" % ((st - 2) % 4)))
                dst = vbuf.rearrange("p (h e) -> p h e", e=65)[:, :, 0:64]
                srcv = ps[:, :].rearrange("p (h e) -> p h e", e=64)
                if layer == 0:
                    self.op(V, "vector", "vs", "tensor_copy", dst, srcv)
                else:
                    bv = S["nbc"][:, SP:2 * SP].rearrange("p (h e) -> p h e", e=64)
                    self.op(V, "vector", "vs", "tensor_tensor", dst, srcv, bv, Alu.add)
                tok["vev%d" % st] = self.op(
                    V, "vector", "vs", "memset",
                    vbuf.rearrange("p (h e) -> p h e", e=65)[:, :, 64:65], 1.0)
                if layer == 0:
                    self.W(V, "vector", tok["vrmm%d" % st])
                    pvr = PS["p%d" % (2 + st % 2)]
                    dvr = S["vr"][:, 260 * sig:260 * sig + 260].rearrange(
                        "p (h e) -> p h e", e=65)[:, :, 0:64]
                    svr = pvr[:, 0:256].rearrange("p (h e) -> p h e", e=64)
                    self.op(V, "vector", "vs", "tensor_copy", dvr, svr)
                    tok["vrev%d" % st] = self.op(
                        V, "vector", "vs", "memset",
                        S["vr"][:, 260 * sig:260 * sig + 260].rearrange(
                            "p (h e) -> p h e", e=65)[:, :, 64:65], 1.0)

            @self.sec("scalar")
            def _(Sc):
                self.W(Sc, "scalar", tok["vev%d" % st])
                varena = S["cx"] if layer == 0 else S["wD"]
                vbuf = varena[:, 1040 * (st % 2):1040 * (st % 2) + 520]
                tok["vw%d" % st] = self.dma(
                    Sc, "dt", H["vh"].ap()[:, 520 * sig:520 * sig + 520], vbuf)

        tok["qkvdone%d" % sp] = [("ts", self.cnt["ts"]), ("vs", self.cnt["vs"]),
                                 ("ss", self.cnt["ss"])]
        tok["ropedone%d" % sp] = ("vs", self.cnt["vs"])

    def info_span(self, sp):
        S, I, H, PS, tok = self.S, self.I, self.Hh, self.PS, self.tok
        xb = 0

        for cc in range(NCH):
            @self.sec("gpsimd")
            def _(G):
                self.W(G, "gpsimd", tok.get("infomm_c%d" % ((cc - 2) % NCH)))
                tok["w1l%d" % cc] = self.dma(
                    G, "dg", S["w1b"][:, 512 * (cc % 2):512 * (cc % 2) + 512],
                    I["w1T"][128 * cc:128 * cc + 128, :])

            @self.sec("tensor")
            def _(Te):
                self.W(Te, "tensor", tok["w1l%d" % cc])
                self.W(Te, "tensor", tok["xnl"])
                if cc == 0:
                    self.W(Te, "tensor", tok.get("infog3"))
                    self.W(Te, "tensor", tok.get("krev0"))
                    self.W(Te, "tensor", tok.get("krev1"))
                wchunk = S["w1b"][:, 512 * (cc % 2):512 * (cc % 2) + 512]
                for fc in range(4):
                    ps = PS["pA" if fc < 2 else "pB"]
                    self.mm(Te, ps[:, 512 * (fc % 2):512 * (fc % 2) + 512],
                            wchunk[:, 128 * fc:128 * fc + 128],
                            S["xnsp"][:, xb + SP * cc:xb + SP * cc + SP],
                            start=(cc == 0), stop=(cc == NCH - 1),
                            inc=(fc == 3))
                tok["infomm_c%d" % cc] = ("ts", self.cnt["ts"])
        tok["infomm"] = ("ts", self.cnt["ts"])

        for fc in range(4):
            @self.sec("scalar")
            def _(Sc):
                self.W(Sc, "scalar", tok["infomm"])
                ps = PS["pA" if fc < 2 else "pB"]
                tok["infog%d" % fc] = self.op(
                    Sc, "scalar", "ss", "activation",
                    S["hsp"][:, 512 * fc:512 * fc + 512],
                    ps[:, 512 * (fc % 2):512 * (fc % 2) + 512], Act.Gelu)

        @self.sec("tensor")
        def _(Te):
            for fc in range(4):
                self.W(Te, "tensor", tok["infog%d" % fc])
            self.W(Te, "tensor", tok["consts"])
            self.W(Te, "tensor", tok["tiny"])
            ps = PS["p2"]
            self.W(Te, "tensor", tok.get("infoev"))
            self.W(Te, "tensor", tok.get("rope6"))
            for fc in range(4):
                self.mm(Te, ps[32 * fc:32 * fc + 1, :],
                        S["tiny"][:, 520 + fc:521 + fc],
                        S["hsp"][:, 512 * fc:512 * fc + 512],
                        start=True, stop=True, tp=(0, 32 * fc),
                        inc=(fc == 3))
            tok["infomm2"] = ("ts", self.cnt["ts"])

        @self.sec("vector")
        def _(V):
            self.W(V, "vector", tok["infomm2"])
            ps = PS["p2"]
            for fc in range(4):
                self.op(V, "vector", "vs", "tensor_copy",
                        S["svec"][:, 512 * fc:512 * fc + 512], ps[32 * fc:32 * fc + 1, :])
            self.op(V, "vector", "vs", "tensor_add", S["svec"][:, 0:512],
                    S["svec"][:, 0:512], S["svec"][:, 512:1024])
            self.op(V, "vector", "vs", "tensor_add", S["svec"][:, 1024:1536],
                    S["svec"][:, 1024:1536], S["svec"][:, 1536:2048])
            tok["infos"] = self.op(V, "vector", "vs", "tensor_add",
                                   S["svec"][:, 0:512], S["svec"][:, 0:512],
                                   S["svec"][:, 1024:1536])

        @self.sec("scalar")
        def _(Sc):
            self.W(Sc, "scalar", tok["infos"])
            tok["infosg"] = self.op(Sc, "scalar", "ss", "activation",
                                    S["svec"][:, 512:1024], S["svec"][:, 0:512],
                                    Act.Sigmoid)

        @self.sec("vector")
        def _(V):
            self.W(V, "vector", tok["infosg"])
            tok["infoev"] = self.op(V, "vector", "vs", "tensor_scalar",
                                    S["gate"][0:1, SP * sp:SP * sp + SP],
                                    S["svec"][:, 512:1024], 0.75, 1.0,
                                    Alu.is_gt, Alu.mult)

    def ffn0_span(self, sp):
        S, I, H, PS, tok = self.S, self.I, self.Hh, self.PS, self.tok
        xb = 0
        agt = self.tok["ag0"]

        @self.sec("sync")
        def _(SY):
            self.W(SY, "sync", agt)
            self.W(SY, "sync", tok.get("ffdone%d" % (sp - 1)))
            if sp == 0:
                self.W(SY, "sync", tok.get("qkvdone%d" % (NSP - 1)))
            lt = []
            for cc in range(NCH):
                lt.append(self.dma(SY, "dl",
                                   S["xnsp"][:, xb + SP * cc:xb + SP * cc + SP],
                                   H["ag_out"].ap()[SP * sp:SP * sp + SP,
                                                    128 * cc:128 * cc + 128],
                                   transpose=True))
            tok["fxnl"] = lt

        for fs4 in range(4):
            wup = S["wA"] if fs4 % 2 == 0 else S["wB"]

            @self.sec("sync")
            def _(SY):
                self.W(SY, "sync", tok["qzwup15"])
                self.W(SY, "sync", tok.get("upmm%d" % (fs4 - 2)))
                lt = []
                for cc in range(NCH):
                    lt.append(self.dma(
                        SY, "dl", wup[:, 512 * cc:512 * cc + 512],
                        H["wqupT"].ap()[512 * fs4:512 * fs4 + 512,
                                        128 * cc:128 * cc + 128],
                        transpose=True))
                tok["upwl"] = lt

            for fi in range(4):
                fcg = 4 * fs4 + fi

                @self.sec("tensor")
                def _(Te):
                    self.W(Te, "tensor", tok["upwl"])
                    self.W(Te, "tensor", tok["fxnl"])
                    if sp == 0 and fs4 == 0:
                        self.W(Te, "tensor", ("vs", self.cnt["vs"]))
                    ps = PS["p%d" % (fi % 2)]
                    self.W(Te, "tensor", tok.get("gel%d" % ((fcg - 2) % 16)))
                    for cc in range(NCH):
                        self.mm(Te, ps[:, :],
                                wup[:, 512 * cc + 128 * fi:512 * cc + 128 * fi + 128],
                                S["xnsp"][:, xb + SP * cc:xb + SP * cc + SP],
                                start=(cc == 0), stop=(cc == NCH - 1),
                                inc=(cc == NCH - 1))
                    tok["upps%d" % fcg] = ("ts", self.cnt["ts"])

                @self.sec("scalar")
                def _(Sc):
                    self.W(Sc, "scalar", tok["upps%d" % fcg])
                    self.W(Sc, "scalar", tok["bupP"])
                    ps = PS["p%d" % (fi % 2)]
                    tok["gel%d" % fcg] = self.op(
                        Sc, "scalar", "ss", "activation",
                        S["hsp"][:, 512 * fcg:512 * fcg + 512], ps[:, :],
                        Act.Gelu_apprx_tanh, S["bias"][:, 20 + fcg:21 + fcg])
            tok["upmm%d" % fs4] = ("ts", self.cnt["ts"])

        for ns in range(4):
            wdn = S["wB"] if ns % 2 == 0 else S["wA"]

            @self.sec("sync")
            def _(SY):
                self.W(SY, "sync", tok["qzwdn15"])
                self.W(SY, "sync",
                       tok["upmm3"] if ns < 2 else tok.get("dnmm%d" % (ns - 2)))
                lt = []
                for fc in range(NCH):
                    lt.append(self.dma(
                        SY, "dl", wdn[:, 512 * fc:512 * fc + 512],
                        H["wqdnT"].ap()[512 * ns:512 * ns + 512,
                                        128 * fc:128 * fc + 128],
                        transpose=True))
                tok["dnwl"] = lt

            for tt in range(4):
                @self.sec("tensor")
                def _(Te):
                    self.W(Te, "tensor", tok["dnwl"])
                    self.W(Te, "tensor", tok["gel15"])
                    ps = PS["p%d" % (tt % 2)]
                    pv = (ns, tt - 2) if tt >= 2 else ((ns - 1) % 4, tt + 2)
                    self.W(Te, "tensor", tok.get("yev%d_%d" % pv))
                    for fc in range(NCH):
                        self.mm(Te, ps[:, :],
                                S["hsp"][:, 512 * fc + 128 * tt:512 * fc + 128 * tt + 128],
                                wdn[:, 512 * fc:512 * fc + 512],
                                start=(fc == 0), stop=(fc == NCH - 1),
                                inc=(fc == NCH - 1))
                    tok["dnps"] = ("ts", self.cnt["ts"])

                @self.sec("vector")
                def _(V):
                    self.W(V, "vector", tok["dnps"])
                    ps = PS["p%d" % (tt % 2)]
                    slot = S["ev"][:, 2048 + 512 * tt:2048 + 512 * tt + 512]
                    pv = (ns, tt - 2) if tt >= 2 else ((ns - 1) % 4, tt + 2)
                    self.W(V, "vector", tok.get("yw%d_%d" % pv, self.qall("dt")))
                    tok["yev%d_%d" % (ns, tt)] = self.op(
                        V, "vector", "vs", "tensor_copy", slot, ps[:, :])

                @self.sec("scalar")
                def _(Sc):
                    self.W(Sc, "scalar", tok["yev%d_%d" % (ns, tt)])
                    tok["yw%d_%d" % (ns, tt)] = self.dma(
                        Sc, "dt",
                        H["rs_in"].ap()[SP * sp + 128 * tt:SP * sp + 128 * tt + 128,
                                        512 * ns:512 * ns + 512],
                        S["ev"][:, 2048 + 512 * tt:2048 + 512 * tt + 512])
            tok["dnmm%d" % ns] = ("ts", self.cnt["ts"])

        tok["ffdone%d" % sp] = [("ts", self.cnt["ts"]), ("vs", self.cnt["vs"]),
                                ("ss", self.cnt["ss"])]

    def attn0_local_span(self, tau):
        S, I, H, PS, tok = self.S, self.I, self.Hh, self.PS, self.tok
        t0 = SP * tau
        sigs = list(range(max(0, 4 * tau - 4), 4 * tau + 4))
        spb = sorted(set(s // 4 for s in sigs))
        spb_ix = {s: i for i, s in enumerate(spb)}

        @self.sec("sync")
        def _(SY):
            self.W(SY, "sync", tok.get("aLdone%d" % (tau - 1)))
            if tau == 0:
                self.W(SY, "sync", tok.get("qkvdone%d" % (NSP - 1)))
            self.W(SY, "sync", self.qall("dt"))
            lt = []
            for i in range(4):
                lt.append(self.dma(SY, "dl", S["qsp"][:, 512 * i:512 * i + 512],
                                   H["q0h"].ap()[128 * i:128 * i + 128, t0:t0 + SP]))
            for bi, sb_ in enumerate(spb):
                for kt in range(4):
                    lt.append(self.dma(
                        SY, "dl",
                        S["kl"][:, 2048 * bi + 512 * kt:2048 * bi + 512 * kt + 512],
                        H["kh"].ap()[:, 2048 * sb_ + 512 * kt:2048 * sb_ + 512 * kt + 512]))
            for i, s in enumerate(sigs):
                lt.append(self.dma(SY, "dl", S["vl"][:, 520 * i:520 * i + 520],
                                   H["vh"].ap()[:, 520 * s:520 * s + 520]))
            tok["ql"] = lt

        for h in range(4):
            pairs = [sigs[2 * i:2 * i + 2] for i in range((len(sigs) + 1) // 2)]
            n = len(pairs)
            ts_p, ss_p, vs_p = self.cnt["ts"], self.cnt["ss"], self.cnt["vs"]
            t_sc, t_av, t_exp, t_msk = {}, {}, {}, {}
            for k in range(n + 2):
                if k < n:
                    ts_p += 1
                    t_sc[k] = ts_p
                if 0 <= k - 2 < n:
                    ts_p += 1
                    t_av[k - 2] = ts_p
            for k in range(n):
                ss_p += 1
                t_exp[k] = ss_p
            for k in range(n):
                vs_p += len(pairs[k])
                t_msk[k] = vs_p

            @self.sec("tensor")
            def _(Te):
                self.W(Te, "tensor", tok["ql"])
                if h == 0:
                    self.W(Te, "tensor", ("vs", self.cnt["vs"]))
                else:
                    self.W(Te, "tensor", tok["ctxl%d" % (h - 1)])
                def emit_sc(k):
                    psc = PS["pA" if k % 2 == 0 else "pB"]
                    if k >= 2:
                        self.W(Te, "tensor", ("ss", t_exp[k - 2]))
                    for k_, s in enumerate(pairs[k]):
                        sb_, st_ = s // 4, s % 4
                        ksl = S["kl"][:, 2048 * spb_ix[sb_] + 512 * h + 128 * st_:
                                      2048 * spb_ix[sb_] + 512 * h + 128 * st_ + 128]
                        v = self.mm(Te, psc[:, 512 * k_:512 * k_ + 512], ksl,
                                    S["qsp"][:, 512 * h:512 * h + 512],
                                    start=True, stop=True,
                                    inc=(k_ == len(pairs[k]) - 1))
                    assert v[1] == t_sc[k]
                def emit_av(k):
                    self.W(Te, "tensor", ("vs", t_msk[k]))
                    base = 2048 if k % 2 == 0 else 3072
                    for k_, s in enumerate(pairs[k]):
                        vi = sigs.index(s)
                        va = S["vl"][:, 520 * vi + 130 * h:520 * vi + 130 * h + 65]
                        vb = S["vl"][:, 520 * vi + 130 * h + 65:520 * vi + 130 * h + 130]
                        first = (k == 0 and k_ == 0)
                        last = (k == n - 1 and k_ == len(pairs[k]) - 1)
                        self.mm(Te, PS["p0"][0:65, :], va,
                                S["att"][:, base + 512 * k_:base + 512 * k_ + 512],
                                start=first, stop=last)
                        v = self.mm(Te, PS["p1"][0:65, :], vb,
                                    S["att"][:, base + 512 * k_:base + 512 * k_ + 512],
                                    start=first, stop=last,
                                    inc=(k_ == len(pairs[k]) - 1))
                    assert v[1] == t_av[k]
                for k in range(n + 2):
                    if k < n:
                        emit_sc(k)
                    if k - 2 >= 0:
                        emit_av(k - 2)
            tok["lav%d" % h] = ("ts", self.cnt["ts"])

            @self.sec("scalar")
            def _(Sc):
                self.W(Sc, "scalar", tok["consts"])
                for k in range(n):
                    self.W(Sc, "scalar", ("ts", t_sc[k]))
                    if k >= 2:
                        self.W(Sc, "scalar", ("ts", t_av[k - 2]))
                    w = 512 * len(pairs[k])
                    base = 0 if k % 2 == 0 else 1024
                    psc = PS["pA" if k % 2 == 0 else "pB"]
                    v = self.op(Sc, "scalar", "ss", "activation",
                                S["att"][:, base:base + w], psc[:, 0:w], Act.Exp,
                                S["bias"][:, BC["mbl"]:BC["mbl"] + 1],
                                S["bias"][:, BC["sc0"]:BC["sc0"] + 1])
                    assert v[1] == t_exp[k]

            @self.sec("vector")
            def _(V):
                self.W(V, "vector", tok["msk"])
                for k in range(n):
                    self.W(V, "vector", ("ss", t_exp[k]))
                    if k >= 2:
                        self.W(V, "vector", ("ts", t_av[k - 2]))
                    base = 0 if k % 2 == 0 else 1024
                    mb = 2048 if k % 2 == 0 else 3072
                    for k_, s in enumerate(pairs[k]):
                        qd = s - 4 * tau + 4
                        v = self.op(V, "vector", "vs", "tensor_tensor",
                                    S["att"][:, mb + 512 * k_:mb + 512 * k_ + 512],
                                    S["att"][:, base + 512 * k_:base + 512 * k_ + 512],
                                    S["msk"][:, SP * qd:SP * qd + SP], Alu.mult)
                    assert v[1] == t_msk[k]

            @self.sec("vector")
            def _(V):
                self.W(V, "vector", tok["lav%d" % h])
                tok["nl%d" % h] = self.op(V, "vector", "vs", "reciprocal",
                                          S["svec"][:, 0:512], PS["p0"][64:65, :])

            @self.sec("scalar")
            def _(Sc):
                self.W(Sc, "scalar", tok["nl%d" % h])
                tok["nlw%d" % h] = self.dma(Sc, "dt", H["nscr"].ap()[0:1, 0:512],
                                            S["svec"][:, 0:512])

            @self.sec("gpsimd")
            def _(G):
                self.W(G, "gpsimd", tok["nlw%d" % h])
                tok["nlb%d" % h] = self.dma(
                    G, "dg", S["nbc"][:, 0:512],
                    H["nscr"].ap()[0:1, 0:512].partition_broadcast(128).squeeze(1))

            @self.sec("vector")
            def _(V):
                self.W(V, "vector", tok["nlb%d" % h])
                self.W(V, "vector", tok.get("ctlw%d" % (tau - 1)))
                self.op(V, "vector", "vs", "tensor_tensor",
                        S["ctl"][0:64, 512 * h:512 * h + 512], PS["p0"][0:64, :],
                        S["nbc"][0:64, 0:512], Alu.mult)
                tok["ctxl%d" % h] = self.op(
                    V, "vector", "vs", "tensor_tensor",
                    S["ctl"][64:128, 512 * h:512 * h + 512], PS["p1"][0:64, :],
                    S["nbc"][64:128, 0:512], Alu.mult)

        @self.sec("scalar")
        def _(Sc):
            self.W(Sc, "scalar", tok["ctxl3"])
            tok["ctlw%d" % tau] = self.dma(
                Sc, "dt", H["ctlh"].ap()[:, 2048 * tau:2048 * tau + 2048],
                S["ctl"][:, :])
        tok["aLdone%d" % tau] = ("ts", self.cnt["ts"])

    def attn0_global_span(self, tau):
        S, I, H, PS, tok = self.S, self.I, self.Hh, self.PS, self.tok
        t0 = SP * tau

        @self.sec("sync")
        def _(SY):
            self.W(SY, "sync", tok.get("aGdone%d" % (tau - 1)))
            if tau == 0:
                self.W(SY, "sync", tok.get("aLdone%d" % (NSP - 1)))
                self.W(SY, "sync", tok.get("wo"))
            self.W(SY, "sync", self.qall("dt"))
            lt = []
            for i in range(4):
                lt.append(self.dma(SY, "dl", S["qsp"][:, 512 * i:512 * i + 512],
                                   H["q0h"].ap()[128 * i:128 * i + 128, t0:t0 + SP]))
            lt.append(self.dma(SY, "dl", S["ctl"][:, :],
                               H["ctlh"].ap()[:, 2048 * tau:2048 * tau + 2048]))
            tok["qgl"] = lt

        for pr in range(2):
            @self.sec("tensor")
            def _(Te):
                self.W(Te, "tensor", tok["qgl"])
                self.W(Te, "tensor", tok["tiny"])
                ps = PS["p2"]
                self.W(Te, "tensor", ("vs", self.cnt["vs"]))
                self.mm(Te, ps[0:64, :], S["tiny"][:, 256:320],
                        S["qsp"][:, 512 * (2 * pr):512 * (2 * pr) + 512],
                        start=True, stop=True, tp=(0, 0))
                tok["qrmm%d" % pr] = self.mm(
                    Te, ps[64:128, :], S["tiny"][:, 256:320],
                    S["qsp"][:, 512 * (2 * pr + 1):512 * (2 * pr + 1) + 512],
                    start=True, stop=True, tp=(0, 64), inc=True)

            @self.sec("vector")
            def _(V):
                self.W(V, "vector", tok["qrmm%d" % pr])
                tok["qrev%d" % pr] = self.op(V, "vector", "vs", "tensor_copy",
                                             S["qr"][:, 512 * pr:512 * pr + 512],
                                             PS["p2"][:, :])

        for pr in range(2):
            n = 32
            ts_p, ss_p = self.cnt["ts"], self.cnt["ss"]
            t_sc, t_av, t_exp = {}, {}, {}
            for k in range(n + 2):
                if k < n:
                    ts_p += 1
                    t_sc[k] = ts_p
                if 0 <= k - 2 < n:
                    ts_p += 1
                    t_av[k - 2] = ts_p
            for k in range(n):
                ss_p += 1
                t_exp[k] = ss_p

            @self.sec("tensor")
            def _(Te):
                self.W(Te, "tensor", tok["qrev0"])
                self.W(Te, "tensor", tok["qrev1"])
                self.W(Te, "tensor", ("vs", self.cnt["vs"]))
                def emit_sc(k):
                    psc = PS["pA" if k % 2 == 0 else "pB"]
                    if k >= 2:
                        self.W(Te, "tensor", ("ss", t_exp[k - 2]))
                    else:
                        self.W(Te, "tensor", ("ss", self.cnt["ss"]))
                    kA = S["kr"][0:64, T * pr + 128 * k:T * pr + 128 * k + 128]
                    kB = S["kr"][64:128, T * pr + 128 * k:T * pr + 128 * k + 128]
                    self.mm(Te, psc[:, 0:512], kA,
                            S["qr"][0:64, 512 * pr:512 * pr + 512],
                            start=True, stop=True)
                    v = self.mm(Te, psc[:, 512:1024], kB,
                                S["qr"][64:128, 512 * pr:512 * pr + 512],
                                start=True, stop=True, inc=True)
                    assert v[1] == t_sc[k]
                def emit_av(k):
                    self.W(Te, "tensor", ("ss", t_exp[k]))
                    base = 0 if k % 2 == 0 else 1024
                    hA, hB = 2 * pr, 2 * pr + 1
                    first, last = (k == 0), (k == 31)
                    self.mm(Te, PS["p2"][0:65, :],
                            S["vr"][:, 260 * k + 65 * hA:260 * k + 65 * hA + 65],
                            S["att"][:, base:base + 512],
                            start=first, stop=last)
                    v = self.mm(Te, PS["p3"][0:65, :],
                                S["vr"][:, 260 * k + 65 * hB:260 * k + 65 * hB + 65],
                                S["att"][:, base + 512:base + 1024],
                                start=first, stop=last, inc=True)
                    assert v[1] == t_av[k]
                for k in range(n + 2):
                    if k < n:
                        emit_sc(k)
                    if k - 2 >= 0:
                        emit_av(k - 2)
            tok["gav%d" % pr] = ("ts", self.cnt["ts"])

            @self.sec("scalar")
            def _(Sc):
                for k in range(n):
                    self.W(Sc, "scalar", ("ts", t_sc[k]))
                    if k >= 2:
                        self.W(Sc, "scalar", ("ts", t_av[k - 2]))
                    base = 0 if k % 2 == 0 else 1024
                    psc = PS["pA" if k % 2 == 0 else "pB"]
                    v = self.op(Sc, "scalar", "ss", "activation",
                                S["att"][:, base:base + 1024], psc[:, :], Act.Exp,
                                S["bias"][:, BC["mbg"]:BC["mbg"] + 1],
                                S["bias"][:, BC["scg"]:BC["scg"] + 1])
                    assert v[1] == t_exp[k]

            for hi in range(2):
                h = 2 * pr + hi
                pg = PS["p2"] if hi == 0 else PS["p3"]

                @self.sec("vector")
                def _(V):
                    self.W(V, "vector", tok["gav%d" % pr])
                    self.op(V, "vector", "vs", "reciprocal", S["svec"][:, 0:512],
                            pg[64:65, :])
                    tok["ng%d" % h] = self.op(
                        V, "vector", "vs", "tensor_tensor", S["svec"][:, 512:1024],
                        S["svec"][:, 0:512], S["gate"][0:1, t0:t0 + SP], Alu.mult)

                @self.sec("scalar")
                def _(Sc):
                    self.W(Sc, "scalar", tok["ng%d" % h])
                    tok["ngw%d" % h] = self.dma(Sc, "dt", H["nscr"].ap()[0:1, SP:2 * SP],
                                                S["svec"][:, 512:1024])

                @self.sec("gpsimd")
                def _(G):
                    self.W(G, "gpsimd", tok["ngw%d" % h])
                    tok["ngb%d" % h] = self.dma(
                        G, "dg", S["nbc"][:, 0:512],
                        H["nscr"].ap()[0:1, SP:2 * SP].partition_broadcast(128).squeeze(1))

                @self.sec("vector")
                def _(V):
                    self.W(V, "vector", tok["ngb%d" % h])
                    tok["cgs%d" % hi] = self.op(
                        V, "vector", "vs", "tensor_tensor",
                        S["ev"][0:64, 512 * hi:512 * hi + 512],
                        pg[0:64, :], S["nbc"][0:64, 0:512], Alu.mult)

            for hi in range(2):
                h = 2 * pr + hi

                @self.sec("tensor")
                def _(Te):
                    self.W(Te, "tensor", tok["cgs%d" % hi])
                    ps = PS["pA"][:, 512 * hi:512 * hi + 512]
                    self.W(Te, "tensor", ("ss", self.cnt["ss"]))
                    self.W(Te, "tensor", tok.get("ctxf%d" % (h - 2)))
                    tok["uomm%d" % h] = self.mm(
                        Te, ps, S["tiny"][0:64, 320:448],
                        S["ev"][0:64, 512 * hi:512 * hi + 512],
                        start=True, stop=True, inc=True)

                @self.sec("vector")
                def _(V):
                    self.W(V, "vector", tok["uomm%d" % h])
                    ps = PS["pA"][:, 512 * hi:512 * hi + 512]
                    self.W(V, "vector", tok.get("womm_t"))
                    tok["ctxf%d" % h] = self.op(
                        V, "vector", "vs", "tensor_tensor",
                        S["cx"][:, 512 * h:512 * h + 512],
                        S["ctl"][:, 512 * h:512 * h + 512], ps, Alu.add)

        for tt in range(4):
            for ns in range(4):
                @self.sec("tensor")
                def _(Te):
                    for h in range(4):
                        self.W(Te, "tensor", tok["ctxf%d" % h])
                    self.W(Te, "tensor", tok["wo"])
                    ps = PS["pB"][:, 512 * (ns % 2):512 * (ns % 2) + 512]
                    self.W(Te, "tensor", ("ss", self.cnt["ss"]))
                    pv = (ns - 2, tt) if ns >= 2 else ((ns + 2), tt - 1)
                    self.W(Te, "tensor", tok.get("woev%d_%d" % (tt, ns - 2) if ns >= 2
                                                 else "woev%d_%d" % (tt - 1, ns + 2)))
                    for cc in range(4):
                        self.mm(Te, ps,
                                S["cx"][:, 512 * cc + 128 * tt:512 * cc + 128 * tt + 128],
                                S["wC"][:, 2048 * cc + 512 * ns:2048 * cc + 512 * ns + 512],
                                start=(cc == 0), stop=(cc == 3), inc=(cc == 3))
                    tok["wops%d_%d" % (tt, ns)] = ("ts", self.cnt["ts"])

                @self.sec("vector")
                def _(V):
                    self.W(V, "vector", tok["wops%d_%d" % (tt, ns)])
                    self.W(V, "vector", tok.get("wow%d_%d" % (tt - 1, ns)))
                    tok["woev%d_%d" % (tt, ns)] = self.op(
                        V, "vector", "vs", "tensor_copy",
                        S["ev"][:, 512 * ns:512 * ns + 512],
                        PS["pB"][:, 512 * (ns % 2):512 * (ns % 2) + 512])

                @self.sec("gpsimd")
                def _(G):
                    self.W(G, "gpsimd", tok["woev%d_%d" % (tt, ns)])
                    if tt == 0 and ns == 0:
                        self.W(G, "gpsimd", self.qall("dt"))
                    tok["wow%d_%d" % (tt, ns)] = self.dma(
                        G, "dg",
                        H["rs_in"].ap()[t0 + 128 * tt:t0 + 128 * tt + 128,
                                        512 * ns:512 * ns + 512],
                        S["ev"][:, 512 * ns:512 * ns + 512], accum=True)
        tok["womm_t"] = ("ts", self.cnt["ts"])
        tok["aGdone%d" % tau] = ("ts", self.cnt["ts"])

    def attn1_span(self, tau):
        S, I, H, PS, tok = self.S, self.I, self.Hh, self.PS, self.tok
        t0 = SP * tau
        nsig = 4 * tau + 4

        @self.sec("sync")
        def _(SY):
            self.W(SY, "sync", tok.get("a1done%d" % (tau - 1)))
            if tau == 0:
                self.W(SY, "sync", tok.get("qkvdone%d" % (NSP - 1)))
                self.W(SY, "sync", tok.get("wo"))
            self.W(SY, "sync", self.qall("dt"))
            lt = []
            for i in range(4):
                lt.append(self.dma(SY, "dl", S["qsp"][:, 512 * i:512 * i + 512],
                                   H["q0h"].ap()[128 * i:128 * i + 128, t0:t0 + SP]))
            tok["ql"] = lt

        for grp in range(2):
            pcs = [PS["p0"], PS["p1"], PS["p2"], PS["p3"]]
            nblk = (nsig + 7) // 8
            for blkb in range(nblk):
                lo, hi = 8 * blkb, min(8 * blkb + 8, nsig)

                @self.sec("sync")
                def _(SY):
                    if blkb == 0:
                        self.W(SY, "sync", ("ts", self.cnt["ts"]))
                    self.W(SY, "sync", tok.get("l1b%d_%d" % (grp, blkb - 1)))
                    lt = []
                    for i, s in enumerate(range(lo, hi)):
                        lt.append(self.dma(SY, "dl", S["vl1"][:, 520 * i:520 * i + 520],
                                           H["vh"].ap()[:, 520 * s:520 * s + 520]))
                        sb_, st_ = s // 4, s % 4
                        for pri in range(2):
                            pr = 2 * grp + pri
                            lt.append(self.dma(
                                SY, "dl",
                                S["kl1"][:, 1024 * i + 128 * pri:1024 * i + 128 * pri + 128],
                                H["kh"].ap()[:, 2048 * sb_ + 512 * pr + 128 * st_:
                                             2048 * sb_ + 512 * pr + 128 * st_ + 128]))
                    tok["kvl"] = lt

                steps = [(s, pri) for s in range(lo, hi) for pri in range(2)]
                n = len(steps)
                ts_p, ss_p, vs_p = self.cnt["ts"], self.cnt["ss"], self.cnt["vs"]
                t_sc, t_av, t_exp, t_msk = {}, {}, {}, {}
                for k in range(n + 2):
                    if k < n:
                        ts_p += 1
                        t_sc[k] = ts_p
                    if 0 <= k - 2 < n:
                        ts_p += 1
                        t_av[k - 2] = ts_p
                for k in range(n):
                    ss_p += 1
                    t_exp[k] = ss_p
                for k in range(n):
                    s, pri = steps[k]
                    if s >= 4 * tau:
                        vs_p += 2
                        t_msk[k] = vs_p

                @self.sec("tensor")
                def _(Te):
                    self.W(Te, "tensor", tok["kvl"])
                    self.W(Te, "tensor", tok["ql"])
                    if blkb == 0 and grp == 0:
                        self.W(Te, "tensor", ("vs", self.cnt["vs"]))
                    if grp == 1 and blkb == 0:
                        self.W(Te, "tensor", [tok["c1f%d" % hh] for hh in range(4)])
                    def emit_sc(k):
                        s, pri = steps[k]
                        pr = 2 * grp + pri
                        i = s - lo
                        pnm = "pA" if k % 2 == 0 else "pB"
                        psc = PS[pnm]
                        if k >= 2:
                            self.W(Te, "tensor", ("ss", t_exp[k - 2]))
                        else:
                            self.W(Te, "tensor", ("ss", self.cnt["ss"]))
                        ks = S["kl1"][:, 1024 * i + 128 * pri:1024 * i + 128 * pri + 128]
                        self.mm(Te, psc[:, 0:512], ks[0:64, :],
                                S["qsp"][0:64, 512 * pr:512 * pr + 512],
                                start=True, stop=True)
                        v = self.mm(Te, psc[:, 512:1024], ks[64:128, :],
                                    S["qsp"][64:128, 512 * pr:512 * pr + 512],
                                    start=True, stop=True, inc=True)
                        assert v[1] == t_sc[k], (v, t_sc[k])
                    def emit_av(k):
                        s, pri = steps[k]
                        i = s - lo
                        diag = (s >= 4 * tau)
                        base = 0 if k % 2 == 0 else 1024
                        if diag:
                            self.W(Te, "tensor", ("vs", t_msk[k]))
                            asl = 2048 + base
                        else:
                            self.W(Te, "tensor", ("ss", t_exp[k]))
                            asl = base
                        pcA, pcB = pcs[2 * pri], pcs[2 * pri + 1]
                        pr = 2 * grp + pri
                        first, last = (s == 0), (s == nsig - 1)
                        self.mm(Te, pcA[0:65, :],
                                S["vl1"][:, 520 * i + 65 * (2 * pr):
                                        520 * i + 65 * (2 * pr) + 65],
                                S["att"][:, asl:asl + 512],
                                start=first, stop=last)
                        v = self.mm(Te, pcB[0:65, :],
                                    S["vl1"][:, 520 * i + 65 * (2 * pr + 1):
                                            520 * i + 65 * (2 * pr + 1) + 65],
                                    S["att"][:, asl + 512:asl + 1024],
                                    start=first, stop=last, inc=True)
                        assert v[1] == t_av[k], (v, t_av[k])
                    for k in range(n + 2):
                        if k < n:
                            emit_sc(k)
                        if k - 2 >= 0:
                            emit_av(k - 2)

                @self.sec("scalar")
                def _(Sc):
                    for k in range(n):
                        self.W(Sc, "scalar", ("ts", t_sc[k]))
                        if k >= 2:
                            self.W(Sc, "scalar", ("ts", t_av[k - 2]))
                        base = 0 if k % 2 == 0 else 1024
                        pnm = "pA" if k % 2 == 0 else "pB"
                        v = self.op(Sc, "scalar", "ss", "activation",
                                    S["att"][:, base:base + 1024], PS[pnm][:, :],
                                    Act.Exp,
                                    S["bias"][:, BC["mb1"]:BC["mb1"] + 1],
                                    S["bias"][:, BC["sc1"]:BC["sc1"] + 1])
                        assert v[1] == t_exp[k], (v, t_exp[k])

                @self.sec("vector")
                def _(V):
                    self.W(V, "vector", tok["msk"])
                    for k in range(n):
                        s, pri = steps[k]
                        if s < 4 * tau:
                            continue
                        self.W(V, "vector", ("ss", t_exp[k]))
                        if k >= 2:
                            self.W(V, "vector", ("ts", t_av[k - 2]))
                        base = 0 if k % 2 == 0 else 1024
                        qd = s - 4 * tau
                        ms = S["msk"][:, SP * qd:SP * qd + SP]
                        self.op(V, "vector", "vs", "tensor_tensor",
                                S["att"][:, 2048 + base:2048 + base + 512],
                                S["att"][:, base:base + 512], ms, Alu.mult)
                        v = self.op(V, "vector", "vs", "tensor_tensor",
                                    S["att"][:, 2048 + base + 512:2048 + base + 1024],
                                    S["att"][:, base + 512:base + 1024], ms, Alu.mult)
                        assert v[1] == t_msk[k], (v, t_msk[k])

                tok["l1b%d_%d" % (grp, blkb)] = ("ts", self.cnt["ts"])

            for pri in range(2):
                pr = 2 * grp + pri
                for hi2 in range(2):
                    h = 2 * pr + hi2
                    pg = pcs[2 * pri + hi2]

                    @self.sec("vector")
                    def _(V):
                        self.W(V, "vector", ("ts", self.cnt["ts"]))
                        tok["n1%d" % h] = self.op(V, "vector", "vs", "reciprocal",
                                                  S["svec"][:, 0:512], pg[64:65, :])

                    @self.sec("scalar")
                    def _(Sc):
                        self.W(Sc, "scalar", tok["n1%d" % h])
                        tok["n1w%d" % h] = self.dma(Sc, "dt", H["nscr"].ap()[0:1, 0:512],
                                                    S["svec"][:, 0:512])

                    @self.sec("gpsimd")
                    def _(G):
                        self.W(G, "gpsimd", tok["n1w%d" % h])
                        tok["n1b%d" % h] = self.dma(
                            G, "dg", S["nbc"][:, 0:512],
                            H["nscr"].ap()[0:1, 0:512].partition_broadcast(128).squeeze(1))

                    @self.sec("vector")
                    def _(V):
                        self.W(V, "vector", tok["n1b%d" % h])
                        self.W(V, "vector", tok.get("womm_t"))
                        tok["c1f%d" % h] = self.op(
                            V, "vector", "vs", "tensor_tensor",
                            S["cx"][64 * (h % 2):64 * (h % 2) + 64,
                                    512 * (h // 2):512 * (h // 2) + 512],
                            pg[0:64, :], S["nbc"][0:64, 0:512], Alu.mult)

        for tt in range(4):
            for ns in range(4):
                @self.sec("tensor")
                def _(Te):
                    for h in range(8):
                        self.W(Te, "tensor", tok["c1f%d" % h])
                    self.W(Te, "tensor", tok["wo"])
                    ps = PS["pB"][:, 512 * (ns % 2):512 * (ns % 2) + 512]
                    self.W(Te, "tensor", ("ss", self.cnt["ss"]))
                    self.W(Te, "tensor", tok.get("woev%d_%d" % (tt, ns - 2) if ns >= 2
                                                 else "woev%d_%d" % (tt - 1, ns + 2)))
                    for cc in range(4):
                        self.mm(Te, ps,
                                S["cx"][:, 512 * cc + 128 * tt:512 * cc + 128 * tt + 128],
                                S["hsp"][:, 2048 * cc + 512 * ns:2048 * cc + 512 * ns + 512],
                                start=(cc == 0), stop=(cc == 3), inc=(cc == 3))
                    tok["wops%d_%d" % (tt, ns)] = ("ts", self.cnt["ts"])

                @self.sec("vector")
                def _(V):
                    self.W(V, "vector", tok["wops%d_%d" % (tt, ns)])
                    self.W(V, "vector", tok.get("wow%d_%d" % (tt - 1, ns)))
                    tok["woev%d_%d" % (tt, ns)] = self.op(
                        V, "vector", "vs", "tensor_copy",
                        S["ctl"][:, 512 * ns:512 * ns + 512],
                        PS["pB"][:, 512 * (ns % 2):512 * (ns % 2) + 512])

                @self.sec("scalar")
                def _(Sc):
                    self.W(Sc, "scalar", tok["woev%d_%d" % (tt, ns)])
                    tok["wow%d_%d" % (tt, ns)] = self.dma(
                        Sc, "dt",
                        H["rs_in"].ap()[t0 + 128 * tt:t0 + 128 * tt + 128,
                                        512 * ns:512 * ns + 512],
                        S["ctl"][:, 512 * ns:512 * ns + 512])
        tok["womm_t"] = ("ts", self.cnt["ts"])
        tok["a1done%d" % tau] = ("ts", self.cnt["ts"])

    def p6(self):
        S, I, H, PS, tok = self.S, self.I, self.Hh, self.PS, self.tok

        @self.sec("gpsimd")
        def _(G):
            self.W(G, "gpsimd", ("vs", self.cnt["vs"]))
            tok["bdnb"] = self.dma(G, "dg", S["bcst"][:, :],
                                   I["bdn"].partition_broadcast(128).squeeze(1))

        for ti in range(8):
            @self.sec("sync")
            def _(SY):
                self.W(SY, "sync", ("vs", self.cnt["vs"]))
                self.W(SY, "sync", tok["rs0"])
                self.W(SY, "sync", tok.get("x1w%d" % (ti - 1)))
                tok["x0l"] = self.dma(SY, "dl", S["xt"][:, :],
                                      I["xin"][128 * ti:128 * ti + 128, :])
                tok["yfl"] = self.dma(SY, "dl", S["cx"][:, :],
                                      H["rs_out"].ap()[128 * ti:128 * ti + 128, :])

            @self.sec("vector")
            def _(V):
                self.W(V, "vector", tok["x0l"])
                self.W(V, "vector", tok["yfl"])
                self.W(V, "vector", tok["rs0"])
                self.W(V, "vector", tok.get("x1w%d" % (ti - 1)))
                self.op(V, "vector", "vs", "tensor_add", S["xt2"][:, :],
                        S["xt"][:, :], S["cx"][:, :])
                self.W(V, "vector", tok["bdnb"])
                tok["x1t"] = self.op(V, "vector", "vs", "tensor_add", S["xt"][:, :],
                                     S["xt2"][:, :], S["bcst"][:, :])

            @self.sec("scalar")
            def _(Sc):
                self.W(Sc, "scalar", tok["x1t"])
                tok["x1w%d" % ti] = self.dma(
                    Sc, "dt", H["x1h"].ap()[128 * ti:128 * ti + 128, :], S["xt"][:, :])

        @self.sec("sync")
        def _(SY):
            self.W(SY, "sync", self.qall("dt"))
        self.p1(layer=1)

    def p9_p10(self):
        S, I, H, PS, tok = self.S, self.I, self.Hh, self.PS, self.tok

        @self.sec("gpsimd")
        def _(G):
            self.W(G, "gpsimd", ("vs", self.cnt["vs"]))
            tok["bffb"] = self.dma(G, "dg", S["bcst"][:, :],
                                   I["bff1"].partition_broadcast(128).squeeze(1))

        @self.sec("gpsimd")
        def _(G):
            self.W(G, "gpsimd", tok["bffb"])
            tok["bcomb"] = self.dma(G, "dg", S["bcst"][:, :],
                                    I["bo1"].partition_broadcast(128).squeeze(1),
                                    accum=True)

        for osp in range(2):
            @self.sec("sync")
            def _(SY):
                self.W(SY, "sync", tok.get("p9done%d" % (osp - 1)))
                lt = []
                for cc in range(NCH):
                    lt.append(self.dma(SY, "dl",
                                       S["xnsp"][:, SP * cc:SP * cc + SP],
                                       H["xqh"].ap()[SP * osp:SP * osp + SP,
                                                     128 * cc:128 * cc + 128],
                                       transpose=True))
                tok["xql"] = lt

            for ns in range(4):
                @self.sec("sync")
                def _(SY):
                    self.W(SY, "sync", tok["tqw15"])
                    self.W(SY, "sync", tok.get("f1done%d" % (ns - 1)))
                    lt = []
                    for kc in range(NCH):
                        lt.append(self.dma(
                            SY, "dl", S["wA"][:, 512 * kc:512 * kc + 512],
                            H["wt1q"].ap()[512 * ns:512 * ns + 512,
                                           128 * kc:128 * kc + 128],
                            transpose=True))
                    tok["wtl"] = lt

                for tt in range(4):
                    ti = 4 * osp + tt

                    @self.sec("tensor")
                    def _(Te):
                        self.W(Te, "tensor", tok["wtl"])
                        self.W(Te, "tensor", tok["xql"])
                        ps = PS["p%d" % (tt % 2)]
                        pv = (ns, tt - 2) if tt >= 2 else ((ns - 1) % 4, tt + 2)
                        self.W(Te, "tensor", tok.get("x2ev%d_%d" % pv))
                        for kc in range(NCH):
                            self.mm(Te, ps[:, :],
                                    S["xnsp"][:, SP * kc + 128 * tt:SP * kc + 128 * tt + 128],
                                    S["wA"][:, 512 * kc:512 * kc + 512],
                                    start=(kc == 0), stop=(kc == NCH - 1),
                                    inc=(kc == NCH - 1))
                        tok["f1ps%d_%d" % (ns, tt)] = ("ts", self.cnt["ts"])

                    @self.sec("sync")
                    def _(SY):
                        self.W(SY, "sync", ("vs", self.cnt["vs"]))
                        self.W(SY, "sync", tok["rs1"])
                        tok["x1l"] = self.dma(
                            SY, "dl", S["xt"][:, 0:512],
                            H["x1h"].ap()[128 * ti:128 * ti + 128,
                                          512 * ns:512 * ns + 512])
                        tok["ya1"] = self.dma(
                            SY, "dl", S["cx"][:, 0:512],
                            H["rs_out"].ap()[128 * ti:128 * ti + 128,
                                             512 * ns:512 * ns + 512])

                    @self.sec("vector")
                    def _(V):
                        self.W(V, "vector", tok["f1ps%d_%d" % (ns, tt)])
                        self.W(V, "vector", self.qall("dt"))
                        self.W(V, "vector", tok["gbc"])
                        self.W(V, "vector", tok["bcomb"])
                        self.W(V, "vector", tok["x1l"])
                        self.W(V, "vector", tok["ya1"])
                        self.W(V, "vector", tok["rs1"])
                        ps = PS["p%d" % (tt % 2)]
                        xs = S["sm"][:, 40 + ti:41 + ti]
                        self.op(V, "vector", "vs", "tensor_scalar",
                                S["xt"][:, 512:1024], ps[:, :], xs, None, Alu.mult)
                        self.op(V, "vector", "vs", "tensor_tensor",
                                S["xt"][:, 1024:1536], S["xt"][:, 512:1024],
                                S["gbc"][:, 512 * ns:512 * ns + 512], Alu.mult)
                        self.op(V, "vector", "vs", "tensor_add",
                                S["xt"][:, 1536:2048], S["xt"][:, 0:512],
                                S["cx"][:, 0:512])
                        self.op(V, "vector", "vs", "tensor_add",
                                S["xt"][:, 1536:2048], S["xt"][:, 1536:2048],
                                S["xt"][:, 1024:1536])
                        self.op(V, "vector", "vs", "tensor_tensor",
                                S["xt2"][:, 0:512], S["xt"][:, 1536:2048],
                                S["bcst"][:, 512 * ns:512 * ns + 512],
                                Alu.add)
                        tok["x2ev%d_%d" % (ns, tt)] = ("vs", self.cnt["vs"])

                    @self.sec("scalar")
                    def _(Sc):
                        self.W(Sc, "scalar", tok["x2ev%d_%d" % (ns, tt)])
                        tok["youtw%d_%d" % (ns, ti)] = self.dma(
                            Sc, "dt",
                            self.yout[128 * ti:128 * ti + 128, 512 * ns:512 * ns + 512],
                            S["xt2"][:, 0:512])
                tok["f1done%d" % ns] = ("ts", self.cnt["ts"])
            tok["p9done%d" % osp] = ("ts", self.cnt["ts"])

    def finish(self):
        S, H = self.S, self.Hh

        @self.sec("gpsimd")
        def _(G):
            for s in ("ts", "vs", "ss", "cc"):
                self.W(G, "gpsimd", (s, self.cnt[s]))
            self.W(G, "gpsimd", self.allq())
            n = 0
            for name, ap in self.dbg_out.items():
                if name in H:
                    G.dma_start(out=ap, in_=H[name].ap()).then_inc(self.h["fin"], 16)
                    n += 1
                elif name in S:
                    G.dma_start(out=ap, in_=S[name][:, :]).then_inc(self.h["fin"], 16)
                    n += 1
            self.cnt["fin"] += 16 * n
            if self.cnt["fin"] > 0:
                G.wait_ge(self.h["fin"], self.cnt["fin"])


def sm_col(S, i):
    return S["sm"][:, i:i + 1]


def build_program(dbg=(), upto="full"):
    k = Kern(dbg, upto)
    nc = k.build()
    return nc, k


def kernel(**inputs):
    nc, k = build_program()
    in_maps = [prep_core_inputs(c, inputs) for c in range(NCORE)]
    res = run_bass_kernel_spmd(nc, in_maps, core_ids=list(range(NCORE)))
    out = np.zeros((B, T, C), np.float32)
    for c in range(NCORE):
        b, j = c // 4, c % 4
        out[b, OWN * j:OWN * j + OWN, :] = res.results[c]["yout"]
    return out
